# version 38
# baseline (speedup 1.0000x reference)
import math
import numpy as np
import concourse.bass as bass
import concourse.mybir as mybir
from concourse.bass_utils import run_bass_kernel_spmd

F32 = mybir.dt.float32
BF16 = mybir.dt.bfloat16
AF = mybir.ActivationFunctionType
ALU = mybir.AluOpType
AX = mybir.AxisListType

ENGINES = ("sync", "gpsimd", "tensor", "scalar", "vector")


def make_cfg(D=2048, TOWN=2048, TOTH=2048):
    c = {}
    c["D"] = D
    c["ND"] = D // 128
    c["H"] = D // 128
    c["TOWN"] = TOWN
    c["TOTH"] = TOTH
    c["TALL"] = TOWN + TOTH
    c["QB"] = min(512, TOWN)
    c["NQC"] = TOWN // c["QB"]
    c["NQT"] = TOWN // 128
    c["NK"] = c["TALL"] // 128
    c["NE"] = 8
    c["R"] = 16
    c["ER"] = 128
    c["EPS"] = 1e-5
    c["SM"] = 1.0 / math.sqrt(128.0)
    c["VCH"] = max(128, D // 8)
    c["NRANKS"] = 2 if TOTH > 0 else 1
    return c


CFG_FULL = make_cfg()


class Builder:
    def __init__(self, cfg):
        self.cfg = cfg
        self.nc = bass.Bass("TRN2", target_bir_lowering=False,
                            dynamic_dma_scratch_size=16384)
        self.plan = {e: [] for e in ENGINES}
        self.sems = {}
        self.cnt = {}

    def _sem(self, name):
        if name not in self.sems:
            self.sems[name] = self.nc.alloc_semaphore(name)
            self.cnt[name] = 0
        return self.sems[name]

    ENG_SEM = {"vector": "dv", "scalar": "ac"}

    def op(self, eng, fn, waits=(), inc=None):
        ws = [(s, v) for (s, v) in waits if v and v > 0]
        es = self.ENG_SEM.get(eng)
        if es is not None:
            prev = self.cnt.get(es, 0)
            if prev:
                ws.append((es, prev))
            if inc is None:
                inc = (es, 1)
            else:
                assert inc[0] == es, (eng, inc)
        if inc is not None:
            self._sem(inc[0])
            self.cnt[inc[0]] += inc[1]
        for s, _ in ws:
            self._sem(s)
        self.plan[eng].append((fn, tuple(ws), inc))
        return self.cnt[inc[0]] if inc is not None else None

    def val(self, name):
        return self.cnt.get(name, 0)

    def dma(self, eng, chan, dst, src, waits=()):
        ws = list(waits)
        prev = self.cnt.get(chan, 0)
        if prev:
            ws.append((chan, prev))
        return self.op(eng, lambda g, d=dst, s=src: g.dma_start(d, s),
                       waits=ws, inc=(chan, 16))

    def play(self):
        with self.nc.Block() as block:
            for e in ENGINES:
                ops = self.plan[e]
                if not ops:
                    continue

                def mk(ops=ops):
                    def body(eng):
                        for fn, waits, inc in ops:
                            for (s, v) in waits:
                                eng.wait_ge(self.sems[s], v)
                            inst = fn(eng)
                            if inc is not None:
                                inst.then_inc(self.sems[inc[0]], inc[1])
                    return body

                getattr(block, e)(mk())

    def build(self):
        c = self.cfg
        nc = self.nc
        D, ND, H = c["D"], c["ND"], c["H"]
        TOWN, TOTH, TALL = c["TOWN"], c["TOTH"], c["TALL"]
        QB, NQC, NQT, NK = c["QB"], c["NQC"], c["NQT"], c["NK"]
        NE, R, ER = c["NE"], c["R"], c["ER"]
        EPS, SM, VCH = c["EPS"], c["SM"], c["VCH"]
        NP = NK // 2
        NQTC = QB // 128
        NVS = D // VCH

        x_h = nc.dram_tensor("x", [TOWN, D], F32, kind="ExternalInput")
        NVS_ = D // max(128, D // 8)
        wkq_h = nc.dram_tensor("wkq", [2 * ND, 128, D], F32, kind="ExternalInput")
        wv_h = nc.dram_tensor("wv", [NVS_, 128, (D // NVS_) * ND], F32, kind="ExternalInput")
        bqkvt_h = nc.dram_tensor("bqkvt", [128, 3 * ND], F32, kind="ExternalInput")
        wout_h = nc.dram_tensor("wout", [ND, 128, D], F32, kind="ExternalInput")
        boutt_h = nc.dram_tensor("boutt", [128, ND], F32, kind="ExternalInput")
        wffnt_h = nc.dram_tensor("wffnt", [ND, 128, D], F32, kind="ExternalInput")
        bffnt_h = nc.dram_tensor("bffnt", [128, ND], F32, kind="ExternalInput")
        aall_h = nc.dram_tensor("aall", [D, ER], F32, kind="ExternalInput")
        ball_h = nc.dram_tensor("ball", [ER, D], F32, kind="ExternalInput")
        gwt_h = nc.dram_tensor("gwt", [D, NE], F32, kind="ExternalInput")
        gb_h = nc.dram_tensor("gb", [NE], F32, kind="ExternalInput")
        ident_h = nc.dram_tensor("ident", [128, 128], F32, kind="ExternalInput")
        e16_h = nc.dram_tensor("e16", [NE, ER], F32, kind="ExternalInput")
        out_h = nc.dram_tensor("out", [TOWN, D], F32, kind="ExternalOutput")
        dbg = c.get("debug", False)
        NRANKS = c["NRANKS"]
        agk_in = nc.dram_tensor("agk_in", [H * 128, TOWN], BF16, kind="Internal")
        agv_in = nc.dram_tensor("agv_in", [H * TOWN, 128], BF16, kind="Internal")
        NAG = 4 if H % 4 == 0 else 1
        HG = H // NAG
        if NRANKS > 1:
            agk_out = [nc.dram_tensor(f"agk_out{g}", [NRANKS * HG * 128, TOWN], BF16, kind="Internal")
                       for g in range(NAG)]
            agv_out = [nc.dram_tensor(f"agv_out{g}", [NRANKS * HG * TOWN, 128], BF16, kind="Internal")
                       for g in range(NAG)]
        if dbg:
            dbgq_h = nc.dram_tensor("dbgq", [128, H * TOWN], BF16, kind="ExternalOutput")
            dbgx_h = nc.dram_tensor("dbgx", [128, ND * TOWN], BF16, kind="ExternalOutput")
            dbga_h = nc.dram_tensor("dbga", [128, H * TOWN], BF16, kind="ExternalOutput")
            dbgh_h = nc.dram_tensor("dbgh", [128, NQT * D], BF16, kind="ExternalOutput")

        A = nc.alloc_sbuf_tensor
        big0 = A("big0", [128, ND * max(TOWN, TOTH)], BF16)
        big1 = A("big1", [128, H * TOWN], BF16)
        bigC = A("bigC", [128, max(2 * ND * VCH, NQTC * D)], BF16)
        pool = A("pool", [128, max(4 * TALL, 8 * D)], BF16)
        e_buf = A("e_buf", [128, max(4 * QB, D)], BF16)
        oevac = A("oevac", [128, QB], BF16)
        gxa_sb = A("gxa_sb", [128, QB], BF16)
        gexp_sb = A("gexp_sb", [128, QB], BF16)
        rzb_sb = A("rzb_sb", [128, QB], BF16)
        rz_bf = A("rz_bf", [1, QB], BF16)
        es4_sb = A("es4_sb", [128, QB], BF16)
        esacc = oevac
        smalls = A("smalls", [128, 16], F32)
        lg_fm = A("lg_fm", [NE, QB], BF16)
        gt_bf = A("gt_bf", [128, NQTC * NE], BF16)
        lg_tm = A("lg_tm", [128, NQTC * NE], F32)
        gt_tm = A("gt_tm", [128, NQTC * NE], F32)
        gt_fm = A("gt_fm", [NE, QB], BF16)
        ident_bf = A("ident_bf", [128, 128], BF16)
        e16_bf = A("e16_bf", [NE, ER], BF16)
        ones_col = A("ones_col", [128, 1], BF16)
        ones_row = A("ones_row", [1, 128], BF16)
        ones_rowq = A("ones_rowq", [1, QB], BF16)
        bqkvt_sb = A("bqkvt_sb", [128, 3 * ND], F32)
        boutt_sb = A("boutt_sb", [128, ND], F32)
        b2out_sb = A("b2out_sb", [128, ND], F32)
        bv_bf = A("bv_bf", [128, ND], BF16)
        bffnt_sb = A("bffnt_sb", [128, ND], F32)
        gb_row = A("gb_row", [1, NE], BF16)
        aall_sb = A("aall_sb", [128, ND * ER], BF16)
        ball_sb = A("ball_sb", [128, D], BF16)
        gwt_sb = A("gwt_sb", [128, ND * NE], BF16)
        wblk = A("wblk", [128, 2 * ND * 128], BF16)
        eps_c = A("eps_c", [128, 1], F32)
        nc.const_aps.register(F32, EPS, eps_c[:, :])

        rem = nc.sbuf_bytes_remaining
        assert rem >= 0, f"SBUF over budget by {-rem} bytes"
        self.sbuf_remaining = rem
        self._smalls = smalls

        stat = smalls[:, 0:16]
        lnz_row = bigC[0:1, 0:2 * QB].bitcast(F32)
        W_bigc = max(2 * ND * VCH, NQTC * D)
        if W_bigc >= 14 * QB:
            eslots = [e_buf[:, 0:2 * QB], e_buf[:, 2 * QB:4 * QB],
                      bigC[:, 4 * QB:6 * QB], bigC[:, 6 * QB:8 * QB]]
            av_sb = bigC[:, 8 * QB:9 * QB]
            eaw = bigC[:, 10 * QB:12 * QB]
            ewt = bigC[:, 12 * QB:14 * QB]
        else:
            ex23 = A("ex23", [128, 4 * QB], BF16)
            eslots = [e_buf[:, 0:2 * QB], e_buf[:, 2 * QB:4 * QB],
                      ex23[:, 0:2 * QB], ex23[:, 2 * QB:4 * QB]]
            av_sb = A("av_sb", [128, QB], BF16)
            eaw = A("eaw", [128, 2 * QB], BF16)
            ewt = A("ewt", [128, 2 * QB], BF16)
        NSL = len(eslots)

        xtm_t = [pool[:, i * D:(i + 1) * D] for i in range(2)]
        vevac_t = [pool[:, 2 * D + i * VCH: 2 * D + (i + 1) * VCH] for i in range(2)]
        TMX = max(TOWN, TOTH)
        kq0 = 2 * D + 2 * VCH
        kqev_t = [pool[:, kq0 + i * TMX: kq0 + (i + 1) * TMX] for i in range(2)]
        kh_t = [pool[:, i * TALL:(i + 1) * TALL] for i in range(2)]
        vh_t = [pool[:, (2 + i) * TALL:(3 + i) * TALL] for i in range(2)]
        poolf = pool[:, :].bitcast(F32)
        xf32_t2 = [poolf[:, 0:D], poolf[:, D:2 * D]]
        outf_t = [poolf[:, D + i * D: D + (i + 1) * D] for i in range(2)]
        rscr_t = poolf[:, 3 * D: 4 * D]

        PA = nc.alloc_psum_tensor("PA", [128, 2048], F32)
        PB = nc.alloc_psum_tensor("PB", [128, 2048], F32)
        PAb = PA[:, :].bitcast(BF16)
        PBb = PB[:, :].bitcast(BF16)

        wblkv = [wblk[:, i * ND * 128:(i + 1) * ND * 128] for i in range(2)]


        self.dma("gpsimd", "cw", ident_bf[:, :], ident_h.ap())
        self.dma("gpsimd", "cw", e16_bf[:, :], e16_h.ap())
        self.dma("gpsimd", "cw", bqkvt_sb[:, :], bqkvt_h.ap())
        self.dma("gpsimd", "cw", boutt_sb[:, :], boutt_h.ap())
        self.dma("gpsimd", "cw", bffnt_sb[:, :], bffnt_h.ap())
        self.dma("gpsimd", "cw", gb_row[:, :], gb_h.ap().rearrange("(o e) -> o e", o=1))
        self.dma("gpsimd", "cw", aall_sb[:, :].rearrange("p (dc e) -> p dc e", dc=ND),
                 aall_h.ap().rearrange("(dc p) e -> p dc e", p=128))
        self.dma("gpsimd", "cw", ball_sb[:, :], ball_h.ap())
        self.dma("gpsimd", "cw", gwt_sb[:, :].rearrange("p (dc e) -> p dc e", dc=ND),
                 gwt_h.ap().rearrange("(dc p) e -> p dc e", p=128))
        consts_g = self.val("cw")
        m = self.op("vector", lambda v: v.memset(ones_col[:, :], 1.0))
        m = self.op("vector", lambda v: v.memset(ones_row[:, :], 1.0))
        ones_dv = self.op("vector", lambda v: v.memset(ones_rowq[:, :], 1.0),
                          inc=("dv", 1))
        self.op("vector", lambda v: v.memset(eps_c[:, :], EPS), inc=("dv", 1))
        bvdv = self.op("vector",
                       lambda v: v.tensor_copy(bv_bf[:, :], bqkvt_sb[:, 2 * ND:3 * ND]),
                       waits=[("cw", consts_g)], inc=("dv", 1))

        halves = [(0, TOWN, True)]
        xload = {}
        tpose = {}
        xfmdv = {}

        spillv_s = {}
        spillk_s = {}
        last_pe_before = 0

        for hi, (toff, TH, own) in enumerate(halves):
            NTT = TH // 128
            NTC = max(1, TH // 512)
            TCB = TH // NTC
            for tt in range(NTT):
                buf = tt % 2
                if tt >= 2:
                    w = [("pe", tpose[(hi, tt - 2)])]
                elif hi > 0:
                    w = [("pe", tpose[(hi - 1, halves[0][1] // 128 - 2 + tt)])]
                else:
                    w = []

                xload[(hi, tt)] = self.dma(
                    "gpsimd", f"gx{tt % 2}",
                    xtm_t[buf], x_h.ap()[toff + tt * 128: toff + (tt + 1) * 128, :],
                    waits=w)
                pdst = PAb if tt % 2 == 0 else PBb
                for dc in range(ND):
                    wts = []
                    if dc == 0:
                        wts = [(f"gx{tt % 2}", xload[(hi, tt)])]
                        prev = None
                        if tt >= 2:
                            prev = (hi, tt - 2)
                        elif hi > 0:
                            prev = (hi - 1, halves[0][1] // 128 - 2 + tt)
                        if prev is not None:
                            wts.append(("dv", xfmdv[prev]))
                        if hi > 0 and tt < 2:
                            wts.append(("ac", self.val("ac")))
                    fin = (dc == ND - 1)
                    tpose_inc = ("pe", 1) if fin else None
                    tpose[(hi, tt)] = self.op(
                        "tensor",
                        lambda t, d=pdst, b=buf, dc=dc:
                        t.transpose(d[:, dc * 128:(dc + 1) * 128],
                                    xtm_t[b][:, dc * 128:(dc + 1) * 128],
                                    ident_bf[:, :]),
                        waits=wts, inc=tpose_inc) or tpose.get((hi, tt))
                xfmdv[(hi, tt)] = self.op(
                    "vector",
                    lambda v, d=pdst, tt=tt, TH=TH:
                    v.tensor_copy(
                        big0[:, 0:ND * TH].rearrange("p (dc t) -> p dc t", dc=ND)[:, :, tt * 128:(tt + 1) * 128],
                        d[:, 0:ND * 128].rearrange("p (dc t) -> p dc t", dc=ND)),
                    waits=[("pe", tpose[(hi, tt)])], inc=("dv", 1))
            all_tp_dv = self.val("dv")
            if dbg and own and not c.get("skip_dumps"):
                self.dma("sync", "sd", dbgx_h.ap(), big0[:, 0:ND * TOWN],
                         waits=[("dv", all_tp_dv)])

            vm_ms = {}
            if hi == 0:
                vevac_all = {}
            vevac_ms = vevac_all.setdefault(hi, {})
            vstrip_pe = {}
            for vs in range(0 if not c.get("skip_v") else 0, 0 if c.get("skip_v") else NVS):
                sb = vs % 2
                if vs >= 2:
                    lw = [("pe", vstrip_pe[vs - 2])]
                elif hi > 0:
                    lw = [("pe", prev_half_vpe[sb])]
                else:
                    lw = []
                hv = ND * VCH // 2
                wvg = [
                    ("gv0", self.dma(
                        "gpsimd", "gv0",
                        bigC[:, sb * ND * VCH: sb * ND * VCH + hv],
                        wv_h.ap()[vs, :, 0:hv], waits=lw)),
                    ("gv1", self.dma(
                        "gpsimd", "gv1",
                        bigC[:, sb * ND * VCH + hv:(sb + 1) * ND * VCH],
                        wv_h.ap()[vs, :, hv:2 * hv], waits=list(lw))),
                ]
                for tt in range(NTT):
                    slot = tt % 2
                    for dc in range(ND):
                        wts = []
                        if dc == 0:
                            wts = wvg + [("dv", all_tp_dv)]
                            if tt >= 2:
                                wts.append(("ac", vevac_ms[(vs, tt - 2)]))
                            elif vs > 0:
                                wts.append(("ac", vevac_ms[(vs - 1, NTT - 2 + tt)]))
                            elif hi > 0:
                                wts.append(("ac", prev_half_ac))
                        fin = (dc == ND - 1)
                        r = self.op(
                            "tensor",
                            lambda t, slot=slot, dc=dc, tt=tt, TH=TH, fin=fin, sb=sb:
                            t.matmul(PA[:, slot * 512: slot * 512 + VCH],
                                     big0[:, 0:ND * TH].rearrange("p (dc t) -> p dc t", dc=ND)[:, dc, tt * 128:(tt + 1) * 128],
                                     bigC[:, sb * ND * VCH:(sb + 1) * ND * VCH].rearrange("p (dc v) -> p dc v", dc=ND)[:, dc, :],
                                     start=(dc == 0), stop=fin),
                            waits=wts, inc=(("pe", 1) if fin else None))
                        if fin:
                            vm_ms[(vs, tt)] = r
                            if tt == NTT - 1:
                                vstrip_pe[vs] = r
                    ebuf = tt % 2
                    wts = [("pe", vm_ms[(vs, tt)])]
                    prevsp = None
                    if tt >= 2:
                        prevsp = (hi, vs, tt - 2)
                    elif vs > 0:
                        prevsp = (hi, vs - 1, NTT - 2 + tt)
                    elif hi > 0:
                        prevsp = (hi - 1, NVS - 1, halves[0][1] // 128 - 2 + tt)
                    if prevsp is not None:
                        wts.append((f"sv{tt % 2}", spillv_s[prevsp]))
                    vevac_ms[(vs, tt)] = self.op(
                        "scalar",
                        lambda sc, slot=slot, eb=ebuf:
                        sc.activation(vevac_t[eb], PA[:, slot * 512: slot * 512 + VCH], AF.Copy),
                        waits=wts, inc=("ac", 1))
                    nh = VCH // 128
                    spillv_s[(hi, vs, tt)] = self.dma(
                        "sync", f"sv{ebuf}",
                        agv_in.ap().rearrange("(h t) e -> h t e", e=128, t=TOWN)
                        [vs * nh:(vs + 1) * nh, tt * 128:(tt + 1) * 128, :]
                        .rearrange("h t e -> t h e"),
                        vevac_t[ebuf].rearrange("t (h e) -> t h e", e=128),
                        waits=[("ac", vevac_ms[(vs, tt)])])

            prev_half_vpe = {0: vstrip_pe.get(NVS - 2, self.val("pe")),
                             1: vstrip_pe.get(NVS - 1, self.val("pe"))}
            def emit_agv(g_):
                rg = [[2 * i, 2 * i + 1] for i in range(4)]
                spg = NVS // NAG
                wts = []
                for vs_ in range(g_ * spg, (g_ + 1) * spg):
                    for par in range(2):
                        key = (hi, vs_, NTT - 1 - par)
                        if key in spillv_s:
                            wts.append((f"sv{(NTT - 1 - par) % 2}", spillv_s[key]))
                self.op("gpsimd",
                        lambda g, g_=g_: g.collective_compute(
                            "AllGather", ALU.bypass, replica_groups=rg,
                            ins=[agv_in.ap()[g_ * HG * TOWN:(g_ + 1) * HG * TOWN, :].opt()],
                            outs=[agv_out[g_].ap().opt()]),
                        waits=wts, inc=(f"agv{g_}", 1))
            jlist = [("k", j) for j in range(ND)] + ([("q", j) for j in range(ND)] if (own and not c.get("skip_q")) else [])
            wbg = {}
            kq_mm = {}
            kq_ac = {}
            pend_ag = []
            for ji, (kind, jc) in enumerate(jlist):
                wbuf = ji % 2
                if NRANKS > 1 and not c.get("skip_v") and ji == 2:
                    for g_ in range(NAG):
                        emit_agv(g_)
                while pend_ag and pend_ag[0][1] <= ji - 3:
                    g_, ji0 = pend_ag.pop(0)
                    rg = [[2 * i, 2 * i + 1] for i in range(4)]
                    wts = [(f"sk{j_ % 2}", spillk_s[(hi, j_)])
                           for j_ in range(ji0 - HG + 1, ji0 + 1)]
                    self.op("gpsimd",
                            lambda g, g_=g_: g.collective_compute(
                                "AllGather", ALU.bypass, replica_groups=rg,
                                ins=[agk_in.ap()[g_ * HG * 128:(g_ + 1) * HG * 128, :].opt()],
                                outs=[agk_out[g_].ap().opt()]),
                            waits=wts, inc=(f"agk{g_}", 1))
                wts = [("pe", kq_mm[ji - 2])] if ji >= 2 else [("pe", self.val("pe"))]
                blk_i = jc if kind == "k" else (ND + jc)
                hw = ND * 64
                wbg[ji] = [
                    (f"gw{2 * wbuf}", self.dma(
                        "gpsimd", f"gw{2 * wbuf}",
                        wblkv[wbuf][:, 0:hw], wkq_h.ap()[blk_i, :, 0:hw],
                        waits=wts)),
                    (f"gw{2 * wbuf + 1}", self.dma(
                        "gpsimd", f"gw{2 * wbuf + 1}",
                        wblkv[wbuf][:, hw:2 * hw], wkq_h.ap()[blk_i, :, hw:2 * hw],
                        waits=list(wts))),
                ]
                pban = PA if ji % 2 == 0 else PB
                for dc in range(ND):
                    for tc in range(NTC):
                        wts = []
                        if dc == 0 and tc == 0:
                            wts = wbg[ji] + [("dv", all_tp_dv)]
                            if ji >= 2:
                                wts.append(("ac", kq_ac[ji - 2]))
                            elif not c.get("skip_v"):
                                wts.append(("ac", vevac_ms[(NVS - 1, NTT - 1)]))
                        fin = (dc == ND - 1 and tc == NTC - 1)
                        r = self.op(
                            "tensor",
                            lambda t, pb=pban, wbuf=wbuf, dc=dc, tc=tc, TCB=TCB, TH=TH:
                            t.matmul(pb[:, tc * TCB:(tc + 1) * TCB],
                                     wblkv[wbuf][:, dc * 128:(dc + 1) * 128],
                                     big0[:, 0:ND * TH].rearrange("p (dc t) -> p dc t", dc=ND)[:, dc, tc * TCB:(tc + 1) * TCB],
                                     start=(dc == 0), stop=(dc == ND - 1)),
                            waits=wts, inc=(("pe", 1) if fin else None))
                        if fin:
                            kq_mm[ji] = r
                bias_col = (ND + jc) if kind == "k" else jc
                if kind == "k":
                    kq_mm_last = kq_mm[ji]
                    kb = ji % 2
                    wts = [("pe", kq_mm[ji])]
                    if ji >= 2:
                        wts.append((f"sk{kb}", spillk_s[(hi, ji - 2)]))
                    elif hi > 0:
                        wts.append((f"sk{kb}", spillk_s[(hi - 1, ND - 2 + ji)]))
                    if c.get("kq_nobias"):
                        kq_ac[ji] = self.op(
                            "scalar",
                            lambda sc, pb=pban, kb=kb, TH=TH:
                            sc.activation(kqev_t[kb][:, 0:TH], pb[:, 0:TH], AF.Copy),
                            waits=wts, inc=("ac", 1))
                    else:
                        kq_ac[ji] = self.op(
                            "scalar",
                            lambda sc, pb=pban, bc=bias_col, kb=kb, TH=TH:
                            sc.activation(kqev_t[kb][:, 0:TH], pb[:, 0:TH], AF.Identity,
                                          bias=bqkvt_sb[:, bc:bc + 1]),
                            waits=wts, inc=("ac", 1))
                    spillk_s[(hi, ji)] = self.dma(
                        "sync", f"sk{kb}",
                        agk_in.ap()[jc * 128:(jc + 1) * 128, :], kqev_t[kb][:, 0:TH],
                        waits=[("ac", kq_ac[ji])])
                    if NRANKS > 1 and jc % HG == HG - 1:
                        pend_ag.append((jc // HG, ji))
                else:
                    kq_mm_last = kq_mm[ji]
                    kq_ac[ji] = self.op(
                        "scalar",
                        lambda sc, pb=pban, bc=bias_col, jc=jc, TH=TH:
                        sc.activation(big1[:, jc * TOWN: jc * TOWN + TH], pb[:, 0:TH], AF.Identity,
                                      bias=bqkvt_sb[:, bc:bc + 1]),
                        waits=[("pe", kq_mm[ji])], inc=("ac", 1))

            prev_half_ac = self.val("ac")

        if NRANKS > 1:
            for g_, ji0 in pend_ag:
                rg = [[2 * i, 2 * i + 1] for i in range(4)]
                wts = [(f"sk{j_ % 2}", spillk_s[(0, j_)])
                       for j_ in range(ji0 - HG + 1, ji0 + 1)]
                self.op("gpsimd",
                        lambda g, g_=g_: g.collective_compute(
                            "AllGather", ALU.bypass, replica_groups=rg,
                            ins=[agk_in.ap()[g_ * HG * 128:(g_ + 1) * HG * 128, :].opt()],
                            outs=[agk_out[g_].ap().opt()]),
                        waits=wts, inc=(f"agk{g_}", 1))
            pend_ag = []
        p1_pe = self.val("pe")
        p1_spill_finals = [(ch, self.val(ch)) for ch in
                           ("sv0", "sv1", "sk0", "sk1") if self.val(ch)]
        p1_ac = self.val("ac")
        if dbg and not c.get("skip_dumps"):
            self.dma("sync", "sd", dbgq_h.ap(), big1[:, :], waits=[("ac", p1_ac)])

        phase_limit = c.get("phase_limit", 99)
        if phase_limit < 2:
            fin = [(ch, self.val(ch)) for ch in
                   ("sv0", "sv1", "sk0", "sk1", "sd") if self.val(ch)]
            self.op("sync", lambda sy: sy.nop(), waits=fin)
            self.play()
            return nc
        kload = {}
        vload = {}
        headdone = {}
        avms = {}
        zm_prev = [None]
        acc_dv = [None]
        esdvs = {}
        prev_exp_final = None
        aca2_prev = [None]
        aca_hold = [None]
        mult_hold = [None]
        rza_hold = [None]
        bca_hold = [None]
        pend = {}

        def fire(key):
            f = pend.pop(key, None)
            if f is not None:
                f()

        def gslot(hd, qc, pp):
            return (((hd * NQC) + qc) * NP + pp) % NSL

        def slot_user_prev(hd, qc, pp):
            gid = (hd * NQC + qc) * NP + pp - NSL
            if gid < 0:
                return None
            c_, pp_ = divmod(gid, NP)
            hd_, qc_ = divmod(c_, NQC)
            return (hd_, qc_, pp_)
        for hd in range(H):
            buf = hd % 2
            wts = ([("pe", p1_pe)] + p1_spill_finals) if hd < 2 else [("pe", headdone[hd - 2])]
            if NRANKS > 1:
                g_ = hd // HG
                hh = hd % HG
                wts = wts + [(f"agk{g_}", 1), (f"agv{g_}", 1)]
                for r in range(NRANKS):
                    kload[hd] = self.dma(
                        "sync", f"kl{buf}",
                        kh_t[buf][:, r * TOWN:(r + 1) * TOWN],
                        agk_out[g_].ap()[r * HG * 128 + hh * 128: r * HG * 128 + (hh + 1) * 128, :],
                        waits=wts if r == 0 else [])
                    vload[hd] = self.dma(
                        "sync", f"vl{buf}",
                        vh_t[buf][:, r * TOWN:(r + 1) * TOWN].rearrange("p (kc f) -> p kc f", f=128),
                        agv_out[g_].ap()[(r * HG + hh) * TOWN:(r * HG + hh + 1) * TOWN, :]
                        .rearrange("(kc p) e -> p kc e", p=128),
                        waits=[])
            else:
                kload[hd] = self.dma(
                    "sync", f"kl{buf}", kh_t[buf][:, 0:TOWN],
                    agk_in.ap()[hd * 128:(hd + 1) * 128, :], waits=wts)
                vload[hd] = self.dma(
                    "sync", f"vl{buf}",
                    vh_t[buf][:, 0:TOWN].rearrange("p (kc f) -> p kc f", f=128),
                    agv_in.ap()[hd * TOWN:(hd + 1) * TOWN, :]
                    .rearrange("(kc p) e -> p kc e", p=128),
                    waits=[])
            for qc in range(NQC):
                exp_base = self.val("ac")
                pe_pair = {}

                def emit_avz(pp):
                    if pp % 2 == 1:
                        ew = [("ac", exp_base + pp + 1)]
                        sa = eslots[gslot(hd, qc, pp - 1)]
                        sb = eslots[gslot(hd, qc, pp)]
                        if pp == 1:
                            wdv = self.op(
                                "vector",
                                lambda v, sa=sa, sb=sb: v.tensor_add(
                                    eaw[:, :], sa[:, :], sb[:, :]),
                                waits=ew, inc=("dv", 1))
                        else:
                            self.op(
                                "vector",
                                lambda v, sa=sa, sb=sb: v.tensor_add(
                                    ewt[:, :], sa[:, :], sb[:, :]),
                                waits=ew, inc=("dv", 1))
                            wdv = self.op(
                                "vector",
                                lambda v: v.tensor_add(eaw[:, :], eaw[:, :], ewt[:, :]),
                                inc=("dv", 1))
                        esdvs[(hd, qc, pp - 1)] = esdvs[(hd, qc, pp)] = wdv
                        if pp == NP - 1:
                            fire("z")
                            fw = [("pe", zm_prev[0])] if zm_prev[0] is not None else []
                            acc_dv[0] = self.op(
                                "vector",
                                lambda v: v.tensor_add(
                                    esacc[:, :], eaw[:, 0:QB], eaw[:, QB:2 * QB]),
                                waits=fw, inc=("dv", 1))
                    for half_k in range(2):
                        kc = pp * 2 + half_k
                        wts = [("ac", exp_base + pp + 1)] if half_k == 0 else []
                        if kc == 0:
                            fire("aca2")
                            if aca2_prev[0] is not None:
                                wts.append(("dv", aca2_prev[0]))
                        fin_av = (half_k == 1)
                        r = self.op(
                            "tensor",
                            lambda t, kc=kc, buf=buf, esl=eslots[gslot(hd, qc, pp)]:
                            t.matmul(PB[:, 0:QB],
                                     vh_t[buf][:, kc * 128:(kc + 1) * 128],
                                     esl[:, (kc % 2) * QB:(kc % 2) * QB + QB],
                                     start=(kc == 0), stop=(kc == NK - 1)),
                            waits=wts, inc=(("pe", 1) if fin_av else None))
                        if fin_av:
                            avms[(hd, qc, pp)] = r

                def sslot(kc):
                    s6 = kc % 6
                    return PA[:, s6 * 512: s6 * 512 + QB] if s6 < 4 \
                        else PB[:, 1024 + (s6 - 4) * 512: 1024 + (s6 - 4) * 512 + QB]

                for pp in range(NP):
                    for half_k in range(2):
                        kc = pp * 2 + half_k
                        wts = []
                        if half_k == 0:
                            if pp == 0:
                                wts = [(f"kl{buf}", kload[hd]), (f"vl{buf}", vload[hd]), ("dv", ones_dv)]
                                if prev_exp_final is not None:
                                    wts.append(("ac", prev_exp_final))
                                if (hd, qc) == (0, 0):
                                    wts.append(("ac", p1_ac))
                            elif pp <= 2:
                                if prev_exp_final is not None:
                                    wts = [("ac", prev_exp_final)]
                            else:
                                wts = [("ac", exp_base + pp - 2)]
                        fin = (half_k == 1)
                        r = self.op(
                            "tensor",
                            lambda t, kc=kc, buf=buf, qc=qc, hd=hd:
                            t.matmul(sslot(kc),
                                     kh_t[buf][:, kc * 128:(kc + 1) * 128],
                                     big1[:, hd * TOWN + qc * QB: hd * TOWN + (qc + 1) * QB],
                                     start=True, stop=True),
                            waits=wts, inc=(("pe", 1) if fin else None))
                        if fin:
                            pe_pair[pp] = r
                    if pp == min(1, NP - 1):
                        fire("z")
                    if pp == 2:
                        fire("aca2")
                    if pp == 4:
                        fire("lzrz")
                    if pp == 7:
                        fire("bca")
                    if pp == 8:
                        fire("aca")
                    if pp >= 3:
                        emit_avz(pp - 3)
                    wts = [("pe", pe_pair[pp])]
                    pu = slot_user_prev(hd, qc, pp)
                    if pu is not None:
                        wts.append(("pe", avms[pu]))
                        wts.append(("dv", esdvs[pu]))
                    s6 = (2 * pp) % 6
                    psrc = (PA[:, :].rearrange("p (s q) -> p s q", s=4)[:, s6:s6 + 2, 0:QB]
                            if s6 < 4 else
                            PB[:, :].rearrange("p (s q) -> p s q", s=4)[:, 2:4, 0:QB])
                    self.op(
                        "scalar",
                        lambda sc, psrc=psrc, esl=eslots[gslot(hd, qc, pp)]:
                        sc.activation(esl[:, :], psrc, AF.Exp, scale=SM),
                        waits=wts, inc=("ac", 1))
                    if pp == 9:
                        fire("mult")
                for tail_pp in range(max(0, NP - 3), NP):
                    emit_avz(tail_pp)
                for key in ("z", "aca2", "lzrz", "bca", "aca", "mult"):
                    fire(key)
                prev_exp_final = exp_base + NP

                def _z(accv=acc_dv[0]):
                    zw = [("dv", accv)]
                    if aca_hold[0] is not None:
                        zw.append(("dv", aca_hold[0]))
                    zm_prev[0] = self.op(
                        "tensor",
                        lambda t: t.matmul(PB[0:1, 512:512 + QB],
                                           ones_col[:, :], esacc[:, :],
                                           start=True, stop=True),
                        waits=zw, inc=("pe", 1))

                def _aca2(avm=avms[(hd, qc, NP - 1)]):
                    aca2_prev[0] = self.op(
                        "vector",
                        lambda v: v.tensor_copy(av_sb[:, :], PB[:, 0:QB]),
                        waits=[("pe", avm)], inc=("dv", 1))

                def _lzrz():
                    def _rz(v):
                        with self.nc.allow_low_precision(
                                "softmax normalizer in bf16 (attn weights are bf16)"):
                            return v.reciprocal(rz_bf[:, :], PB[0:1, 512:512 + QB])
                    rza_hold[0] = self.op("vector", _rz,
                                          waits=[("pe", zm_prev[0])], inc=("dv", 1))

                def _bca():
                    bca_hold[0] = self.op(
                        "tensor",
                        lambda t: t.matmul(PB[:, 512:512 + QB], ones_row[:, :],
                                           rz_bf[:, :], start=True, stop=True),
                        waits=[("dv", rza_hold[0])], inc=("pe", 1))

                def _aca():
                    aca_hold[0] = self.op(
                        "vector",
                        lambda v: v.tensor_copy(rzb_sb[:, :], PB[:, 512:512 + QB]),
                        waits=[("pe", bca_hold[0])], inc=("dv", 1))

                def _mult(hd=hd, qc=qc):
                    mult_hold[0] = self.op(
                        "vector",
                        lambda v: v.tensor_tensor(
                            big0[:, hd * TOWN + qc * QB: hd * TOWN + (qc + 1) * QB],
                            av_sb[:, :], rzb_sb[:, :], ALU.mult),
                        inc=("dv", 1))
                pend.update(z=_z, aca2=_aca2, lzrz=_lzrz, bca=_bca,
                            aca=_aca, mult=_mult)
                dk = c.get("defer_keys")
                if dk is not None:
                    for key in ("z", "aca2", "lzrz", "bca", "aca", "mult"):
                        if key not in dk:
                            fire(key)
            headdone[hd] = self.val("pe")
        for key in ("z", "aca2", "lzrz", "bca", "aca", "mult"):
            fire(key)
        attn_all_dv = self.val("dv")
        attn_all_pe = self.val("pe")
        if dbg:
            self.dma("sync", "sd", dbga_h.ap(), big0[:, 0:H * TOWN],
                     waits=[("dv", attn_all_dv)])

        if phase_limit < 3:
            fin = [(ch, self.val(ch)) for ch in
                   ("sv0", "sv1", "sk0", "sk1", "sd") if self.val(ch)]
            self.op("sync", lambda sy: sy.nop(), waits=fin)
            self.play()
            return nc
        oproj = {}
        oevac_ms = {}
        oevac_t = [oevac, es4_sb]
        pend_otr = []
        pend_ftr = []
        otr = {}
        ocp = {}
        b2_dv = {}
        xf32 = {}
        r1_dv = {}
        h1_dv = {}
        sq_ac = {}
        ln1_dv = {}
        pend_ln1 = []

        def otm_dst(qc, oc):
            if qc % 2 == 0:
                return bigC[:, 0:NQTC * D].rearrange(
                    "p (qt d) -> p qt d", d=D)[:, :, oc * 128:(oc + 1) * 128]
            return big0[:, oc * TOWN + (qc - 1) * QB: oc * TOWN + qc * QB] \
                .rearrange("p (qt e) -> p qt e", e=128)

        def otm_row(qc, qt_):
            if qc % 2 == 0:
                return bigC[:, 0:NQTC * D].rearrange(
                    "p (qt d) -> p qt d", d=D)[:, qt_, :]
            return big0[:, 0:ND * TOWN].rearrange(
                "p (dc t) -> p dc t", dc=ND)[:, :, (qc - 1) * QB + qt_ * 128:
                                             (qc - 1) * QB + (qt_ + 1) * 128]

        def emit_ln1_qt(qc, qt_):
            qt = qc * NQTC + qt_
            xb = qt % 2
            if qt == 0:
                wts = [("pe", headdone[H - 1])]
            else:
                wts = [("dv", r1_dv[qt - 2])] if qt >= 2 else []
            xf32[qt] = self.dma("sync", "sx", xf32_t2[xb],
                                x_h.ap()[qt * 128:(qt + 1) * 128, :], waits=wts)
            wts = [("sx", xf32[qt]), ("dv", ocp[(qc, ND - 1)])]
            if qt > 0:
                wts.append(("dv", sq_ac[qt - 1]))
            r1_dv[qt] = self.op(
                "vector",
                lambda v, row=otm_row(qc, qt_), xb=xb: v.scalar_tensor_tensor(
                    rscr_t, row,
                    1.0, xf32_t2[xb], ALU.mult, ALU.add,
                    accum_out=stat[:, 0:1]),
                waits=wts, inc=("dv", 1))
            sq_ac[qt] = self.op(
                "vector",
                lambda v: v.scalar_tensor_tensor(
                    e_buf[:, 0:D], rscr_t, 1.0, rscr_t, ALU.mult, ALU.mult,
                    accum_out=stat[:, 1:2]),
                waits=[("dv", r1_dv[qt])], inc=("dv", 1))
            self._ln_stats(sq_ac[qt], EPS, D, sem="dv")
            h1_dv[qt] = self.op(
                "vector",
                lambda v, qt=qt: v.tensor_scalar(
                    big1[:, qt * D:(qt + 1) * D], rscr_t,
                    stat[:, 2:3], stat[:, 7:8], ALU.subtract, ALU.mult),
                waits=[("ac", self.val("ac"))], inc=("dv", 1))
            ln1_dv[qt] = h1_dv[qt]

        for qc in range(NQC):
            for oc in range(ND):
                wbuf = oc % 2
                if (qc, oc) == (0, 0):
                    wts = [("pe", kq_mm_last)]
                elif qc == 0 and oc == 1:
                    wts = []
                else:
                    prev = (qc, oc - 2) if oc >= 2 else (qc - 1, ND - 2 + oc)
                    wts = [("pe", oproj[prev])]
                hw = ND * 64
                wg = [
                    (f"gw{2 * wbuf}", self.dma(
                        "gpsimd", f"gw{2 * wbuf}",
                        wblkv[wbuf][:, 0:hw], wout_h.ap()[oc, :, 0:hw],
                        waits=wts)),
                    (f"gw{2 * wbuf + 1}", self.dma(
                        "gpsimd", f"gw{2 * wbuf + 1}",
                        wblkv[wbuf][:, hw:2 * hw], wout_h.ap()[oc, :, hw:2 * hw],
                        waits=list(wts))),
                ]
                if qc == 0:
                    bm = None
                    for dc in range(ND):
                        wts = (wg + [("dv", bvdv)]) if dc == 0 else []
                        if dc == 0 and oc == 0:
                            wts.append(("dv", attn_all_dv))
                        if dc == 0 and oc > 0:
                            wts.append(("dv", b2_dv[oc - 1]))
                        fin = (dc == ND - 1)
                        r = self.op(
                            "tensor",
                            lambda t, dc=dc, wbuf=wbuf:
                            t.matmul(PB[:, 1024:1025],
                                     wblkv[wbuf][:, dc * 128:(dc + 1) * 128],
                                     bv_bf[:, dc:dc + 1],
                                     start=(dc == 0), stop=(dc == ND - 1)),
                            waits=wts, inc=(("pe", 1) if fin else None))
                        if fin:
                            bm = r
                    b2_dv[oc] = self.op(
                        "vector",
                        lambda v, oc=oc: v.tensor_add(
                            b2out_sb[:, oc:oc + 1], boutt_sb[:, oc:oc + 1],
                            PB[:, 1024:1025]),
                        waits=[("pe", bm)], inc=("dv", 1))
                for hd in range(H):
                    wts = []
                    if hd == 0:
                        wts = wg + [("dv", attn_all_dv)]
                        if qc > 0 or oc >= 4:
                            prev = (qc, oc - 4) if oc >= 4 else (qc - 1, ND - 4 + oc)
                            wts.append(("ac", oevac_ms[prev]))
                    fin = (hd == H - 1)
                    r = self.op(
                        "tensor",
                        lambda t, oc=oc, hd=hd, qc=qc, wbuf=wbuf, fin=fin:
                        t.matmul(PA[:, (oc % 4) * 512:(oc % 4) * 512 + QB],
                                 wblkv[wbuf][:, hd * 128:(hd + 1) * 128],
                                 big0[:, hd * TOWN + qc * QB: hd * TOWN + (qc + 1) * QB],
                                 start=(hd == 0), stop=fin),
                        waits=wts, inc=(("pe", 1) if fin else None))
                    if fin:
                        oproj[(qc, oc)] = r
                wts = [("pe", oproj[(qc, oc)]), ("dv", b2_dv[oc])]
                if qc > 0 or oc >= 2:
                    prev = (qc, oc - 2) if oc >= 2 else (qc - 1, ND - 2 + oc)
                    wts.append(("pe", otr[prev]))
                oevac_ms[(qc, oc)] = self.op(
                    "scalar",
                    lambda sc, oc=oc: sc.activation(
                        oevac_t[oc % 2][:, 0:QB],
                        PA[:, (oc % 4) * 512:(oc % 4) * 512 + QB],
                        AF.Identity, bias=b2out_sb[:, oc:oc + 1]),
                    waits=wts, inc=("ac", 1))

                def emit_otr(qc, oc):
                    for qt_ in range(NQTC):
                        wts = [("ac", oevac_ms[(qc, oc)])] if qt_ == 0 else []
                        if qt_ == 0 and (qc > 0 or oc >= 2):
                            prev = (qc, oc - 2) if oc >= 2 else (qc - 1, ND - 2 + oc)
                            wts.append(("dv", ocp[prev]))
                        fin = (qt_ == NQTC - 1)
                        r = self.op(
                            "tensor",
                            lambda t, oc=oc, qt_=qt_:
                            t.transpose(PBb[:, (oc % 2) * 1024 + qt_ * 128:(oc % 2) * 1024 + (qt_ + 1) * 128],
                                        oevac_t[oc % 2][:, qt_ * 128:(qt_ + 1) * 128],
                                        ident_bf[:, :]),
                            waits=wts, inc=(("pe", 1) if fin else None))
                        if fin:
                            otr[(qc, oc)] = r
                    ocp[(qc, oc)] = self.op(
                        "vector",
                        lambda v, dst=otm_dst(qc, oc), oc=oc: v.tensor_copy(
                            dst,
                            PBb[:, (oc % 2) * 1024:(oc % 2) * 1024 + QB].rearrange("p (qt e) -> p qt e", e=128)),
                        waits=[("pe", otr[(qc, oc)])], inc=("dv", 1))
                if pend_otr:
                    emit_otr(*pend_otr.pop(0))
                pend_otr.append((qc, oc))
                if oc in (8, 10, 12, 14) and pend_ln1:
                    emit_ln1_qt(*pend_ln1.pop(0))
            pend_ln1.extend((qc, qt_) for qt_ in range(NQTC))
        while pend_otr:
            emit_otr(*pend_otr.pop(0))
        while pend_ln1:
            emit_ln1_qt(*pend_ln1.pop(0))
        ln1_all_dv = self.val("dv")
        if dbg:
            self.dma("sync", "sd", dbgh_h.ap(), big1[:, :],
                     waits=[("dv", ln1_all_dv)])

        if phase_limit < 4:
            fin = [(ch, self.val(ch)) for ch in
                   ("sv0", "sv1", "sk0", "sk1", "sd", "sx") if self.val(ch)]
            self.op("sync", lambda sy: sy.nop(), waits=fin)
            self.play()
            return nc
        h1t = {}
        h1tdv = {}
        for qt in range(NQT):
            pdst = PAb if qt % 2 == 0 else PBb
            for dc in range(ND):
                wts = []
                if dc == 0:
                    wts = [("dv", ln1_dv[qt])]
                    if qt <= 1:
                        wts.append(("ac", oevac_ms[(NQC - 1, ND - 1)]))
                        wts.append(("dv", ocp[(NQC - 1, ND - 1)]))
                    if qt >= 2:
                        wts.append(("dv", h1tdv[qt - 2]))
                fin = (dc == ND - 1)
                r = self.op(
                    "tensor",
                    lambda t, d=pdst, qt=qt, dc=dc:
                    t.transpose(d[:, dc * 128:(dc + 1) * 128],
                                big1[:, qt * D + dc * 128: qt * D + (dc + 1) * 128],
                                ident_bf[:, :]),
                    waits=wts, inc=(("pe", 1) if fin else None))
                if fin:
                    h1t[qt] = r
            h1tdv[qt] = self.op(
                "vector",
                lambda v, d=pdst, qt=qt: v.tensor_copy(
                    big0[:, 0:ND * TOWN].rearrange("p (dc t) -> p dc t", dc=ND)[:, :, qt * 128:(qt + 1) * 128],
                    d[:, 0:ND * 128].rearrange("p (dc t) -> p dc t", dc=ND)),
                waits=[("pe", h1t[qt])], inc=("dv", 1))
        h1fm_dv = self.val("dv")

        fproj = {}
        fevac = {}
        ftr = {}
        fcp = {}
        gxa_ms = {}
        gexac_ms = {}
        lgtdv_ms = {}
        gtfdv_ms = {}
        xamm_ms = {}
        lgac_ms = {}
        r2_dv = {}
        sq2_ac = {}
        st_s = {}
        pend_ln2 = []
        gxa_t = [gxa_sb, rzb_sb]

        def pregame_a(qc):
            for dc in range(ND):
                wts = []
                if dc == 0:
                    wts = [("dv", h1fm_dv)]
                    if qc > 0:
                        wts.append(("dv", gxa_ms[qc - 1]))
                fin = (dc == ND - 1)
                xamm_ms[qc] = self.op(
                    "tensor",
                    lambda t, dc=dc, qc=qc, fin=fin:
                    t.matmul(PB[:, 0:QB],
                             aall_sb[:, dc * ER:(dc + 1) * ER],
                             big0[:, 0:ND * TOWN].rearrange("p (dc t) -> p dc t", dc=ND)[:, dc, qc * QB:(qc + 1) * QB],
                             start=(dc == 0), stop=fin),
                    waits=wts, inc=(("pe", 1) if fin else None))
            for dc in range(ND):
                wts = []
                if dc == 0 and qc > 0:
                    wts = [("ac", gexac_ms[qc - 1])]
                self.op(
                    "tensor",
                    lambda t, dc=dc, qc=qc:
                    t.matmul(PB[0:NE, 512:512 + QB],
                             gwt_sb[:, dc * NE:(dc + 1) * NE],
                             big0[:, 0:ND * TOWN].rearrange("p (dc t) -> p dc t", dc=ND)[:, dc, qc * QB:(qc + 1) * QB],
                             start=(dc == 0), stop=False),
                    waits=wts)
            lgmm = self.op(
                "tensor",
                lambda t: t.matmul(PB[0:NE, 512:512 + QB], gb_row[:, :], ones_rowq[:, :],
                                   start=False, stop=True),
                waits=[("dv", ones_dv)], inc=("pe", 1))
            lgac_ms[qc] = self.op(
                "scalar",
                lambda sc: sc.activation(lg_fm[:, :], PB[0:NE, 512:512 + QB], AF.Copy),
                waits=[("pe", lgmm)], inc=("ac", 1))

        def pregame_b(qc):
            for qt_ in range(NQTC):
                wts = [("ac", lgac_ms[qc])] if qt_ == 0 else []
                if qt_ == 0 and qc > 0:
                    wts.append(("dv", lgtdv_ms[qc - 1]))
                fin = (qt_ == NQTC - 1)
                lgtp = self.op(
                    "tensor",
                    lambda t, qt_=qt_:
                    t.transpose(PAb[:, 3072 + qt_ * 256: 3072 + qt_ * 256 + NE],
                                lg_fm[:, qt_ * 128:(qt_ + 1) * 128], ident_bf[0:NE, 0:NE]),
                    waits=wts, inc=(("pe", 1) if fin else None))
            lgtdv_ms[qc] = lgtdv = self.op(
                "vector",
                lambda v: v.tensor_copy(
                    lg_tm[:, :].rearrange("p (qt e) -> p qt e", e=NE),
                    PAb[:, 3072:4096].rearrange("p (qt x) -> p qt x", x=256)[:, 0:NQTC, 0:NE]),
                waits=[("pe", lgtp)], inc=("dv", 1))
            gat = {}
            for qt_ in range(NQTC):
                lg = lg_tm[:, qt_ * NE:(qt_ + 1) * NE]
                gt = gt_tm[:, qt_ * NE:(qt_ + 1) * NE]
                wts = [("dv", lgtdv)] if qt_ == 0 else []
                self.op("vector", lambda v, lg=lg: v.reduce_max(stat[:, 8:9], lg, AX.X), waits=wts)
                self.op("vector", lambda v, lg=lg, gt=gt: v.tensor_scalar(gt, lg, stat[:, 8:9], None, ALU.is_equal))
                self.op("vector", lambda v, gt=gt: v.tensor_scalar(gt, gt, -1e30, None, ALU.mult))
                self.op("vector", lambda v, lg=lg, gt=gt: v.tensor_add(gt, gt, lg))
                self.op("vector", lambda v, gt=gt: v.reduce_max(stat[:, 9:10], gt, AX.X))
                vdv = self.op("vector", lambda v: v.tensor_scalar(stat[:, 10:11], stat[:, 8:9], -1.0, None, ALU.mult),
                              inc=("dv", 1))
                self.op("scalar", lambda sc, lg=lg, gt=gt: sc.activation(gt, lg, AF.Exp, bias=stat[:, 10:11]),
                        waits=[("dv", vdv)])
                e2 = self.op("scalar", lambda sc: sc.activation(stat[:, 11:12], stat[:, 9:10], AF.Exp, bias=stat[:, 10:11]),
                             inc=("ac", 1))
                self.op("vector", lambda v: v.tensor_scalar(stat[:, 12:13], stat[:, 11:12], 1.0, None, ALU.add),
                        waits=[("ac", e2)])
                self.op("vector", lambda v: v.reciprocal(stat[:, 13:14], stat[:, 12:13]))
                self.op("vector", lambda v, lg=lg: v.tensor_scalar(lg, lg, stat[:, 9:10], None, ALU.is_ge))
                self.op("vector", lambda v, lg=lg, gt=gt: v.tensor_mul(gt, gt, lg))
                gat[qt_] = self.op("vector", lambda v, gt=gt: v.tensor_scalar(gt, gt, stat[:, 13:14], None, ALU.mult),
                                   inc=("dv", 1))
            self._gat_last = gat[NQTC - 1]

        def pregame_c(qc):
            gbdv = self.op("vector",
                           lambda v: v.tensor_copy(gt_bf[:, :], gt_tm[:, :]),
                           waits=[("dv", self._gat_last)], inc=("dv", 1))
            for qt_ in range(NQTC):
                wts = [("dv", gbdv)]
                if qt_ == 0 and qc > 0:
                    wts.append(("dv", gtfdv_ms[qc - 1]))
                fin = (qt_ == NQTC - 1)
                gttp = self.op(
                    "tensor",
                    lambda t, qt_=qt_:
                    t.transpose(PAb[0:NE, 2048 + qt_ * 256: 2048 + qt_ * 256 + 128],
                                gt_bf[:, qt_ * NE:(qt_ + 1) * NE], ident_bf[:, :]),
                    waits=wts, inc=(("pe", 1) if fin else None))
            gtfdv_ms[qc] = gtfdv = self.op(
                "vector",
                lambda v: v.tensor_copy(
                    gt_fm[:, :].rearrange("o (qt t) -> o qt t", t=128),
                    PAb[0:NE, 2048:3072].rearrange("o (qt x) -> o qt x", x=256)[:, 0:NQTC, 0:128]),
                waits=[("pe", gttp)], inc=("dv", 1))
            gexmm = self.op("tensor",
                            lambda t: t.matmul(PB[:, 512:512 + QB], e16_bf[:, :], gt_fm[:, :],
                                               start=True, stop=True),
                            waits=[("dv", gtfdv)], inc=("pe", 1))
            gexac_ms[qc] = gexac = self.op(
                "scalar",
                lambda sc: sc.activation(gexp_sb[:, :], PB[:, 512:512 + QB], AF.Copy),
                waits=[("pe", gexmm)], inc=("ac", 1))
            gxa_ms[qc] = self.op(
                "vector",
                lambda v, qc=qc: v.tensor_tensor(gxa_t[qc % 2][:, :], PB[:, 0:QB],
                                                 gexp_sb[:, :], ALU.mult),
                waits=[("ac", gexac), ("pe", xamm_ms[qc])], inc=("dv", 1))

        def ftm_dst(qc, oc):
            if qc % 2 == 0:
                return bigC[:, 0:NQTC * D].rearrange(
                    "p (qt d) -> p qt d", d=D)[:, :, oc * 128:(oc + 1) * 128]
            return big0[:, oc * TOWN + (qc - 1) * QB: oc * TOWN + qc * QB] \
                .rearrange("p (qt e) -> p qt e", e=128)

        def ftm_row(qc, qt_):
            if qc % 2 == 0:
                return bigC[:, 0:NQTC * D].rearrange(
                    "p (qt d) -> p qt d", d=D)[:, qt_, :]
            return big0[:, 0:ND * TOWN].rearrange(
                "p (dc t) -> p dc t", dc=ND)[:, :, (qc - 1) * QB + qt_ * 128:
                                             (qc - 1) * QB + (qt_ + 1) * 128]

        def emit_ln2_qt(qc, qt_):
            qt = qc * NQTC + qt_
            obuf = qt % 2
            wts = [("dv", fcp[(qc, ND - 1)])]
            if qt > 0:
                wts.append(("dv", sq2_ac[qt - 1]))
            r2_dv[qt] = self.op(
                "vector",
                lambda v, row=ftm_row(qc, qt_), qt=qt: v.scalar_tensor_tensor(
                    rscr_t, row,
                    1.0, big1[:, qt * D:(qt + 1) * D], ALU.mult, ALU.add,
                    accum_out=stat[:, 0:1]),
                waits=wts, inc=("dv", 1))
            sq2_ac[qt] = self.op(
                "vector",
                lambda v: v.scalar_tensor_tensor(
                    e_buf[:, 0:D], rscr_t, 1.0, rscr_t, ALU.mult, ALU.mult,
                    accum_out=stat[:, 1:2]),
                waits=[("dv", r2_dv[qt])], inc=("dv", 1))
            self._ln_stats(sq2_ac[qt], EPS, D, sem="dv")
            wts = [("ac", self.val("ac"))]
            if qt >= 2:
                wts.append((f"so{obuf}", st_s[qt - 2]))
            odv = self.op(
                "vector",
                lambda v, obuf=obuf: v.tensor_scalar(
                    outf_t[obuf], rscr_t,
                    stat[:, 2:3], stat[:, 7:8], ALU.subtract, ALU.mult),
                waits=wts, inc=("dv", 1))
            st_s[qt] = self.dma("sync", f"so{obuf}",
                                out_h.ap()[qt * 128:(qt + 1) * 128, :],
                                outf_t[obuf], waits=[("dv", odv)])

        pregame_a(0)
        pregame_b(0)
        pregame_c(0)
        for qc in range(NQC):
            for oc in range(ND):
                wbuf = oc % 2
                if qc == 0 and oc < 2:
                    wts = [("pe", oproj[(NQC - 1, ND - 1)])]
                else:
                    prev = (qc, oc - 2) if oc >= 2 else (qc - 1, ND - 2 + oc)
                    wts = [("pe", fproj[prev])]
                hw = ND * 64
                wg = [
                    (f"gw{2 * wbuf}", self.dma(
                        "gpsimd", f"gw{2 * wbuf}",
                        wblkv[wbuf][:, 0:hw], wffnt_h.ap()[oc, :, 0:hw],
                        waits=wts)),
                    (f"gw{2 * wbuf + 1}", self.dma(
                        "gpsimd", f"gw{2 * wbuf + 1}",
                        wblkv[wbuf][:, hw:2 * hw], wffnt_h.ap()[oc, :, hw:2 * hw],
                        waits=list(wts))),
                ]
                for dc in range(ND):
                    wts = []
                    if dc == 0:
                        wts = list(wg)
                        if qc == 0 and oc == 0:
                            wts.append(("dv", gtfdv_ms[0]))
                        if qc > 0 or oc >= 2:
                            prev = (qc, oc - 2) if oc >= 2 else (qc - 1, ND - 2 + oc)
                            wts.append(("ac", fevac[prev]))
                    self.op(
                        "tensor",
                        lambda t, oc=oc, dc=dc, qc=qc, wbuf=wbuf:
                        t.matmul(PA[:, (oc % 2) * 512:(oc % 2) * 512 + QB],
                                 wblkv[wbuf][:, dc * 128:(dc + 1) * 128],
                                 big0[:, 0:ND * TOWN].rearrange("p (dc t) -> p dc t", dc=ND)[:, dc, qc * QB:(qc + 1) * QB],
                                 start=(dc == 0), stop=False),
                        waits=wts)
                fproj[(qc, oc)] = self.op(
                    "tensor",
                    lambda t, oc=oc, qc=qc:
                    t.matmul(PA[:, (oc % 2) * 512:(oc % 2) * 512 + QB],
                             ball_sb[:, oc * 128:(oc + 1) * 128],
                             gxa_t[qc % 2][:, :], start=False, stop=True),
                    waits=[("dv", gxa_ms[qc])], inc=("pe", 1))
                wts = [("pe", fproj[(qc, oc)])]
                if qc > 0 or oc >= 2:
                    prev = (qc, oc - 2) if oc >= 2 else (qc - 1, ND - 2 + oc)
                    wts.append(("pe", ftr[prev]))
                fevac[(qc, oc)] = self.op(
                    "scalar",
                    lambda sc, oc=oc: sc.activation(
                        oevac_t[oc % 2][:, 0:QB],
                        PA[:, (oc % 2) * 512:(oc % 2) * 512 + QB],
                        AF.Identity, bias=bffnt_sb[:, oc:oc + 1]),
                    waits=wts, inc=("ac", 1))

                def emit_ftr(qc, oc):
                    for qt_ in range(NQTC):
                        wts = [("ac", fevac[(qc, oc)])] if qt_ == 0 else []
                        if qt_ == 0 and (qc > 0 or oc >= 2):
                            prev = (qc, oc - 2) if oc >= 2 else (qc - 1, ND - 2 + oc)
                            wts.append(("dv", fcp[prev]))
                        fin = (qt_ == NQTC - 1)
                        r = self.op(
                            "tensor",
                            lambda t, oc=oc, qt_=qt_:
                            t.transpose(PBb[:, 2048 + (oc % 2) * 1024 + qt_ * 128:
                                            2048 + (oc % 2) * 1024 + (qt_ + 1) * 128],
                                        oevac_t[oc % 2][:, qt_ * 128:(qt_ + 1) * 128],
                                        ident_bf[:, :]),
                            waits=wts, inc=(("pe", 1) if fin else None))
                        if fin:
                            ftr[(qc, oc)] = r
                    fcp[(qc, oc)] = self.op(
                        "vector",
                        lambda v, oc=oc, dst=ftm_dst(qc, oc): v.tensor_copy(
                            dst,
                            PBb[:, 2048 + (oc % 2) * 1024: 2048 + (oc % 2) * 1024 + QB].rearrange("p (qt e) -> p qt e", e=128)),
                        waits=[("pe", ftr[(qc, oc)])], inc=("dv", 1))
                if pend_ftr:
                    emit_ftr(*pend_ftr.pop(0))
                pend_ftr.append((qc, oc))
                if qc + 1 < NQC:
                    if oc == 2:
                        pregame_a(qc + 1)
                    elif oc == 4:
                        pregame_b(qc + 1)
                    elif oc == 6:
                        pregame_c(qc + 1)
                if oc in (8, 10, 12, 14) and pend_ln2:
                    emit_ln2_qt(*pend_ln2.pop(0))
            pend_ln2.extend((qc, qt_) for qt_ in range(NQTC))
        while pend_ftr:
            emit_ftr(*pend_ftr.pop(0))
        while pend_ln2:
            emit_ln2_qt(*pend_ln2.pop(0))

        fin = [(ch, self.val(ch)) for ch in ("so0", "so1", "sd") if self.val(ch)]
        self.op("sync", lambda sy: sy.nop(), waits=fin)
        self.play()
        return nc

    def _ln_stats(self, sqac, EPS, D, sem="ac"):
        stat = self._smalls[:, 0:16]
        self.op("vector", lambda v: v.tensor_scalar(stat[:, 2:3], stat[:, 0:1], 1.0 / D, None, ALU.mult),
                waits=[(sem, sqac)])
        self.op("vector", lambda v: v.tensor_scalar(stat[:, 3:4], stat[:, 1:2], 1.0 / D, None, ALU.mult))
        self.op("vector", lambda v: v.tensor_scalar(stat[:, 4:5], stat[:, 2:3], stat[:, 2:3], None, ALU.mult))
        vdv = self.op("vector", lambda v: v.tensor_sub(stat[:, 5:6], stat[:, 3:4], stat[:, 4:5]),
                      inc=("dv", 1))
        self.op("scalar", lambda sc: sc.activation(stat[:, 6:7], stat[:, 5:6], AF.Ln, bias=EPS),
                waits=[("dv", vdv)])
        self.op("scalar", lambda sc: sc.activation(stat[:, 7:8], stat[:, 6:7], AF.Exp, scale=-0.5),
                inc=("ac", 1))


def build_core_program(cfg):
    b = Builder(cfg)
    nc = b.build()
    return nc, b


def stage_inputs(L0_attn_x, L0_attn_w_qkv, L0_attn_b_qkv, L0_attn_w_out,
                 L0_attn_b_out, L0_ffn_W, L0_ffn_b, L0_ffn_lora_A,
                 L0_ffn_lora_B, L0_ffn_gate_w, L0_ffn_gate_b, cfg, n_cores=8):
    D, ND, NE, R, ER = cfg["D"], cfg["ND"], cfg["NE"], cfg["R"], cfg["ER"]
    TOWN = cfg["TOWN"]
    x = np.ascontiguousarray(np.asarray(L0_attn_x, dtype=np.float32))
    wqkv = np.ascontiguousarray(np.asarray(L0_attn_w_qkv, dtype=np.float32))
    bqkv = np.ascontiguousarray(np.asarray(L0_attn_b_qkv, dtype=np.float32))
    wout = np.ascontiguousarray(np.asarray(L0_attn_w_out, dtype=np.float32))
    bout = np.ascontiguousarray(np.asarray(L0_attn_b_out, dtype=np.float32))
    W = np.asarray(L0_ffn_W, dtype=np.float32)
    bf = np.ascontiguousarray(np.asarray(L0_ffn_b, dtype=np.float32))
    lA = np.asarray(L0_ffn_lora_A, dtype=np.float32)
    lB = np.asarray(L0_ffn_lora_B, dtype=np.float32)
    gw = np.asarray(L0_ffn_gate_w, dtype=np.float32)
    gb = np.ascontiguousarray(np.asarray(L0_ffn_gate_b, dtype=np.float32))

    VCH = cfg["VCH"]
    NVS = D // VCH

    def block_w(w):
        Dout = w.shape[1]
        nb = Dout // 128
        return np.ascontiguousarray(
            w.reshape(ND, 128, nb, 128).transpose(2, 1, 0, 3).reshape(nb, 128, ND * 128))

    wq_b = block_w(wqkv[:, 0:D])
    wk_b = block_w(wqkv[:, D:2 * D])
    wv_cols = wqkv[:, 2 * D:3 * D]
    wv_b = np.ascontiguousarray(
        wv_cols.reshape(ND, 128, NVS, VCH).transpose(2, 1, 0, 3).reshape(NVS, 128, ND * VCH))
    common = {
        "wkq": np.ascontiguousarray(np.concatenate([wk_b, wq_b], axis=0)),
        "wv": wv_b,
        "bqkvt": np.ascontiguousarray(bqkv.reshape(3 * ND, 128).T),
        "wout": block_w(wout),
        "boutt": np.ascontiguousarray(bout.reshape(ND, 128).T),
        "wffnt": block_w(W.T),
        "bffnt": np.ascontiguousarray(bf.reshape(ND, 128).T),
        "aall": np.ascontiguousarray(lA.transpose(2, 0, 1).reshape(D, ER)),
        "ball": np.ascontiguousarray(lB.transpose(0, 2, 1).reshape(ER, D)),
        "gwt": np.ascontiguousarray(gw.T),
        "gb": gb,
        "ident": np.eye(128, dtype=np.float32),
        "e16": np.ascontiguousarray(
            np.repeat(np.eye(NE, dtype=np.float32), R, axis=1)),
    }
    in_maps = []
    for c in range(n_cores):
        bidx, h = c // 2, c % 2
        xb = x[bidx]
        if cfg["TOTH"] > 0:
            xr = np.ascontiguousarray(xb[h * TOWN:(h + 1) * TOWN])
        else:
            xr = np.ascontiguousarray(xb[:TOWN])
        m = dict(common)
        m["x"] = xr
        in_maps.append(m)
    return in_maps


def kernel(L0_attn_x, L0_attn_w_qkv, L0_attn_b_qkv, L0_attn_w_out, L0_attn_b_out,
           L0_ffn_W, L0_ffn_b, L0_ffn_lora_A, L0_ffn_lora_B, L0_ffn_gate_w,
           L0_ffn_gate_b):
    cfg = CFG_FULL
    in_maps = stage_inputs(L0_attn_x, L0_attn_w_qkv, L0_attn_b_qkv,
                           L0_attn_w_out, L0_attn_b_out, L0_ffn_W, L0_ffn_b,
                           L0_ffn_lora_A, L0_ffn_lora_B, L0_ffn_gate_w,
                           L0_ffn_gate_b, cfg)
    nc, _ = build_core_program(cfg)
    res = run_bass_kernel_spmd(nc, in_maps, core_ids=list(range(8)))
    out = np.empty((4, 4096, 2048), dtype=np.float32)
    for c in range(8):
        bidx, h = c // 2, c % 2
        out[bidx, h * 2048:(h + 1) * 2048, :] = res.results[c]["out"]
    return out


# version 39
# speedup vs baseline: 1.0285x; 1.0285x over previous
import math
import numpy as np
import concourse.bass as bass
import concourse.mybir as mybir
from concourse.bass_utils import run_bass_kernel_spmd

F32 = mybir.dt.float32
BF16 = mybir.dt.bfloat16
AF = mybir.ActivationFunctionType
ALU = mybir.AluOpType
AX = mybir.AxisListType

ENGINES = ("sync", "gpsimd", "tensor", "scalar", "vector")


def make_cfg(D=2048, TOWN=2048, TOTH=2048):
    c = {}
    c["D"] = D
    c["ND"] = D // 128
    c["H"] = D // 128
    c["TOWN"] = TOWN
    c["TOTH"] = TOTH
    c["TALL"] = TOWN + TOTH
    c["QB"] = min(512, TOWN)
    c["NQC"] = TOWN // c["QB"]
    c["NQT"] = TOWN // 128
    c["NK"] = c["TALL"] // 128
    c["NE"] = 8
    c["R"] = 16
    c["ER"] = 128
    c["EPS"] = 1e-5
    c["SM"] = 1.0 / math.sqrt(128.0)
    c["VCH"] = max(128, D // 8)
    c["NRANKS"] = 2 if TOTH > 0 else 1
    return c


CFG_FULL = make_cfg()


class Builder:
    def __init__(self, cfg):
        self.cfg = cfg
        self.nc = bass.Bass("TRN2", target_bir_lowering=False,
                            dynamic_dma_scratch_size=16384)
        self.plan = {e: [] for e in ENGINES}
        self.sems = {}
        self.cnt = {}

    def _sem(self, name):
        if name not in self.sems:
            self.sems[name] = self.nc.alloc_semaphore(name)
            self.cnt[name] = 0
        return self.sems[name]

    ENG_SEM = {"vector": "dv", "scalar": "ac"}

    def op(self, eng, fn, waits=(), inc=None):
        ws = [(s, v) for (s, v) in waits if v and v > 0]
        es = self.ENG_SEM.get(eng)
        if es is not None:
            prev = self.cnt.get(es, 0)
            if prev:
                ws.append((es, prev))
            if inc is None:
                inc = (es, 1)
            else:
                assert inc[0] == es, (eng, inc)
        if inc is not None:
            self._sem(inc[0])
            self.cnt[inc[0]] += inc[1]
        for s, _ in ws:
            self._sem(s)
        self.plan[eng].append((fn, tuple(ws), inc))
        return self.cnt[inc[0]] if inc is not None else None

    def val(self, name):
        return self.cnt.get(name, 0)

    def dma(self, eng, chan, dst, src, waits=()):
        ws = list(waits)
        prev = self.cnt.get(chan, 0)
        if prev:
            ws.append((chan, prev))
        return self.op(eng, lambda g, d=dst, s=src: g.dma_start(d, s),
                       waits=ws, inc=(chan, 16))

    def play(self):
        with self.nc.Block() as block:
            for e in ENGINES:
                ops = self.plan[e]
                if not ops:
                    continue

                def mk(ops=ops):
                    def body(eng):
                        for fn, waits, inc in ops:
                            for (s, v) in waits:
                                eng.wait_ge(self.sems[s], v)
                            inst = fn(eng)
                            if inc is not None:
                                inst.then_inc(self.sems[inc[0]], inc[1])
                    return body

                getattr(block, e)(mk())

    def build(self):
        c = self.cfg
        nc = self.nc
        D, ND, H = c["D"], c["ND"], c["H"]
        TOWN, TOTH, TALL = c["TOWN"], c["TOTH"], c["TALL"]
        QB, NQC, NQT, NK = c["QB"], c["NQC"], c["NQT"], c["NK"]
        NE, R, ER = c["NE"], c["R"], c["ER"]
        EPS, SM, VCH = c["EPS"], c["SM"], c["VCH"]
        NP = NK // 2
        NQTC = QB // 128
        NVS = D // VCH

        x_h = nc.dram_tensor("x", [TOWN, D], F32, kind="ExternalInput")
        NVS_ = D // max(128, D // 8)
        wkq_h = nc.dram_tensor("wkq", [2 * ND, 128, D], F32, kind="ExternalInput")
        wv_h = nc.dram_tensor("wv", [NVS_, 128, (D // NVS_) * ND], F32, kind="ExternalInput")
        bqkvt_h = nc.dram_tensor("bqkvt", [128, 3 * ND], F32, kind="ExternalInput")
        wout_h = nc.dram_tensor("wout", [ND, 128, D], F32, kind="ExternalInput")
        boutt_h = nc.dram_tensor("boutt", [128, ND], F32, kind="ExternalInput")
        wffnt_h = nc.dram_tensor("wffnt", [ND, 128, D], F32, kind="ExternalInput")
        bffnt_h = nc.dram_tensor("bffnt", [128, ND], F32, kind="ExternalInput")
        aall_h = nc.dram_tensor("aall", [D, ER], F32, kind="ExternalInput")
        ball_h = nc.dram_tensor("ball", [ER, D], F32, kind="ExternalInput")
        gwt_h = nc.dram_tensor("gwt", [D, NE], F32, kind="ExternalInput")
        gb_h = nc.dram_tensor("gb", [NE], F32, kind="ExternalInput")
        ident_h = nc.dram_tensor("ident", [128, 128], F32, kind="ExternalInput")
        e16_h = nc.dram_tensor("e16", [NE, ER], F32, kind="ExternalInput")
        out_h = nc.dram_tensor("out", [TOWN, D], F32, kind="ExternalOutput")
        dbg = c.get("debug", False)
        NRANKS = c["NRANKS"]
        agk_in = nc.dram_tensor("agk_in", [H * 128, TOWN], BF16, kind="Internal")
        agv_in = nc.dram_tensor("agv_in", [H * TOWN, 128], BF16, kind="Internal")
        NAG = 4 if H % 4 == 0 else 1
        HG = H // NAG
        if NRANKS > 1:
            agk_out = [nc.dram_tensor(f"agk_out{g}", [NRANKS * HG * 128, TOWN], BF16, kind="Internal")
                       for g in range(NAG)]
            agv_out = [nc.dram_tensor(f"agv_out{g}", [NRANKS * HG * TOWN, 128], BF16, kind="Internal")
                       for g in range(NAG)]
        if dbg:
            dbgq_h = nc.dram_tensor("dbgq", [128, H * TOWN], BF16, kind="ExternalOutput")
            dbgx_h = nc.dram_tensor("dbgx", [128, ND * TOWN], BF16, kind="ExternalOutput")
            dbga_h = nc.dram_tensor("dbga", [128, H * TOWN], BF16, kind="ExternalOutput")
            dbgh_h = nc.dram_tensor("dbgh", [128, NQT * D], BF16, kind="ExternalOutput")

        A = nc.alloc_sbuf_tensor
        big0 = A("big0", [128, ND * max(TOWN, TOTH)], BF16)
        big1 = A("big1", [128, H * TOWN], BF16)
        bigC = A("bigC", [128, max(2 * ND * VCH, NQTC * D)], BF16)
        pool = A("pool", [128, max(4 * TALL, 8 * D)], BF16)
        e_buf = A("e_buf", [128, max(4 * QB, D)], BF16)
        oevac = A("oevac", [128, QB], BF16)
        gxa_sb = A("gxa_sb", [128, QB], BF16)
        gexp_sb = A("gexp_sb", [128, QB], BF16)
        rzb_sb = A("rzb_sb", [128, QB], BF16)
        rz_bf = A("rz_bf", [1, QB], BF16)
        es4_sb = A("es4_sb", [128, QB], BF16)
        esacc = oevac
        smalls = A("smalls", [128, 16], F32)
        lg_fm = A("lg_fm", [NE, QB], BF16)
        gt_bf = A("gt_bf", [128, NQTC * NE], BF16)
        lg_tm = A("lg_tm", [128, NQTC * NE], F32)
        gt_tm = A("gt_tm", [128, NQTC * NE], F32)
        gt_fm = A("gt_fm", [NE, QB], BF16)
        ident_bf = A("ident_bf", [128, 128], BF16)
        e16_bf = A("e16_bf", [NE, ER], BF16)
        ones_col = A("ones_col", [128, 1], BF16)
        ones_row = A("ones_row", [1, 128], BF16)
        ones_rowq = A("ones_rowq", [1, QB], BF16)
        bqkvt_sb = A("bqkvt_sb", [128, 3 * ND], F32)
        boutt_sb = A("boutt_sb", [128, ND], F32)
        b2out_sb = A("b2out_sb", [128, ND], F32)
        bv_bf = A("bv_bf", [128, ND], BF16)
        bffnt_sb = A("bffnt_sb", [128, ND], F32)
        gb_row = A("gb_row", [1, NE], BF16)
        aall_sb = A("aall_sb", [128, ND * ER], BF16)
        ball_sb = A("ball_sb", [128, D], BF16)
        gwt_sb = A("gwt_sb", [128, ND * NE], BF16)
        wblk = A("wblk", [128, 2 * ND * 128], BF16)
        eps_c = A("eps_c", [128, 1], F32)
        nc.const_aps.register(F32, EPS, eps_c[:, :])

        rem = nc.sbuf_bytes_remaining
        assert rem >= 0, f"SBUF over budget by {-rem} bytes"
        self.sbuf_remaining = rem
        self._smalls = smalls

        stat = smalls[:, 0:16]
        lnz_row = bigC[0:1, 0:2 * QB].bitcast(F32)
        W_bigc = max(2 * ND * VCH, NQTC * D)
        if W_bigc >= 14 * QB:
            eslots = [e_buf[:, 0:2 * QB], e_buf[:, 2 * QB:4 * QB],
                      bigC[:, 4 * QB:6 * QB], bigC[:, 6 * QB:8 * QB]]
            av_sb = bigC[:, 8 * QB:9 * QB]
            eaw = bigC[:, 10 * QB:12 * QB]
            ewt = bigC[:, 12 * QB:14 * QB]
        else:
            ex23 = A("ex23", [128, 4 * QB], BF16)
            eslots = [e_buf[:, 0:2 * QB], e_buf[:, 2 * QB:4 * QB],
                      ex23[:, 0:2 * QB], ex23[:, 2 * QB:4 * QB]]
            av_sb = A("av_sb", [128, QB], BF16)
            eaw = A("eaw", [128, 2 * QB], BF16)
            ewt = A("ewt", [128, 2 * QB], BF16)
        NSL = len(eslots)

        xtm_t = [pool[:, i * D:(i + 1) * D] for i in range(2)]
        vevac_t = [pool[:, 2 * D + i * VCH: 2 * D + (i + 1) * VCH] for i in range(2)]
        TMX = max(TOWN, TOTH)
        kq0 = 2 * D + 2 * VCH
        kqev_t = [pool[:, kq0 + i * TMX: kq0 + (i + 1) * TMX] for i in range(2)]
        kh_t = [pool[:, i * TALL:(i + 1) * TALL] for i in range(2)]
        vh_t = [pool[:, (2 + i) * TALL:(3 + i) * TALL] for i in range(2)]
        poolf = pool[:, :].bitcast(F32)
        xf32_t2 = [poolf[:, 0:D], poolf[:, D:2 * D]]
        outf_t = [poolf[:, D + i * D: D + (i + 1) * D] for i in range(2)]
        rscr_t = poolf[:, 3 * D: 4 * D]

        PA = nc.alloc_psum_tensor("PA", [128, 2048], F32)
        PB = nc.alloc_psum_tensor("PB", [128, 2048], F32)
        PAb = PA[:, :].bitcast(BF16)
        PBb = PB[:, :].bitcast(BF16)

        wblkv = [wblk[:, i * ND * 128:(i + 1) * ND * 128] for i in range(2)]


        self.dma("gpsimd", "cw", ident_bf[:, :], ident_h.ap())
        self.dma("gpsimd", "cw", e16_bf[:, :], e16_h.ap())
        self.dma("gpsimd", "cw", bqkvt_sb[:, :], bqkvt_h.ap())
        self.dma("gpsimd", "cw", boutt_sb[:, :], boutt_h.ap())
        self.dma("gpsimd", "cw", bffnt_sb[:, :], bffnt_h.ap())
        self.dma("gpsimd", "cw", gb_row[:, :], gb_h.ap().rearrange("(o e) -> o e", o=1))
        self.dma("gpsimd", "cw", aall_sb[:, :].rearrange("p (dc e) -> p dc e", dc=ND),
                 aall_h.ap().rearrange("(dc p) e -> p dc e", p=128))
        self.dma("gpsimd", "cw", ball_sb[:, :], ball_h.ap())
        self.dma("gpsimd", "cw", gwt_sb[:, :].rearrange("p (dc e) -> p dc e", dc=ND),
                 gwt_h.ap().rearrange("(dc p) e -> p dc e", p=128))
        consts_g = self.val("cw")
        m = self.op("vector", lambda v: v.memset(ones_col[:, :], 1.0))
        m = self.op("vector", lambda v: v.memset(ones_row[:, :], 1.0))
        ones_dv = self.op("vector", lambda v: v.memset(ones_rowq[:, :], 1.0),
                          inc=("dv", 1))
        self.op("vector", lambda v: v.memset(eps_c[:, :], EPS), inc=("dv", 1))
        bvdv = self.op("vector",
                       lambda v: v.tensor_copy(bv_bf[:, :], bqkvt_sb[:, 2 * ND:3 * ND]),
                       waits=[("cw", consts_g)], inc=("dv", 1))

        halves = [(0, TOWN, True)]
        xload = {}
        tpose = {}
        xfmdv = {}

        spillv_s = {}
        spillk_s = {}
        last_pe_before = 0

        for hi, (toff, TH, own) in enumerate(halves):
            NTT = TH // 128
            NTC = max(1, TH // 512)
            TCB = TH // NTC
            for tt in range(NTT):
                buf = tt % 2
                if tt >= 2:
                    w = [("pe", tpose[(hi, tt - 2)])]
                elif hi > 0:
                    w = [("pe", tpose[(hi - 1, halves[0][1] // 128 - 2 + tt)])]
                else:
                    w = []

                xload[(hi, tt)] = self.dma(
                    "gpsimd", f"gx{tt % 2}",
                    xtm_t[buf], x_h.ap()[toff + tt * 128: toff + (tt + 1) * 128, :],
                    waits=w)
                pdst = PAb if tt % 2 == 0 else PBb
                for dc in range(ND):
                    wts = []
                    if dc == 0:
                        wts = [(f"gx{tt % 2}", xload[(hi, tt)])]
                        prev = None
                        if tt >= 2:
                            prev = (hi, tt - 2)
                        elif hi > 0:
                            prev = (hi - 1, halves[0][1] // 128 - 2 + tt)
                        if prev is not None:
                            wts.append(("dv", xfmdv[prev]))
                        if hi > 0 and tt < 2:
                            wts.append(("ac", self.val("ac")))
                    fin = (dc == ND - 1)
                    tpose_inc = ("pe", 1) if fin else None
                    tpose[(hi, tt)] = self.op(
                        "tensor",
                        lambda t, d=pdst, b=buf, dc=dc:
                        t.transpose(d[:, dc * 128:(dc + 1) * 128],
                                    xtm_t[b][:, dc * 128:(dc + 1) * 128],
                                    ident_bf[:, :]),
                        waits=wts, inc=tpose_inc) or tpose.get((hi, tt))
                xfmdv[(hi, tt)] = self.op(
                    "vector",
                    lambda v, d=pdst, tt=tt, TH=TH:
                    v.tensor_copy(
                        big0[:, 0:ND * TH].rearrange("p (dc t) -> p dc t", dc=ND)[:, :, tt * 128:(tt + 1) * 128],
                        d[:, 0:ND * 128].rearrange("p (dc t) -> p dc t", dc=ND)),
                    waits=[("pe", tpose[(hi, tt)])], inc=("dv", 1))
            all_tp_dv = self.val("dv")
            if dbg and own and not c.get("skip_dumps"):
                self.dma("sync", "sd", dbgx_h.ap(), big0[:, 0:ND * TOWN],
                         waits=[("dv", all_tp_dv)])

            vm_ms = {}
            if hi == 0:
                vevac_all = {}
            vevac_ms = vevac_all.setdefault(hi, {})
            vstrip_pe = {}
            for vs in range(0 if not c.get("skip_v") else 0, 0 if c.get("skip_v") else NVS):
                sb = vs % 2
                if vs >= 2:
                    lw = [("pe", vstrip_pe[vs - 2])]
                elif hi > 0:
                    lw = [("pe", prev_half_vpe[sb])]
                else:
                    lw = []
                hv = ND * VCH // 2
                wvg = [
                    ("gv0", self.dma(
                        "gpsimd", "gv0",
                        bigC[:, sb * ND * VCH: sb * ND * VCH + hv],
                        wv_h.ap()[vs, :, 0:hv], waits=lw)),
                    ("gv1", self.dma(
                        "gpsimd", "gv1",
                        bigC[:, sb * ND * VCH + hv:(sb + 1) * ND * VCH],
                        wv_h.ap()[vs, :, hv:2 * hv], waits=list(lw))),
                ]
                for tt in range(NTT):
                    slot = tt % 2
                    for dc in range(ND):
                        wts = []
                        if dc == 0:
                            wts = wvg + [("dv", all_tp_dv)]
                            if tt >= 2:
                                wts.append(("ac", vevac_ms[(vs, tt - 2)]))
                            elif vs > 0:
                                wts.append(("ac", vevac_ms[(vs - 1, NTT - 2 + tt)]))
                            elif hi > 0:
                                wts.append(("ac", prev_half_ac))
                        fin = (dc == ND - 1)
                        r = self.op(
                            "tensor",
                            lambda t, slot=slot, dc=dc, tt=tt, TH=TH, fin=fin, sb=sb:
                            t.matmul(PA[:, slot * 512: slot * 512 + VCH],
                                     big0[:, 0:ND * TH].rearrange("p (dc t) -> p dc t", dc=ND)[:, dc, tt * 128:(tt + 1) * 128],
                                     bigC[:, sb * ND * VCH:(sb + 1) * ND * VCH].rearrange("p (dc v) -> p dc v", dc=ND)[:, dc, :],
                                     start=(dc == 0), stop=fin),
                            waits=wts, inc=(("pe", 1) if fin else None))
                        if fin:
                            vm_ms[(vs, tt)] = r
                            if tt == NTT - 1:
                                vstrip_pe[vs] = r
                    ebuf = tt % 2
                    wts = [("pe", vm_ms[(vs, tt)])]
                    prevsp = None
                    if tt >= 2:
                        prevsp = (hi, vs, tt - 2)
                    elif vs > 0:
                        prevsp = (hi, vs - 1, NTT - 2 + tt)
                    elif hi > 0:
                        prevsp = (hi - 1, NVS - 1, halves[0][1] // 128 - 2 + tt)
                    if prevsp is not None:
                        wts.append((f"sv{tt % 2}", spillv_s[prevsp]))
                    vevac_ms[(vs, tt)] = self.op(
                        "scalar",
                        lambda sc, slot=slot, eb=ebuf:
                        sc.activation(vevac_t[eb], PA[:, slot * 512: slot * 512 + VCH], AF.Copy),
                        waits=wts, inc=("ac", 1))
                    nh = VCH // 128
                    spillv_s[(hi, vs, tt)] = self.dma(
                        "sync", f"sv{ebuf}",
                        agv_in.ap().rearrange("(h t) e -> h t e", e=128, t=TOWN)
                        [vs * nh:(vs + 1) * nh, tt * 128:(tt + 1) * 128, :]
                        .rearrange("h t e -> t h e"),
                        vevac_t[ebuf].rearrange("t (h e) -> t h e", e=128),
                        waits=[("ac", vevac_ms[(vs, tt)])])

            prev_half_vpe = {0: vstrip_pe.get(NVS - 2, self.val("pe")),
                             1: vstrip_pe.get(NVS - 1, self.val("pe"))}
            def emit_agv(g_):
                rg = [[2 * i, 2 * i + 1] for i in range(4)]
                spg = NVS // NAG
                wts = []
                for vs_ in range(g_ * spg, (g_ + 1) * spg):
                    for par in range(2):
                        key = (hi, vs_, NTT - 1 - par)
                        if key in spillv_s:
                            wts.append((f"sv{(NTT - 1 - par) % 2}", spillv_s[key]))
                self.op("gpsimd",
                        lambda g, g_=g_: g.collective_compute(
                            "AllGather", ALU.bypass, replica_groups=rg,
                            ins=[agv_in.ap()[g_ * HG * TOWN:(g_ + 1) * HG * TOWN, :].opt()],
                            outs=[agv_out[g_].ap().opt()]),
                        waits=wts, inc=(f"agv{g_}", 1))
            jlist = [("k", j) for j in range(ND)] + ([("q", j) for j in range(ND)] if (own and not c.get("skip_q")) else [])
            wbg = {}
            kq_mm = {}
            kq_ac = {}
            pend_ag = []
            for ji, (kind, jc) in enumerate(jlist):
                wbuf = ji % 2
                if NRANKS > 1 and not c.get("skip_v") and ji == 2:
                    for g_ in range(NAG):
                        emit_agv(g_)
                while pend_ag and pend_ag[0][1] <= ji - 3:
                    g_, ji0 = pend_ag.pop(0)
                    rg = [[2 * i, 2 * i + 1] for i in range(4)]
                    wts = [(f"sk{j_ % 2}", spillk_s[(hi, j_)])
                           for j_ in range(ji0 - HG + 1, ji0 + 1)]
                    self.op("gpsimd",
                            lambda g, g_=g_: g.collective_compute(
                                "AllGather", ALU.bypass, replica_groups=rg,
                                ins=[agk_in.ap()[g_ * HG * 128:(g_ + 1) * HG * 128, :].opt()],
                                outs=[agk_out[g_].ap().opt()]),
                            waits=wts, inc=(f"agk{g_}", 1))
                wts = [("pe", kq_mm[ji - 2])] if ji >= 2 else [("pe", self.val("pe"))]
                blk_i = jc if kind == "k" else (ND + jc)
                hw = ND * 64
                wbg[ji] = [
                    (f"gw{2 * wbuf}", self.dma(
                        "gpsimd", f"gw{2 * wbuf}",
                        wblkv[wbuf][:, 0:hw], wkq_h.ap()[blk_i, :, 0:hw],
                        waits=wts)),
                    (f"gw{2 * wbuf + 1}", self.dma(
                        "gpsimd", f"gw{2 * wbuf + 1}",
                        wblkv[wbuf][:, hw:2 * hw], wkq_h.ap()[blk_i, :, hw:2 * hw],
                        waits=list(wts))),
                ]
                pban = PA if ji % 2 == 0 else PB
                for dc in range(ND):
                    for tc in range(NTC):
                        wts = []
                        if dc == 0 and tc == 0:
                            wts = wbg[ji] + [("dv", all_tp_dv)]
                            if ji >= 2:
                                wts.append(("ac", kq_ac[ji - 2]))
                            elif not c.get("skip_v"):
                                wts.append(("ac", vevac_ms[(NVS - 1, NTT - 1)]))
                        fin = (dc == ND - 1 and tc == NTC - 1)
                        r = self.op(
                            "tensor",
                            lambda t, pb=pban, wbuf=wbuf, dc=dc, tc=tc, TCB=TCB, TH=TH:
                            t.matmul(pb[:, tc * TCB:(tc + 1) * TCB],
                                     wblkv[wbuf][:, dc * 128:(dc + 1) * 128],
                                     big0[:, 0:ND * TH].rearrange("p (dc t) -> p dc t", dc=ND)[:, dc, tc * TCB:(tc + 1) * TCB],
                                     start=(dc == 0), stop=(dc == ND - 1)),
                            waits=wts, inc=(("pe", 1) if fin else None))
                        if fin:
                            kq_mm[ji] = r
                bias_col = (ND + jc) if kind == "k" else jc
                if kind == "k":
                    kq_mm_last = kq_mm[ji]
                    kb = ji % 2
                    wts = [("pe", kq_mm[ji])]
                    if ji >= 2:
                        wts.append((f"sk{kb}", spillk_s[(hi, ji - 2)]))
                    elif hi > 0:
                        wts.append((f"sk{kb}", spillk_s[(hi - 1, ND - 2 + ji)]))
                    if c.get("kq_nobias"):
                        kq_ac[ji] = self.op(
                            "scalar",
                            lambda sc, pb=pban, kb=kb, TH=TH:
                            sc.activation(kqev_t[kb][:, 0:TH], pb[:, 0:TH], AF.Copy),
                            waits=wts, inc=("ac", 1))
                    else:
                        kq_ac[ji] = self.op(
                            "scalar",
                            lambda sc, pb=pban, bc=bias_col, kb=kb, TH=TH:
                            sc.activation(kqev_t[kb][:, 0:TH], pb[:, 0:TH], AF.Identity,
                                          bias=bqkvt_sb[:, bc:bc + 1]),
                            waits=wts, inc=("ac", 1))
                    spillk_s[(hi, ji)] = self.dma(
                        "sync", f"sk{kb}",
                        agk_in.ap()[jc * 128:(jc + 1) * 128, :], kqev_t[kb][:, 0:TH],
                        waits=[("ac", kq_ac[ji])])
                    if NRANKS > 1 and jc % HG == HG - 1:
                        pend_ag.append((jc // HG, ji))
                else:
                    kq_mm_last = kq_mm[ji]
                    kq_ac[ji] = self.op(
                        "scalar",
                        lambda sc, pb=pban, bc=bias_col, jc=jc, TH=TH:
                        sc.activation(big1[:, jc * TOWN: jc * TOWN + TH], pb[:, 0:TH], AF.Identity,
                                      bias=bqkvt_sb[:, bc:bc + 1]),
                        waits=[("pe", kq_mm[ji])], inc=("ac", 1))

            prev_half_ac = self.val("ac")

        if NRANKS > 1:
            for g_, ji0 in pend_ag:
                rg = [[2 * i, 2 * i + 1] for i in range(4)]
                wts = [(f"sk{j_ % 2}", spillk_s[(0, j_)])
                       for j_ in range(ji0 - HG + 1, ji0 + 1)]
                self.op("gpsimd",
                        lambda g, g_=g_: g.collective_compute(
                            "AllGather", ALU.bypass, replica_groups=rg,
                            ins=[agk_in.ap()[g_ * HG * 128:(g_ + 1) * HG * 128, :].opt()],
                            outs=[agk_out[g_].ap().opt()]),
                        waits=wts, inc=(f"agk{g_}", 1))
            pend_ag = []
        p1_pe = self.val("pe")
        p1_spill_finals = [(ch, self.val(ch)) for ch in
                           ("sv0", "sv1", "sk0", "sk1") if self.val(ch)]
        p1_ac = self.val("ac")
        if dbg and not c.get("skip_dumps"):
            self.dma("sync", "sd", dbgq_h.ap(), big1[:, :], waits=[("ac", p1_ac)])

        phase_limit = c.get("phase_limit", 99)
        if phase_limit < 2:
            fin = [(ch, self.val(ch)) for ch in
                   ("sv0", "sv1", "sk0", "sk1", "sd") if self.val(ch)]
            self.op("sync", lambda sy: sy.nop(), waits=fin)
            self.play()
            return nc
        kload = {}
        vload = {}
        headdone = {}
        avms = {}
        zm_prev = [None]
        acc_dv = [None]
        esdvs = {}
        prev_exp_final = None
        aca2_prev = [None]
        aca_hold = [None]
        mult_hold = [None]
        rza_hold = [None]
        bca_hold = [None]
        pend = {}

        def fire(key):
            f = pend.pop(key, None)
            if f is not None:
                f()

        def gslot(hd, qc, pp):
            return (((hd * NQC) + qc) * NP + pp) % NSL

        def slot_user_prev(hd, qc, pp):
            gid = (hd * NQC + qc) * NP + pp - NSL
            if gid < 0:
                return None
            c_, pp_ = divmod(gid, NP)
            hd_, qc_ = divmod(c_, NQC)
            return (hd_, qc_, pp_)
        for hd in range(H):
            buf = hd % 2
            wts = ([("pe", p1_pe)] + p1_spill_finals) if hd < 2 else [("pe", headdone[hd - 2])]
            if NRANKS > 1:
                g_ = hd // HG
                hh = hd % HG
                wts = wts + [(f"agk{g_}", 1), (f"agv{g_}", 1)]
                for r in range(NRANKS):
                    kload[hd] = self.dma(
                        "sync", f"kl{buf}",
                        kh_t[buf][:, r * TOWN:(r + 1) * TOWN],
                        agk_out[g_].ap()[r * HG * 128 + hh * 128: r * HG * 128 + (hh + 1) * 128, :],
                        waits=wts if r == 0 else [])
                    vload[hd] = self.dma(
                        "sync", f"vl{buf}",
                        vh_t[buf][:, r * TOWN:(r + 1) * TOWN].rearrange("p (kc f) -> p kc f", f=128),
                        agv_out[g_].ap()[(r * HG + hh) * TOWN:(r * HG + hh + 1) * TOWN, :]
                        .rearrange("(kc p) e -> p kc e", p=128),
                        waits=[])
            else:
                kload[hd] = self.dma(
                    "sync", f"kl{buf}", kh_t[buf][:, 0:TOWN],
                    agk_in.ap()[hd * 128:(hd + 1) * 128, :], waits=wts)
                vload[hd] = self.dma(
                    "sync", f"vl{buf}",
                    vh_t[buf][:, 0:TOWN].rearrange("p (kc f) -> p kc f", f=128),
                    agv_in.ap()[hd * TOWN:(hd + 1) * TOWN, :]
                    .rearrange("(kc p) e -> p kc e", p=128),
                    waits=[])
            for qc in range(NQC):
                exp_base = self.val("ac")
                pe_pair = {}

                def emit_avz(pp):
                    if pp % 2 == 1:
                        ew = [("ac", exp_base + pp + 1)]
                        sa = eslots[gslot(hd, qc, pp - 1)]
                        sb = eslots[gslot(hd, qc, pp)]
                        if pp == 1:
                            wdv = self.op(
                                "vector",
                                lambda v, sa=sa, sb=sb: v.tensor_add(
                                    eaw[:, :], sa[:, :], sb[:, :]),
                                waits=ew, inc=("dv", 1))
                        else:
                            self.op(
                                "vector",
                                lambda v, sa=sa, sb=sb: v.tensor_add(
                                    ewt[:, :], sa[:, :], sb[:, :]),
                                waits=ew, inc=("dv", 1))
                            wdv = self.op(
                                "vector",
                                lambda v: v.tensor_add(eaw[:, :], eaw[:, :], ewt[:, :]),
                                inc=("dv", 1))
                        esdvs[(hd, qc, pp - 1)] = esdvs[(hd, qc, pp)] = wdv
                        if pp == NP - 1:
                            fire("z")
                            fw = [("pe", zm_prev[0])] if zm_prev[0] is not None else []
                            acc_dv[0] = self.op(
                                "vector",
                                lambda v: v.tensor_add(
                                    esacc[:, :], eaw[:, 0:QB], eaw[:, QB:2 * QB]),
                                waits=fw, inc=("dv", 1))
                    for half_k in range(2):
                        kc = pp * 2 + half_k
                        wts = [("ac", exp_base + pp + 1)] if half_k == 0 else []
                        if kc == 0:
                            fire("aca2")
                            if aca2_prev[0] is not None:
                                wts.append(("dv", aca2_prev[0]))
                        fin_av = (half_k == 1)
                        r = self.op(
                            "tensor",
                            lambda t, kc=kc, buf=buf, esl=eslots[gslot(hd, qc, pp)]:
                            t.matmul(PB[:, 0:QB],
                                     vh_t[buf][:, kc * 128:(kc + 1) * 128],
                                     esl[:, (kc % 2) * QB:(kc % 2) * QB + QB],
                                     start=(kc == 0), stop=(kc == NK - 1)),
                            waits=wts, inc=(("pe", 1) if fin_av else None))
                        if fin_av:
                            avms[(hd, qc, pp)] = r

                def sslot(kc):
                    s6 = kc % 6
                    return PA[:, s6 * 512: s6 * 512 + QB] if s6 < 4 \
                        else PB[:, 1024 + (s6 - 4) * 512: 1024 + (s6 - 4) * 512 + QB]

                for pp in range(NP):
                    for half_k in range(2):
                        kc = pp * 2 + half_k
                        wts = []
                        if half_k == 0:
                            if pp == 0:
                                wts = [(f"kl{buf}", kload[hd]), (f"vl{buf}", vload[hd]), ("dv", ones_dv)]
                                if prev_exp_final is not None:
                                    wts.append(("ac", prev_exp_final))
                                if (hd, qc) == (0, 0):
                                    wts.append(("ac", p1_ac))
                            elif pp <= 2:
                                if prev_exp_final is not None:
                                    wts = [("ac", prev_exp_final)]
                            else:
                                wts = [("ac", exp_base + pp - 2)]
                        fin = (half_k == 1)
                        r = self.op(
                            "tensor",
                            lambda t, kc=kc, buf=buf, qc=qc, hd=hd:
                            t.matmul(sslot(kc),
                                     kh_t[buf][:, kc * 128:(kc + 1) * 128],
                                     big1[:, hd * TOWN + qc * QB: hd * TOWN + (qc + 1) * QB],
                                     start=True, stop=True),
                            waits=wts, inc=(("pe", 1) if fin else None))
                        if fin:
                            pe_pair[pp] = r
                    if pp == min(1, NP - 1):
                        fire("z")
                    if pp == 2:
                        fire("aca2")
                    if pp == 4:
                        fire("lzrz")
                    if pp == 7:
                        fire("bca")
                    if pp == 8:
                        fire("aca")
                    if pp >= 3:
                        emit_avz(pp - 3)
                    wts = [("pe", pe_pair[pp])]
                    pu = slot_user_prev(hd, qc, pp)
                    if pu is not None:
                        wts.append(("pe", avms[pu]))
                        wts.append(("dv", esdvs[pu]))
                    s6 = (2 * pp) % 6
                    psrc = (PA[:, :].rearrange("p (s q) -> p s q", s=4)[:, s6:s6 + 2, 0:QB]
                            if s6 < 4 else
                            PB[:, :].rearrange("p (s q) -> p s q", s=4)[:, 2:4, 0:QB])
                    self.op(
                        "scalar",
                        lambda sc, psrc=psrc, esl=eslots[gslot(hd, qc, pp)]:
                        sc.activation(esl[:, :], psrc, AF.Exp, scale=SM),
                        waits=wts, inc=("ac", 1))
                    if pp == 9:
                        fire("mult")
                for tail_pp in range(max(0, NP - 3), NP):
                    emit_avz(tail_pp)
                for key in ("z", "aca2", "lzrz", "bca", "aca", "mult"):
                    fire(key)
                prev_exp_final = exp_base + NP

                def _z(accv=acc_dv[0]):
                    zw = [("dv", accv)]
                    if aca_hold[0] is not None:
                        zw.append(("dv", aca_hold[0]))
                    zm_prev[0] = self.op(
                        "tensor",
                        lambda t: t.matmul(PB[0:1, 512:512 + QB],
                                           ones_col[:, :], esacc[:, :],
                                           start=True, stop=True),
                        waits=zw, inc=("pe", 1))

                def _aca2(avm=avms[(hd, qc, NP - 1)]):
                    aca2_prev[0] = self.op(
                        "vector",
                        lambda v: v.tensor_copy(av_sb[:, :], PB[:, 0:QB]),
                        waits=[("pe", avm)], inc=("dv", 1))

                def _lzrz():
                    def _rz(v):
                        with self.nc.allow_low_precision(
                                "softmax normalizer in bf16 (attn weights are bf16)"):
                            return v.reciprocal(rz_bf[:, :], PB[0:1, 512:512 + QB])
                    rza_hold[0] = self.op("vector", _rz,
                                          waits=[("pe", zm_prev[0])], inc=("dv", 1))

                def _bca():
                    bca_hold[0] = self.op(
                        "tensor",
                        lambda t: t.matmul(PB[:, 512:512 + QB], ones_row[:, :],
                                           rz_bf[:, :], start=True, stop=True),
                        waits=[("dv", rza_hold[0])], inc=("pe", 1))

                def _aca():
                    aca_hold[0] = self.op(
                        "vector",
                        lambda v: v.tensor_copy(rzb_sb[:, :], PB[:, 512:512 + QB]),
                        waits=[("pe", bca_hold[0])], inc=("dv", 1))

                def _mult(hd=hd, qc=qc):
                    mult_hold[0] = self.op(
                        "vector",
                        lambda v: v.tensor_tensor(
                            big0[:, hd * TOWN + qc * QB: hd * TOWN + (qc + 1) * QB],
                            av_sb[:, :], rzb_sb[:, :], ALU.mult),
                        inc=("dv", 1))
                pend.update(z=_z, aca2=_aca2, lzrz=_lzrz, bca=_bca,
                            aca=_aca, mult=_mult)
                dk = c.get("defer_keys")
                if dk is not None:
                    for key in ("z", "aca2", "lzrz", "bca", "aca", "mult"):
                        if key not in dk:
                            fire(key)
            headdone[hd] = self.val("pe")
        for key in ("z", "aca2", "lzrz", "bca", "aca", "mult"):
            fire(key)
        attn_all_dv = self.val("dv")
        attn_all_pe = self.val("pe")
        if dbg:
            self.dma("sync", "sd", dbga_h.ap(), big0[:, 0:H * TOWN],
                     waits=[("dv", attn_all_dv)])

        if phase_limit < 3:
            fin = [(ch, self.val(ch)) for ch in
                   ("sv0", "sv1", "sk0", "sk1", "sd") if self.val(ch)]
            self.op("sync", lambda sy: sy.nop(), waits=fin)
            self.play()
            return nc
        oproj = {}
        oevac_ms = {}
        oevac_t = [oevac, es4_sb]
        pend_otr = []
        pend_ftr = []
        otr = {}
        ocp = {}
        b2_dv = {}
        xf32 = {}
        r1_dv = {}
        h1_dv = {}
        sq_ac = {}
        ln1_dv = {}
        pend_ln1 = []

        def otm_dst(qc, oc):
            if qc % 2 == 0:
                return bigC[:, 0:NQTC * D].rearrange(
                    "p (qt d) -> p qt d", d=D)[:, :, oc * 128:(oc + 1) * 128]
            return big0[:, oc * TOWN + (qc - 1) * QB: oc * TOWN + qc * QB] \
                .rearrange("p (qt e) -> p qt e", e=128)

        def otm_row(qc, qt_):
            if qc % 2 == 0:
                return bigC[:, 0:NQTC * D].rearrange(
                    "p (qt d) -> p qt d", d=D)[:, qt_, :]
            return big0[:, 0:ND * TOWN].rearrange(
                "p (dc t) -> p dc t", dc=ND)[:, :, (qc - 1) * QB + qt_ * 128:
                                             (qc - 1) * QB + (qt_ + 1) * 128]

        def emit_ln1_qt(qc, qt_):
            qt = qc * NQTC + qt_
            xb = qt % 2
            if qt == 0:
                wts = [("pe", headdone[H - 1])]
            else:
                wts = [("dv", r1_dv[qt - 2])] if qt >= 2 else []
            xf32[qt] = self.dma("sync", "sx", xf32_t2[xb],
                                x_h.ap()[qt * 128:(qt + 1) * 128, :], waits=wts)
            wts = [("sx", xf32[qt]), ("dv", ocp[(qc, ND - 1)])]
            if qt > 0:
                wts.append(("dv", sq_ac[qt - 1]))
            r1_dv[qt] = self.op(
                "vector",
                lambda v, row=otm_row(qc, qt_), xb=xb: v.scalar_tensor_tensor(
                    rscr_t, row,
                    1.0, xf32_t2[xb], ALU.mult, ALU.add,
                    accum_out=stat[:, 0:1]),
                waits=wts, inc=("dv", 1))
            sq_ac[qt] = self.op(
                "vector",
                lambda v: v.scalar_tensor_tensor(
                    e_buf[:, 0:D], rscr_t, 1.0, rscr_t, ALU.mult, ALU.mult,
                    accum_out=stat[:, 1:2]),
                waits=[("dv", r1_dv[qt])], inc=("dv", 1))
            self._ln_stats(sq_ac[qt], EPS, D, sem="dv")
            h1_dv[qt] = self.op(
                "vector",
                lambda v, qt=qt: v.tensor_scalar(
                    big1[:, qt * D:(qt + 1) * D], rscr_t,
                    stat[:, 2:3], stat[:, 7:8], ALU.subtract, ALU.mult),
                waits=[("ac", self.val("ac"))], inc=("dv", 1))
            ln1_dv[qt] = h1_dv[qt]

        for qc in range(NQC):
            for oc in range(ND):
                wbuf = oc % 2
                if (qc, oc) == (0, 0):
                    wts = [("pe", kq_mm_last)]
                elif qc == 0 and oc == 1:
                    wts = []
                else:
                    prev = (qc, oc - 2) if oc >= 2 else (qc - 1, ND - 2 + oc)
                    wts = [("pe", oproj[prev])]
                wg = self.dma(
                    "gpsimd", f"gw{wbuf}",
                    wblkv[wbuf], wout_h.ap()[oc, :, :],
                    waits=wts)
                if qc == 0:
                    bm = None
                    for dc in range(ND):
                        wts = [(f"gw{wbuf}", wg), ("dv", bvdv)] if dc == 0 else []
                        if dc == 0 and oc == 0:
                            wts.append(("dv", attn_all_dv))
                        if dc == 0 and oc > 0:
                            wts.append(("dv", b2_dv[oc - 1]))
                        fin = (dc == ND - 1)
                        r = self.op(
                            "tensor",
                            lambda t, dc=dc, wbuf=wbuf:
                            t.matmul(PB[:, 1024:1025],
                                     wblkv[wbuf][:, dc * 128:(dc + 1) * 128],
                                     bv_bf[:, dc:dc + 1],
                                     start=(dc == 0), stop=(dc == ND - 1)),
                            waits=wts, inc=(("pe", 1) if fin else None))
                        if fin:
                            bm = r
                    b2_dv[oc] = self.op(
                        "vector",
                        lambda v, oc=oc: v.tensor_add(
                            b2out_sb[:, oc:oc + 1], boutt_sb[:, oc:oc + 1],
                            PB[:, 1024:1025]),
                        waits=[("pe", bm)], inc=("dv", 1))
                for hd in range(H):
                    wts = []
                    if hd == 0:
                        wts = [(f"gw{wbuf}", wg), ("dv", attn_all_dv)]
                        if qc > 0 or oc >= 4:
                            prev = (qc, oc - 4) if oc >= 4 else (qc - 1, ND - 4 + oc)
                            wts.append(("ac", oevac_ms[prev]))
                    fin = (hd == H - 1)
                    r = self.op(
                        "tensor",
                        lambda t, oc=oc, hd=hd, qc=qc, wbuf=wbuf, fin=fin:
                        t.matmul(PA[:, (oc % 4) * 512:(oc % 4) * 512 + QB],
                                 wblkv[wbuf][:, hd * 128:(hd + 1) * 128],
                                 big0[:, hd * TOWN + qc * QB: hd * TOWN + (qc + 1) * QB],
                                 start=(hd == 0), stop=fin),
                        waits=wts, inc=(("pe", 1) if fin else None))
                    if fin:
                        oproj[(qc, oc)] = r
                wts = [("pe", oproj[(qc, oc)]), ("dv", b2_dv[oc])]
                if qc > 0 or oc >= 2:
                    prev = (qc, oc - 2) if oc >= 2 else (qc - 1, ND - 2 + oc)
                    wts.append(("pe", otr[prev]))
                oevac_ms[(qc, oc)] = self.op(
                    "scalar",
                    lambda sc, oc=oc: sc.activation(
                        oevac_t[oc % 2][:, 0:QB],
                        PA[:, (oc % 4) * 512:(oc % 4) * 512 + QB],
                        AF.Identity, bias=b2out_sb[:, oc:oc + 1]),
                    waits=wts, inc=("ac", 1))

                def emit_otr(qc, oc):
                    for qt_ in range(NQTC):
                        wts = [("ac", oevac_ms[(qc, oc)])] if qt_ == 0 else []
                        if qt_ == 0 and (qc > 0 or oc >= 2):
                            prev = (qc, oc - 2) if oc >= 2 else (qc - 1, ND - 2 + oc)
                            wts.append(("dv", ocp[prev]))
                        fin = (qt_ == NQTC - 1)
                        r = self.op(
                            "tensor",
                            lambda t, oc=oc, qt_=qt_:
                            t.transpose(PBb[:, (oc % 2) * 1024 + qt_ * 128:(oc % 2) * 1024 + (qt_ + 1) * 128],
                                        oevac_t[oc % 2][:, qt_ * 128:(qt_ + 1) * 128],
                                        ident_bf[:, :]),
                            waits=wts, inc=(("pe", 1) if fin else None))
                        if fin:
                            otr[(qc, oc)] = r
                    ocp[(qc, oc)] = self.op(
                        "vector",
                        lambda v, dst=otm_dst(qc, oc), oc=oc: v.tensor_copy(
                            dst,
                            PBb[:, (oc % 2) * 1024:(oc % 2) * 1024 + QB].rearrange("p (qt e) -> p qt e", e=128)),
                        waits=[("pe", otr[(qc, oc)])], inc=("dv", 1))
                if pend_otr:
                    emit_otr(*pend_otr.pop(0))
                pend_otr.append((qc, oc))
                if oc in (8, 10, 12, 14) and pend_ln1:
                    emit_ln1_qt(*pend_ln1.pop(0))
            pend_ln1.extend((qc, qt_) for qt_ in range(NQTC))
        while pend_otr:
            emit_otr(*pend_otr.pop(0))
        while pend_ln1:
            emit_ln1_qt(*pend_ln1.pop(0))
        ln1_all_dv = self.val("dv")
        if dbg:
            self.dma("sync", "sd", dbgh_h.ap(), big1[:, :],
                     waits=[("dv", ln1_all_dv)])

        if phase_limit < 4:
            fin = [(ch, self.val(ch)) for ch in
                   ("sv0", "sv1", "sk0", "sk1", "sd", "sx") if self.val(ch)]
            self.op("sync", lambda sy: sy.nop(), waits=fin)
            self.play()
            return nc
        h1t = {}
        h1tdv = {}
        for qt in range(NQT):
            pdst = PAb if qt % 2 == 0 else PBb
            for dc in range(ND):
                wts = []
                if dc == 0:
                    wts = [("dv", ln1_dv[qt])]
                    if qt <= 1:
                        wts.append(("ac", oevac_ms[(NQC - 1, ND - 1)]))
                        wts.append(("dv", ocp[(NQC - 1, ND - 1)]))
                    if qt >= 2:
                        wts.append(("dv", h1tdv[qt - 2]))
                fin = (dc == ND - 1)
                r = self.op(
                    "tensor",
                    lambda t, d=pdst, qt=qt, dc=dc:
                    t.transpose(d[:, dc * 128:(dc + 1) * 128],
                                big1[:, qt * D + dc * 128: qt * D + (dc + 1) * 128],
                                ident_bf[:, :]),
                    waits=wts, inc=(("pe", 1) if fin else None))
                if fin:
                    h1t[qt] = r
            h1tdv[qt] = self.op(
                "vector",
                lambda v, d=pdst, qt=qt: v.tensor_copy(
                    big0[:, 0:ND * TOWN].rearrange("p (dc t) -> p dc t", dc=ND)[:, :, qt * 128:(qt + 1) * 128],
                    d[:, 0:ND * 128].rearrange("p (dc t) -> p dc t", dc=ND)),
                waits=[("pe", h1t[qt])], inc=("dv", 1))
        h1fm_dv = self.val("dv")

        fproj = {}
        fevac = {}
        ftr = {}
        fcp = {}
        gxa_ms = {}
        gexac_ms = {}
        lgtdv_ms = {}
        gtfdv_ms = {}
        xamm_ms = {}
        lgac_ms = {}
        r2_dv = {}
        sq2_ac = {}
        st_s = {}
        pend_ln2 = []
        gxa_t = [gxa_sb, rzb_sb]

        def pregame_a(qc):
            for dc in range(ND):
                wts = []
                if dc == 0:
                    wts = [("dv", h1fm_dv)]
                    if qc > 0:
                        wts.append(("dv", gxa_ms[qc - 1]))
                fin = (dc == ND - 1)
                xamm_ms[qc] = self.op(
                    "tensor",
                    lambda t, dc=dc, qc=qc, fin=fin:
                    t.matmul(PB[:, 0:QB],
                             aall_sb[:, dc * ER:(dc + 1) * ER],
                             big0[:, 0:ND * TOWN].rearrange("p (dc t) -> p dc t", dc=ND)[:, dc, qc * QB:(qc + 1) * QB],
                             start=(dc == 0), stop=fin),
                    waits=wts, inc=(("pe", 1) if fin else None))
            for dc in range(ND):
                wts = []
                if dc == 0 and qc > 0:
                    wts = [("ac", gexac_ms[qc - 1])]
                self.op(
                    "tensor",
                    lambda t, dc=dc, qc=qc:
                    t.matmul(PB[0:NE, 512:512 + QB],
                             gwt_sb[:, dc * NE:(dc + 1) * NE],
                             big0[:, 0:ND * TOWN].rearrange("p (dc t) -> p dc t", dc=ND)[:, dc, qc * QB:(qc + 1) * QB],
                             start=(dc == 0), stop=False),
                    waits=wts)
            lgmm = self.op(
                "tensor",
                lambda t: t.matmul(PB[0:NE, 512:512 + QB], gb_row[:, :], ones_rowq[:, :],
                                   start=False, stop=True),
                waits=[("dv", ones_dv)], inc=("pe", 1))
            lgac_ms[qc] = self.op(
                "scalar",
                lambda sc: sc.activation(lg_fm[:, :], PB[0:NE, 512:512 + QB], AF.Copy),
                waits=[("pe", lgmm)], inc=("ac", 1))

        def pregame_b(qc):
            for qt_ in range(NQTC):
                wts = [("ac", lgac_ms[qc])] if qt_ == 0 else []
                if qt_ == 0 and qc > 0:
                    wts.append(("dv", lgtdv_ms[qc - 1]))
                fin = (qt_ == NQTC - 1)
                lgtp = self.op(
                    "tensor",
                    lambda t, qt_=qt_:
                    t.transpose(PAb[:, 3072 + qt_ * 256: 3072 + qt_ * 256 + NE],
                                lg_fm[:, qt_ * 128:(qt_ + 1) * 128], ident_bf[0:NE, 0:NE]),
                    waits=wts, inc=(("pe", 1) if fin else None))
            lgtdv_ms[qc] = lgtdv = self.op(
                "vector",
                lambda v: v.tensor_copy(
                    lg_tm[:, :].rearrange("p (qt e) -> p qt e", e=NE),
                    PAb[:, 3072:4096].rearrange("p (qt x) -> p qt x", x=256)[:, 0:NQTC, 0:NE]),
                waits=[("pe", lgtp)], inc=("dv", 1))
            gat = {}
            for qt_ in range(NQTC):
                lg = lg_tm[:, qt_ * NE:(qt_ + 1) * NE]
                gt = gt_tm[:, qt_ * NE:(qt_ + 1) * NE]
                wts = [("dv", lgtdv)] if qt_ == 0 else []
                self.op("vector", lambda v, lg=lg: v.reduce_max(stat[:, 8:9], lg, AX.X), waits=wts)
                self.op("vector", lambda v, lg=lg, gt=gt: v.tensor_scalar(gt, lg, stat[:, 8:9], None, ALU.is_equal))
                self.op("vector", lambda v, gt=gt: v.tensor_scalar(gt, gt, -1e30, None, ALU.mult))
                self.op("vector", lambda v, lg=lg, gt=gt: v.tensor_add(gt, gt, lg))
                self.op("vector", lambda v, gt=gt: v.reduce_max(stat[:, 9:10], gt, AX.X))
                vdv = self.op("vector", lambda v: v.tensor_scalar(stat[:, 10:11], stat[:, 8:9], -1.0, None, ALU.mult),
                              inc=("dv", 1))
                self.op("scalar", lambda sc, lg=lg, gt=gt: sc.activation(gt, lg, AF.Exp, bias=stat[:, 10:11]),
                        waits=[("dv", vdv)])
                e2 = self.op("scalar", lambda sc: sc.activation(stat[:, 11:12], stat[:, 9:10], AF.Exp, bias=stat[:, 10:11]),
                             inc=("ac", 1))
                self.op("vector", lambda v: v.tensor_scalar(stat[:, 12:13], stat[:, 11:12], 1.0, None, ALU.add),
                        waits=[("ac", e2)])
                self.op("vector", lambda v: v.reciprocal(stat[:, 13:14], stat[:, 12:13]))
                self.op("vector", lambda v, lg=lg: v.tensor_scalar(lg, lg, stat[:, 9:10], None, ALU.is_ge))
                self.op("vector", lambda v, lg=lg, gt=gt: v.tensor_mul(gt, gt, lg))
                gat[qt_] = self.op("vector", lambda v, gt=gt: v.tensor_scalar(gt, gt, stat[:, 13:14], None, ALU.mult),
                                   inc=("dv", 1))
            self._gat_last = gat[NQTC - 1]

        def pregame_c(qc):
            gbdv = self.op("vector",
                           lambda v: v.tensor_copy(gt_bf[:, :], gt_tm[:, :]),
                           waits=[("dv", self._gat_last)], inc=("dv", 1))
            for qt_ in range(NQTC):
                wts = [("dv", gbdv)]
                if qt_ == 0 and qc > 0:
                    wts.append(("dv", gtfdv_ms[qc - 1]))
                fin = (qt_ == NQTC - 1)
                gttp = self.op(
                    "tensor",
                    lambda t, qt_=qt_:
                    t.transpose(PAb[0:NE, 2048 + qt_ * 256: 2048 + qt_ * 256 + 128],
                                gt_bf[:, qt_ * NE:(qt_ + 1) * NE], ident_bf[:, :]),
                    waits=wts, inc=(("pe", 1) if fin else None))
            gtfdv_ms[qc] = gtfdv = self.op(
                "vector",
                lambda v: v.tensor_copy(
                    gt_fm[:, :].rearrange("o (qt t) -> o qt t", t=128),
                    PAb[0:NE, 2048:3072].rearrange("o (qt x) -> o qt x", x=256)[:, 0:NQTC, 0:128]),
                waits=[("pe", gttp)], inc=("dv", 1))
            gexmm = self.op("tensor",
                            lambda t: t.matmul(PB[:, 512:512 + QB], e16_bf[:, :], gt_fm[:, :],
                                               start=True, stop=True),
                            waits=[("dv", gtfdv)], inc=("pe", 1))
            gexac_ms[qc] = gexac = self.op(
                "scalar",
                lambda sc: sc.activation(gexp_sb[:, :], PB[:, 512:512 + QB], AF.Copy),
                waits=[("pe", gexmm)], inc=("ac", 1))
            gxa_ms[qc] = self.op(
                "vector",
                lambda v, qc=qc: v.tensor_tensor(gxa_t[qc % 2][:, :], PB[:, 0:QB],
                                                 gexp_sb[:, :], ALU.mult),
                waits=[("ac", gexac), ("pe", xamm_ms[qc])], inc=("dv", 1))

        def ftm_dst(qc, oc):
            if qc % 2 == 0:
                return bigC[:, 0:NQTC * D].rearrange(
                    "p (qt d) -> p qt d", d=D)[:, :, oc * 128:(oc + 1) * 128]
            return big0[:, oc * TOWN + (qc - 1) * QB: oc * TOWN + qc * QB] \
                .rearrange("p (qt e) -> p qt e", e=128)

        def ftm_row(qc, qt_):
            if qc % 2 == 0:
                return bigC[:, 0:NQTC * D].rearrange(
                    "p (qt d) -> p qt d", d=D)[:, qt_, :]
            return big0[:, 0:ND * TOWN].rearrange(
                "p (dc t) -> p dc t", dc=ND)[:, :, (qc - 1) * QB + qt_ * 128:
                                             (qc - 1) * QB + (qt_ + 1) * 128]

        def emit_ln2_qt(qc, qt_):
            qt = qc * NQTC + qt_
            obuf = qt % 2
            wts = [("dv", fcp[(qc, ND - 1)])]
            if qt > 0:
                wts.append(("dv", sq2_ac[qt - 1]))
            r2_dv[qt] = self.op(
                "vector",
                lambda v, row=ftm_row(qc, qt_), qt=qt: v.scalar_tensor_tensor(
                    rscr_t, row,
                    1.0, big1[:, qt * D:(qt + 1) * D], ALU.mult, ALU.add,
                    accum_out=stat[:, 0:1]),
                waits=wts, inc=("dv", 1))
            sq2_ac[qt] = self.op(
                "vector",
                lambda v: v.scalar_tensor_tensor(
                    e_buf[:, 0:D], rscr_t, 1.0, rscr_t, ALU.mult, ALU.mult,
                    accum_out=stat[:, 1:2]),
                waits=[("dv", r2_dv[qt])], inc=("dv", 1))
            self._ln_stats(sq2_ac[qt], EPS, D, sem="dv")
            wts = [("ac", self.val("ac"))]
            if qt >= 2:
                wts.append((f"so{obuf}", st_s[qt - 2]))
            odv = self.op(
                "vector",
                lambda v, obuf=obuf: v.tensor_scalar(
                    outf_t[obuf], rscr_t,
                    stat[:, 2:3], stat[:, 7:8], ALU.subtract, ALU.mult),
                waits=wts, inc=("dv", 1))
            st_s[qt] = self.dma("sync", f"so{obuf}",
                                out_h.ap()[qt * 128:(qt + 1) * 128, :],
                                outf_t[obuf], waits=[("dv", odv)])

        pregame_a(0)
        pregame_b(0)
        pregame_c(0)
        for qc in range(NQC):
            for oc in range(ND):
                wbuf = oc % 2
                if qc == 0 and oc < 2:
                    wts = [("pe", oproj[(NQC - 1, ND - 1)])]
                else:
                    prev = (qc, oc - 2) if oc >= 2 else (qc - 1, ND - 2 + oc)
                    wts = [("pe", fproj[prev])]
                wg = self.dma(
                    "gpsimd", f"gw{wbuf}",
                    wblkv[wbuf], wffnt_h.ap()[oc, :, :],
                    waits=wts)
                for dc in range(ND):
                    wts = []
                    if dc == 0:
                        wts = [(f"gw{wbuf}", wg)]
                        if qc == 0 and oc == 0:
                            wts.append(("dv", gtfdv_ms[0]))
                        if qc > 0 or oc >= 2:
                            prev = (qc, oc - 2) if oc >= 2 else (qc - 1, ND - 2 + oc)
                            wts.append(("ac", fevac[prev]))
                    self.op(
                        "tensor",
                        lambda t, oc=oc, dc=dc, qc=qc, wbuf=wbuf:
                        t.matmul(PA[:, (oc % 2) * 512:(oc % 2) * 512 + QB],
                                 wblkv[wbuf][:, dc * 128:(dc + 1) * 128],
                                 big0[:, 0:ND * TOWN].rearrange("p (dc t) -> p dc t", dc=ND)[:, dc, qc * QB:(qc + 1) * QB],
                                 start=(dc == 0), stop=False),
                        waits=wts)
                fproj[(qc, oc)] = self.op(
                    "tensor",
                    lambda t, oc=oc, qc=qc:
                    t.matmul(PA[:, (oc % 2) * 512:(oc % 2) * 512 + QB],
                             ball_sb[:, oc * 128:(oc + 1) * 128],
                             gxa_t[qc % 2][:, :], start=False, stop=True),
                    waits=[("dv", gxa_ms[qc])], inc=("pe", 1))
                wts = [("pe", fproj[(qc, oc)])]
                if qc > 0 or oc >= 2:
                    prev = (qc, oc - 2) if oc >= 2 else (qc - 1, ND - 2 + oc)
                    wts.append(("pe", ftr[prev]))
                fevac[(qc, oc)] = self.op(
                    "scalar",
                    lambda sc, oc=oc: sc.activation(
                        oevac_t[oc % 2][:, 0:QB],
                        PA[:, (oc % 2) * 512:(oc % 2) * 512 + QB],
                        AF.Identity, bias=bffnt_sb[:, oc:oc + 1]),
                    waits=wts, inc=("ac", 1))

                def emit_ftr(qc, oc):
                    for qt_ in range(NQTC):
                        wts = [("ac", fevac[(qc, oc)])] if qt_ == 0 else []
                        if qt_ == 0 and (qc > 0 or oc >= 2):
                            prev = (qc, oc - 2) if oc >= 2 else (qc - 1, ND - 2 + oc)
                            wts.append(("dv", fcp[prev]))
                        fin = (qt_ == NQTC - 1)
                        r = self.op(
                            "tensor",
                            lambda t, oc=oc, qt_=qt_:
                            t.transpose(PBb[:, 2048 + (oc % 2) * 1024 + qt_ * 128:
                                            2048 + (oc % 2) * 1024 + (qt_ + 1) * 128],
                                        oevac_t[oc % 2][:, qt_ * 128:(qt_ + 1) * 128],
                                        ident_bf[:, :]),
                            waits=wts, inc=(("pe", 1) if fin else None))
                        if fin:
                            ftr[(qc, oc)] = r
                    fcp[(qc, oc)] = self.op(
                        "vector",
                        lambda v, oc=oc, dst=ftm_dst(qc, oc): v.tensor_copy(
                            dst,
                            PBb[:, 2048 + (oc % 2) * 1024: 2048 + (oc % 2) * 1024 + QB].rearrange("p (qt e) -> p qt e", e=128)),
                        waits=[("pe", ftr[(qc, oc)])], inc=("dv", 1))
                if pend_ftr:
                    emit_ftr(*pend_ftr.pop(0))
                pend_ftr.append((qc, oc))
                if qc + 1 < NQC:
                    if oc == 2:
                        pregame_a(qc + 1)
                    elif oc == 4:
                        pregame_b(qc + 1)
                    elif oc == 6:
                        pregame_c(qc + 1)
                if oc in (8, 10, 12, 14) and pend_ln2:
                    emit_ln2_qt(*pend_ln2.pop(0))
            pend_ln2.extend((qc, qt_) for qt_ in range(NQTC))
        while pend_ftr:
            emit_ftr(*pend_ftr.pop(0))
        while pend_ln2:
            emit_ln2_qt(*pend_ln2.pop(0))

        fin = [(ch, self.val(ch)) for ch in ("so0", "so1", "sd") if self.val(ch)]
        self.op("sync", lambda sy: sy.nop(), waits=fin)
        self.play()
        return nc

    def _ln_stats(self, sqac, EPS, D, sem="ac"):
        stat = self._smalls[:, 0:16]
        self.op("vector", lambda v: v.tensor_scalar(stat[:, 2:3], stat[:, 0:1], 1.0 / D, None, ALU.mult),
                waits=[(sem, sqac)])
        self.op("vector", lambda v: v.tensor_scalar(stat[:, 3:4], stat[:, 1:2], 1.0 / D, None, ALU.mult))
        self.op("vector", lambda v: v.tensor_scalar(stat[:, 4:5], stat[:, 2:3], stat[:, 2:3], None, ALU.mult))
        vdv = self.op("vector", lambda v: v.tensor_sub(stat[:, 5:6], stat[:, 3:4], stat[:, 4:5]),
                      inc=("dv", 1))
        self.op("scalar", lambda sc: sc.activation(stat[:, 6:7], stat[:, 5:6], AF.Ln, bias=EPS),
                waits=[("dv", vdv)])
        self.op("scalar", lambda sc: sc.activation(stat[:, 7:8], stat[:, 6:7], AF.Exp, scale=-0.5),
                inc=("ac", 1))


def build_core_program(cfg):
    b = Builder(cfg)
    nc = b.build()
    return nc, b


def stage_inputs(L0_attn_x, L0_attn_w_qkv, L0_attn_b_qkv, L0_attn_w_out,
                 L0_attn_b_out, L0_ffn_W, L0_ffn_b, L0_ffn_lora_A,
                 L0_ffn_lora_B, L0_ffn_gate_w, L0_ffn_gate_b, cfg, n_cores=8):
    D, ND, NE, R, ER = cfg["D"], cfg["ND"], cfg["NE"], cfg["R"], cfg["ER"]
    TOWN = cfg["TOWN"]
    x = np.ascontiguousarray(np.asarray(L0_attn_x, dtype=np.float32))
    wqkv = np.ascontiguousarray(np.asarray(L0_attn_w_qkv, dtype=np.float32))
    bqkv = np.ascontiguousarray(np.asarray(L0_attn_b_qkv, dtype=np.float32))
    wout = np.ascontiguousarray(np.asarray(L0_attn_w_out, dtype=np.float32))
    bout = np.ascontiguousarray(np.asarray(L0_attn_b_out, dtype=np.float32))
    W = np.asarray(L0_ffn_W, dtype=np.float32)
    bf = np.ascontiguousarray(np.asarray(L0_ffn_b, dtype=np.float32))
    lA = np.asarray(L0_ffn_lora_A, dtype=np.float32)
    lB = np.asarray(L0_ffn_lora_B, dtype=np.float32)
    gw = np.asarray(L0_ffn_gate_w, dtype=np.float32)
    gb = np.ascontiguousarray(np.asarray(L0_ffn_gate_b, dtype=np.float32))

    VCH = cfg["VCH"]
    NVS = D // VCH

    def block_w(w):
        Dout = w.shape[1]
        nb = Dout // 128
        return np.ascontiguousarray(
            w.reshape(ND, 128, nb, 128).transpose(2, 1, 0, 3).reshape(nb, 128, ND * 128))

    wq_b = block_w(wqkv[:, 0:D])
    wk_b = block_w(wqkv[:, D:2 * D])
    wv_cols = wqkv[:, 2 * D:3 * D]
    wv_b = np.ascontiguousarray(
        wv_cols.reshape(ND, 128, NVS, VCH).transpose(2, 1, 0, 3).reshape(NVS, 128, ND * VCH))
    common = {
        "wkq": np.ascontiguousarray(np.concatenate([wk_b, wq_b], axis=0)),
        "wv": wv_b,
        "bqkvt": np.ascontiguousarray(bqkv.reshape(3 * ND, 128).T),
        "wout": block_w(wout),
        "boutt": np.ascontiguousarray(bout.reshape(ND, 128).T),
        "wffnt": block_w(W.T),
        "bffnt": np.ascontiguousarray(bf.reshape(ND, 128).T),
        "aall": np.ascontiguousarray(lA.transpose(2, 0, 1).reshape(D, ER)),
        "ball": np.ascontiguousarray(lB.transpose(0, 2, 1).reshape(ER, D)),
        "gwt": np.ascontiguousarray(gw.T),
        "gb": gb,
        "ident": np.eye(128, dtype=np.float32),
        "e16": np.ascontiguousarray(
            np.repeat(np.eye(NE, dtype=np.float32), R, axis=1)),
    }
    in_maps = []
    for c in range(n_cores):
        bidx, h = c // 2, c % 2
        xb = x[bidx]
        if cfg["TOTH"] > 0:
            xr = np.ascontiguousarray(xb[h * TOWN:(h + 1) * TOWN])
        else:
            xr = np.ascontiguousarray(xb[:TOWN])
        m = dict(common)
        m["x"] = xr
        in_maps.append(m)
    return in_maps


def kernel(L0_attn_x, L0_attn_w_qkv, L0_attn_b_qkv, L0_attn_w_out, L0_attn_b_out,
           L0_ffn_W, L0_ffn_b, L0_ffn_lora_A, L0_ffn_lora_B, L0_ffn_gate_w,
           L0_ffn_gate_b):
    cfg = CFG_FULL
    in_maps = stage_inputs(L0_attn_x, L0_attn_w_qkv, L0_attn_b_qkv,
                           L0_attn_w_out, L0_attn_b_out, L0_ffn_W, L0_ffn_b,
                           L0_ffn_lora_A, L0_ffn_lora_B, L0_ffn_gate_w,
                           L0_ffn_gate_b, cfg)
    nc, _ = build_core_program(cfg)
    res = run_bass_kernel_spmd(nc, in_maps, core_ids=list(range(8)))
    out = np.empty((4, 4096, 2048), dtype=np.float32)
    for c in range(8):
        bidx, h = c // 2, c % 2
        out[bidx, h * 2048:(h + 1) * 2048, :] = res.results[c]["out"]
    return out


# version 40
# speedup vs baseline: 1.0579x; 1.0286x over previous
import math
import numpy as np
import concourse.bass as bass
import concourse.mybir as mybir
from concourse.bass_utils import run_bass_kernel_spmd

F32 = mybir.dt.float32
BF16 = mybir.dt.bfloat16
AF = mybir.ActivationFunctionType
ALU = mybir.AluOpType
AX = mybir.AxisListType

ENGINES = ("sync", "gpsimd", "tensor", "scalar", "vector")


def make_cfg(D=2048, TOWN=2048, TOTH=2048):
    c = {}
    c["D"] = D
    c["ND"] = D // 128
    c["H"] = D // 128
    c["TOWN"] = TOWN
    c["TOTH"] = TOTH
    c["TALL"] = TOWN + TOTH
    c["QB"] = min(512, TOWN)
    c["NQC"] = TOWN // c["QB"]
    c["NQT"] = TOWN // 128
    c["NK"] = c["TALL"] // 128
    c["NE"] = 8
    c["R"] = 16
    c["ER"] = 128
    c["EPS"] = 1e-5
    c["SM"] = 1.0 / math.sqrt(128.0)
    c["VCH"] = max(128, D // 8)
    c["NRANKS"] = 2 if TOTH > 0 else 1
    return c


CFG_FULL = make_cfg()


class Builder:
    def __init__(self, cfg):
        self.cfg = cfg
        self.nc = bass.Bass("TRN2", target_bir_lowering=False,
                            dynamic_dma_scratch_size=16384)
        self.plan = {e: [] for e in ENGINES}
        self.sems = {}
        self.cnt = {}

    def _sem(self, name):
        if name not in self.sems:
            self.sems[name] = self.nc.alloc_semaphore(name)
            self.cnt[name] = 0
        return self.sems[name]

    ENG_SEM = {"vector": "dv", "scalar": "ac"}

    def op(self, eng, fn, waits=(), inc=None):
        ws = [(s, v) for (s, v) in waits if v and v > 0]
        es = self.ENG_SEM.get(eng)
        if es is not None:
            prev = self.cnt.get(es, 0)
            if prev:
                ws.append((es, prev))
            if inc is None:
                inc = (es, 1)
            else:
                assert inc[0] == es, (eng, inc)
        if inc is not None:
            self._sem(inc[0])
            self.cnt[inc[0]] += inc[1]
        for s, _ in ws:
            self._sem(s)
        self.plan[eng].append((fn, tuple(ws), inc))
        return self.cnt[inc[0]] if inc is not None else None

    def val(self, name):
        return self.cnt.get(name, 0)

    def dma(self, eng, chan, dst, src, waits=()):
        ws = list(waits)
        prev = self.cnt.get(chan, 0)
        if prev:
            ws.append((chan, prev))
        return self.op(eng, lambda g, d=dst, s=src: g.dma_start(d, s),
                       waits=ws, inc=(chan, 16))

    def play(self):
        with self.nc.Block() as block:
            for e in ENGINES:
                ops = self.plan[e]
                if not ops:
                    continue

                def mk(ops=ops):
                    def body(eng):
                        for fn, waits, inc in ops:
                            for (s, v) in waits:
                                eng.wait_ge(self.sems[s], v)
                            inst = fn(eng)
                            if inc is not None:
                                inst.then_inc(self.sems[inc[0]], inc[1])
                    return body

                getattr(block, e)(mk())

    def build(self):
        c = self.cfg
        nc = self.nc
        D, ND, H = c["D"], c["ND"], c["H"]
        TOWN, TOTH, TALL = c["TOWN"], c["TOTH"], c["TALL"]
        QB, NQC, NQT, NK = c["QB"], c["NQC"], c["NQT"], c["NK"]
        NE, R, ER = c["NE"], c["R"], c["ER"]
        EPS, SM, VCH = c["EPS"], c["SM"], c["VCH"]
        NP = NK // 2
        NQTC = QB // 128
        NVS = D // VCH

        x_h = nc.dram_tensor("x", [TOWN, D], F32, kind="ExternalInput")
        NVS_ = D // max(128, D // 8)
        wkq_h = nc.dram_tensor("wkq", [2 * ND, 128, D], F32, kind="ExternalInput")
        wv_h = nc.dram_tensor("wv", [NVS_, 128, (D // NVS_) * ND], F32, kind="ExternalInput")
        bqkvt_h = nc.dram_tensor("bqkvt", [128, 3 * ND], F32, kind="ExternalInput")
        wout_h = nc.dram_tensor("wout", [ND, 128, D], F32, kind="ExternalInput")
        boutt_h = nc.dram_tensor("boutt", [128, ND], F32, kind="ExternalInput")
        wffnt_h = nc.dram_tensor("wffnt", [ND, 128, D], F32, kind="ExternalInput")
        bffnt_h = nc.dram_tensor("bffnt", [128, ND], F32, kind="ExternalInput")
        aall_h = nc.dram_tensor("aall", [D, ER], F32, kind="ExternalInput")
        ball_h = nc.dram_tensor("ball", [ER, D], F32, kind="ExternalInput")
        gwt_h = nc.dram_tensor("gwt", [D, NE], F32, kind="ExternalInput")
        gb_h = nc.dram_tensor("gb", [NE], F32, kind="ExternalInput")
        ident_h = nc.dram_tensor("ident", [128, 128], F32, kind="ExternalInput")
        e16_h = nc.dram_tensor("e16", [NE, ER], F32, kind="ExternalInput")
        out_h = nc.dram_tensor("out", [TOWN, D], F32, kind="ExternalOutput")
        woutbf_h = nc.dram_tensor("woutbf", [ND, 128, D], BF16, kind="Internal")
        wffnbf_h = nc.dram_tensor("wffnbf", [ND, 128, D], BF16, kind="Internal")
        dbg = c.get("debug", False)
        NRANKS = c["NRANKS"]
        agk_in = nc.dram_tensor("agk_in", [H * 128, TOWN], BF16, kind="Internal")
        agv_in = nc.dram_tensor("agv_in", [H * TOWN, 128], BF16, kind="Internal")
        NAG = 4 if H % 4 == 0 else 1
        HG = H // NAG
        if NRANKS > 1:
            agk_out = [nc.dram_tensor(f"agk_out{g}", [NRANKS * HG * 128, TOWN], BF16, kind="Internal")
                       for g in range(NAG)]
            agv_out = [nc.dram_tensor(f"agv_out{g}", [NRANKS * HG * TOWN, 128], BF16, kind="Internal")
                       for g in range(NAG)]
        if dbg:
            dbgq_h = nc.dram_tensor("dbgq", [128, H * TOWN], BF16, kind="ExternalOutput")
            dbgx_h = nc.dram_tensor("dbgx", [128, ND * TOWN], BF16, kind="ExternalOutput")
            dbga_h = nc.dram_tensor("dbga", [128, H * TOWN], BF16, kind="ExternalOutput")
            dbgh_h = nc.dram_tensor("dbgh", [128, NQT * D], BF16, kind="ExternalOutput")

        A = nc.alloc_sbuf_tensor
        big0 = A("big0", [128, ND * max(TOWN, TOTH)], BF16)
        big1 = A("big1", [128, H * TOWN], BF16)
        bigC = A("bigC", [128, max(2 * ND * VCH, NQTC * D)], BF16)
        pool = A("pool", [128, max(4 * TALL, 8 * D)], BF16)
        e_buf = A("e_buf", [128, max(4 * QB, D)], BF16)
        oevac = A("oevac", [128, QB], BF16)
        gxa_sb = A("gxa_sb", [128, QB], BF16)
        gexp_sb = A("gexp_sb", [128, QB], BF16)
        rzb_sb = A("rzb_sb", [128, QB], BF16)
        rz_bf = A("rz_bf", [1, QB], BF16)
        es4_sb = A("es4_sb", [128, QB], BF16)
        esacc = oevac
        smalls = A("smalls", [128, 16], F32)
        lg_fm = A("lg_fm", [NE, QB], BF16)
        gt_bf = A("gt_bf", [128, NQTC * NE], BF16)
        lg_tm = A("lg_tm", [128, NQTC * NE], F32)
        gt_tm = A("gt_tm", [128, NQTC * NE], F32)
        gt_fm = A("gt_fm", [NE, QB], BF16)
        ident_bf = A("ident_bf", [128, 128], BF16)
        e16_bf = A("e16_bf", [NE, ER], BF16)
        ones_col = A("ones_col", [128, 1], BF16)
        ones_row = A("ones_row", [1, 128], BF16)
        ones_rowq = A("ones_rowq", [1, QB], BF16)
        bqkvt_sb = A("bqkvt_sb", [128, 3 * ND], F32)
        boutt_sb = A("boutt_sb", [128, ND], F32)
        b2out_sb = A("b2out_sb", [128, ND], F32)
        bv_bf = A("bv_bf", [128, ND], BF16)
        bffnt_sb = A("bffnt_sb", [128, ND], F32)
        gb_row = A("gb_row", [1, NE], BF16)
        aall_sb = A("aall_sb", [128, ND * ER], BF16)
        ball_sb = A("ball_sb", [128, D], BF16)
        gwt_sb = A("gwt_sb", [128, ND * NE], BF16)
        wblk = A("wblk", [128, 2 * ND * 128], BF16)
        eps_c = A("eps_c", [128, 1], F32)
        nc.const_aps.register(F32, EPS, eps_c[:, :])

        rem = nc.sbuf_bytes_remaining
        assert rem >= 0, f"SBUF over budget by {-rem} bytes"
        self.sbuf_remaining = rem
        self._smalls = smalls

        stat = smalls[:, 0:16]
        lnz_row = bigC[0:1, 0:2 * QB].bitcast(F32)
        W_bigc = max(2 * ND * VCH, NQTC * D)
        if W_bigc >= 14 * QB:
            eslots = [e_buf[:, 0:2 * QB], e_buf[:, 2 * QB:4 * QB],
                      bigC[:, 4 * QB:6 * QB], bigC[:, 6 * QB:8 * QB]]
            av_sb = bigC[:, 8 * QB:9 * QB]
            eaw = bigC[:, 10 * QB:12 * QB]
            ewt = bigC[:, 12 * QB:14 * QB]
        else:
            ex23 = A("ex23", [128, 4 * QB], BF16)
            eslots = [e_buf[:, 0:2 * QB], e_buf[:, 2 * QB:4 * QB],
                      ex23[:, 0:2 * QB], ex23[:, 2 * QB:4 * QB]]
            av_sb = A("av_sb", [128, QB], BF16)
            eaw = A("eaw", [128, 2 * QB], BF16)
            ewt = A("ewt", [128, 2 * QB], BF16)
        NSL = len(eslots)

        xtm_t = [pool[:, i * D:(i + 1) * D] for i in range(2)]
        vevac_t = [pool[:, 2 * D + i * VCH: 2 * D + (i + 1) * VCH] for i in range(2)]
        TMX = max(TOWN, TOTH)
        kq0 = 2 * D + 2 * VCH
        kqev_t = [pool[:, kq0 + i * TMX: kq0 + (i + 1) * TMX] for i in range(2)]
        kh_t = [pool[:, i * TALL:(i + 1) * TALL] for i in range(2)]
        vh_t = [pool[:, (2 + i) * TALL:(3 + i) * TALL] for i in range(2)]
        poolf = pool[:, :].bitcast(F32)
        xf32_t2 = [poolf[:, 0:D], poolf[:, D:2 * D]]
        outf_t = [poolf[:, D + i * D: D + (i + 1) * D] for i in range(2)]
        rscr_t = poolf[:, 3 * D: 4 * D]

        PA = nc.alloc_psum_tensor("PA", [128, 2048], F32)
        PB = nc.alloc_psum_tensor("PB", [128, 2048], F32)
        PAb = PA[:, :].bitcast(BF16)
        PBb = PB[:, :].bitcast(BF16)

        wblkv = [wblk[:, i * ND * 128:(i + 1) * ND * 128] for i in range(2)]


        self.dma("gpsimd", "cw", ident_bf[:, :], ident_h.ap())
        self.dma("gpsimd", "cw", e16_bf[:, :], e16_h.ap())
        self.dma("gpsimd", "cw", bqkvt_sb[:, :], bqkvt_h.ap())
        self.dma("gpsimd", "cw", boutt_sb[:, :], boutt_h.ap())
        self.dma("gpsimd", "cw", bffnt_sb[:, :], bffnt_h.ap())
        self.dma("gpsimd", "cw", gb_row[:, :], gb_h.ap().rearrange("(o e) -> o e", o=1))
        self.dma("gpsimd", "cw", aall_sb[:, :].rearrange("p (dc e) -> p dc e", dc=ND),
                 aall_h.ap().rearrange("(dc p) e -> p dc e", p=128))
        self.dma("gpsimd", "cw", ball_sb[:, :], ball_h.ap())
        self.dma("gpsimd", "cw", gwt_sb[:, :].rearrange("p (dc e) -> p dc e", dc=ND),
                 gwt_h.ap().rearrange("(dc p) e -> p dc e", p=128))
        consts_g = self.val("cw")
        m = self.op("vector", lambda v: v.memset(ones_col[:, :], 1.0))
        m = self.op("vector", lambda v: v.memset(ones_row[:, :], 1.0))
        ones_dv = self.op("vector", lambda v: v.memset(ones_rowq[:, :], 1.0),
                          inc=("dv", 1))
        self.op("vector", lambda v: v.memset(eps_c[:, :], EPS), inc=("dv", 1))
        bvdv = self.op("vector",
                       lambda v: v.tensor_copy(bv_bf[:, :], bqkvt_sb[:, 2 * ND:3 * ND]),
                       waits=[("cw", consts_g)], inc=("dv", 1))

        halves = [(0, TOWN, True)]
        xload = {}
        tpose = {}
        xfmdv = {}

        spillv_s = {}
        spillk_s = {}
        last_pe_before = 0

        for hi, (toff, TH, own) in enumerate(halves):
            NTT = TH // 128
            NTC = max(1, TH // 512)
            TCB = TH // NTC
            for tt in range(NTT):
                buf = tt % 2
                if tt >= 2:
                    w = [("pe", tpose[(hi, tt - 2)])]
                elif hi > 0:
                    w = [("pe", tpose[(hi - 1, halves[0][1] // 128 - 2 + tt)])]
                else:
                    w = []

                xload[(hi, tt)] = self.dma(
                    "gpsimd", f"gx{tt % 2}",
                    xtm_t[buf], x_h.ap()[toff + tt * 128: toff + (tt + 1) * 128, :],
                    waits=w)
                pdst = PAb if tt % 2 == 0 else PBb
                for dc in range(ND):
                    wts = []
                    if dc == 0:
                        wts = [(f"gx{tt % 2}", xload[(hi, tt)])]
                        prev = None
                        if tt >= 2:
                            prev = (hi, tt - 2)
                        elif hi > 0:
                            prev = (hi - 1, halves[0][1] // 128 - 2 + tt)
                        if prev is not None:
                            wts.append(("dv", xfmdv[prev]))
                        if hi > 0 and tt < 2:
                            wts.append(("ac", self.val("ac")))
                    fin = (dc == ND - 1)
                    tpose_inc = ("pe", 1) if fin else None
                    tpose[(hi, tt)] = self.op(
                        "tensor",
                        lambda t, d=pdst, b=buf, dc=dc:
                        t.transpose(d[:, dc * 128:(dc + 1) * 128],
                                    xtm_t[b][:, dc * 128:(dc + 1) * 128],
                                    ident_bf[:, :]),
                        waits=wts, inc=tpose_inc) or tpose.get((hi, tt))
                xfmdv[(hi, tt)] = self.op(
                    "vector",
                    lambda v, d=pdst, tt=tt, TH=TH:
                    v.tensor_copy(
                        big0[:, 0:ND * TH].rearrange("p (dc t) -> p dc t", dc=ND)[:, :, tt * 128:(tt + 1) * 128],
                        d[:, 0:ND * 128].rearrange("p (dc t) -> p dc t", dc=ND)),
                    waits=[("pe", tpose[(hi, tt)])], inc=("dv", 1))
            all_tp_dv = self.val("dv")
            if dbg and own and not c.get("skip_dumps"):
                self.dma("sync", "sd", dbgx_h.ap(), big0[:, 0:ND * TOWN],
                         waits=[("dv", all_tp_dv)])

            vm_ms = {}
            if hi == 0:
                vevac_all = {}
            vevac_ms = vevac_all.setdefault(hi, {})
            vstrip_pe = {}
            for vs in range(0 if not c.get("skip_v") else 0, 0 if c.get("skip_v") else NVS):
                sb = vs % 2
                if vs >= 2:
                    lw = [("pe", vstrip_pe[vs - 2])]
                elif hi > 0:
                    lw = [("pe", prev_half_vpe[sb])]
                else:
                    lw = []
                hv = ND * VCH // 2
                wvg = [
                    ("gv0", self.dma(
                        "gpsimd", "gv0",
                        bigC[:, sb * ND * VCH: sb * ND * VCH + hv],
                        wv_h.ap()[vs, :, 0:hv], waits=lw)),
                    ("gv1", self.dma(
                        "gpsimd", "gv1",
                        bigC[:, sb * ND * VCH + hv:(sb + 1) * ND * VCH],
                        wv_h.ap()[vs, :, hv:2 * hv], waits=list(lw))),
                ]
                for tt in range(NTT):
                    slot = tt % 2
                    for dc in range(ND):
                        wts = []
                        if dc == 0:
                            wts = wvg + [("dv", all_tp_dv)]
                            if tt >= 2:
                                wts.append(("ac", vevac_ms[(vs, tt - 2)]))
                            elif vs > 0:
                                wts.append(("ac", vevac_ms[(vs - 1, NTT - 2 + tt)]))
                            elif hi > 0:
                                wts.append(("ac", prev_half_ac))
                        fin = (dc == ND - 1)
                        r = self.op(
                            "tensor",
                            lambda t, slot=slot, dc=dc, tt=tt, TH=TH, fin=fin, sb=sb:
                            t.matmul(PA[:, slot * 512: slot * 512 + VCH],
                                     big0[:, 0:ND * TH].rearrange("p (dc t) -> p dc t", dc=ND)[:, dc, tt * 128:(tt + 1) * 128],
                                     bigC[:, sb * ND * VCH:(sb + 1) * ND * VCH].rearrange("p (dc v) -> p dc v", dc=ND)[:, dc, :],
                                     start=(dc == 0), stop=fin),
                            waits=wts, inc=(("pe", 1) if fin else None))
                        if fin:
                            vm_ms[(vs, tt)] = r
                            if tt == NTT - 1:
                                vstrip_pe[vs] = r
                    ebuf = tt % 2
                    wts = [("pe", vm_ms[(vs, tt)])]
                    prevsp = None
                    if tt >= 2:
                        prevsp = (hi, vs, tt - 2)
                    elif vs > 0:
                        prevsp = (hi, vs - 1, NTT - 2 + tt)
                    elif hi > 0:
                        prevsp = (hi - 1, NVS - 1, halves[0][1] // 128 - 2 + tt)
                    if prevsp is not None:
                        wts.append((f"sv{tt % 2}", spillv_s[prevsp]))
                    vevac_ms[(vs, tt)] = self.op(
                        "scalar",
                        lambda sc, slot=slot, eb=ebuf:
                        sc.activation(vevac_t[eb], PA[:, slot * 512: slot * 512 + VCH], AF.Copy),
                        waits=wts, inc=("ac", 1))
                    nh = VCH // 128
                    spillv_s[(hi, vs, tt)] = self.dma(
                        "sync", f"sv{ebuf}",
                        agv_in.ap().rearrange("(h t) e -> h t e", e=128, t=TOWN)
                        [vs * nh:(vs + 1) * nh, tt * 128:(tt + 1) * 128, :]
                        .rearrange("h t e -> t h e"),
                        vevac_t[ebuf].rearrange("t (h e) -> t h e", e=128),
                        waits=[("ac", vevac_ms[(vs, tt)])])

            prev_half_vpe = {0: vstrip_pe.get(NVS - 2, self.val("pe")),
                             1: vstrip_pe.get(NVS - 1, self.val("pe"))}
            def emit_agv(g_):
                rg = [[2 * i, 2 * i + 1] for i in range(4)]
                spg = NVS // NAG
                wts = []
                for vs_ in range(g_ * spg, (g_ + 1) * spg):
                    for par in range(2):
                        key = (hi, vs_, NTT - 1 - par)
                        if key in spillv_s:
                            wts.append((f"sv{(NTT - 1 - par) % 2}", spillv_s[key]))
                self.op("gpsimd",
                        lambda g, g_=g_: g.collective_compute(
                            "AllGather", ALU.bypass, replica_groups=rg,
                            ins=[agv_in.ap()[g_ * HG * TOWN:(g_ + 1) * HG * TOWN, :].opt()],
                            outs=[agv_out[g_].ap().opt()]),
                        waits=wts, inc=(f"agv{g_}", 1))
            jlist = [("k", j) for j in range(ND)] + ([("q", j) for j in range(ND)] if (own and not c.get("skip_q")) else [])
            wbg = {}
            kq_mm = {}
            kq_ac = {}
            pend_ag = []
            for ji, (kind, jc) in enumerate(jlist):
                wbuf = ji % 2
                if NRANKS > 1 and not c.get("skip_v") and ji == 2:
                    for g_ in range(NAG):
                        emit_agv(g_)
                while pend_ag and pend_ag[0][1] <= ji - 3:
                    g_, ji0 = pend_ag.pop(0)
                    rg = [[2 * i, 2 * i + 1] for i in range(4)]
                    wts = [(f"sk{j_ % 2}", spillk_s[(hi, j_)])
                           for j_ in range(ji0 - HG + 1, ji0 + 1)]
                    self.op("gpsimd",
                            lambda g, g_=g_: g.collective_compute(
                                "AllGather", ALU.bypass, replica_groups=rg,
                                ins=[agk_in.ap()[g_ * HG * 128:(g_ + 1) * HG * 128, :].opt()],
                                outs=[agk_out[g_].ap().opt()]),
                            waits=wts, inc=(f"agk{g_}", 1))
                wts = [("pe", kq_mm[ji - 2])] if ji >= 2 else [("pe", self.val("pe"))]
                blk_i = jc if kind == "k" else (ND + jc)
                hw = ND * 64
                wbg[ji] = [
                    (f"gw{2 * wbuf}", self.dma(
                        "gpsimd", f"gw{2 * wbuf}",
                        wblkv[wbuf][:, 0:hw], wkq_h.ap()[blk_i, :, 0:hw],
                        waits=wts)),
                    (f"gw{2 * wbuf + 1}", self.dma(
                        "gpsimd", f"gw{2 * wbuf + 1}",
                        wblkv[wbuf][:, hw:2 * hw], wkq_h.ap()[blk_i, :, hw:2 * hw],
                        waits=list(wts))),
                ]
                pban = PA if ji % 2 == 0 else PB
                for dc in range(ND):
                    for tc in range(NTC):
                        wts = []
                        if dc == 0 and tc == 0:
                            wts = wbg[ji] + [("dv", all_tp_dv)]
                            if ji >= 2:
                                wts.append(("ac", kq_ac[ji - 2]))
                            elif not c.get("skip_v"):
                                wts.append(("ac", vevac_ms[(NVS - 1, NTT - 1)]))
                        fin = (dc == ND - 1 and tc == NTC - 1)
                        r = self.op(
                            "tensor",
                            lambda t, pb=pban, wbuf=wbuf, dc=dc, tc=tc, TCB=TCB, TH=TH:
                            t.matmul(pb[:, tc * TCB:(tc + 1) * TCB],
                                     wblkv[wbuf][:, dc * 128:(dc + 1) * 128],
                                     big0[:, 0:ND * TH].rearrange("p (dc t) -> p dc t", dc=ND)[:, dc, tc * TCB:(tc + 1) * TCB],
                                     start=(dc == 0), stop=(dc == ND - 1)),
                            waits=wts, inc=(("pe", 1) if fin else None))
                        if fin:
                            kq_mm[ji] = r
                bias_col = (ND + jc) if kind == "k" else jc
                if kind == "k":
                    kq_mm_last = kq_mm[ji]
                    kb = ji % 2
                    wts = [("pe", kq_mm[ji])]
                    if ji >= 2:
                        wts.append((f"sk{kb}", spillk_s[(hi, ji - 2)]))
                    elif hi > 0:
                        wts.append((f"sk{kb}", spillk_s[(hi - 1, ND - 2 + ji)]))
                    if c.get("kq_nobias"):
                        kq_ac[ji] = self.op(
                            "scalar",
                            lambda sc, pb=pban, kb=kb, TH=TH:
                            sc.activation(kqev_t[kb][:, 0:TH], pb[:, 0:TH], AF.Copy),
                            waits=wts, inc=("ac", 1))
                    else:
                        kq_ac[ji] = self.op(
                            "scalar",
                            lambda sc, pb=pban, bc=bias_col, kb=kb, TH=TH:
                            sc.activation(kqev_t[kb][:, 0:TH], pb[:, 0:TH], AF.Identity,
                                          bias=bqkvt_sb[:, bc:bc + 1]),
                            waits=wts, inc=("ac", 1))
                    spillk_s[(hi, ji)] = self.dma(
                        "sync", f"sk{kb}",
                        agk_in.ap()[jc * 128:(jc + 1) * 128, :], kqev_t[kb][:, 0:TH],
                        waits=[("ac", kq_ac[ji])])
                    if NRANKS > 1 and jc % HG == HG - 1:
                        pend_ag.append((jc // HG, ji))
                else:
                    kq_mm_last = kq_mm[ji]
                    kq_ac[ji] = self.op(
                        "scalar",
                        lambda sc, pb=pban, bc=bias_col, jc=jc, TH=TH:
                        sc.activation(big1[:, jc * TOWN: jc * TOWN + TH], pb[:, 0:TH], AF.Identity,
                                      bias=bqkvt_sb[:, bc:bc + 1]),
                        waits=[("pe", kq_mm[ji])], inc=("ac", 1))

            prev_half_ac = self.val("ac")

        if NRANKS > 1:
            for g_, ji0 in pend_ag:
                rg = [[2 * i, 2 * i + 1] for i in range(4)]
                wts = [(f"sk{j_ % 2}", spillk_s[(0, j_)])
                       for j_ in range(ji0 - HG + 1, ji0 + 1)]
                self.op("gpsimd",
                        lambda g, g_=g_: g.collective_compute(
                            "AllGather", ALU.bypass, replica_groups=rg,
                            ins=[agk_in.ap()[g_ * HG * 128:(g_ + 1) * HG * 128, :].opt()],
                            outs=[agk_out[g_].ap().opt()]),
                        waits=wts, inc=(f"agk{g_}", 1))
            pend_ag = []
        p1_pe = self.val("pe")
        p1_spill_finals = [(ch, self.val(ch)) for ch in
                           ("sv0", "sv1", "sk0", "sk1") if self.val(ch)]
        p1_ac = self.val("ac")
        if dbg and not c.get("skip_dumps"):
            self.dma("sync", "sd", dbgq_h.ap(), big1[:, :], waits=[("ac", p1_ac)])

        phase_limit = c.get("phase_limit", 99)
        if phase_limit < 2:
            fin = [(ch, self.val(ch)) for ch in
                   ("sv0", "sv1", "sk0", "sk1", "sd") if self.val(ch)]
            self.op("sync", lambda sy: sy.nop(), waits=fin)
            self.play()
            return nc
        for oc in range(ND):
            self.dma("gpsimd", "pc", woutbf_h.ap()[oc, :, :],
                     wout_h.ap()[oc, :, :])
        pc_wout = self.val("pc")
        for oc in range(ND):
            self.dma("gpsimd", "pc", wffnbf_h.ap()[oc, :, :],
                     wffnt_h.ap()[oc, :, :])
        pc_all = self.val("pc")
        kload = {}
        vload = {}
        headdone = {}
        avms = {}
        zm_prev = [None]
        acc_dv = [None]
        esdvs = {}
        prev_exp_final = None
        aca2_prev = [None]
        aca_hold = [None]
        mult_hold = [None]
        rza_hold = [None]
        bca_hold = [None]
        pend = {}

        def fire(key):
            f = pend.pop(key, None)
            if f is not None:
                f()

        def gslot(hd, qc, pp):
            return (((hd * NQC) + qc) * NP + pp) % NSL

        def slot_user_prev(hd, qc, pp):
            gid = (hd * NQC + qc) * NP + pp - NSL
            if gid < 0:
                return None
            c_, pp_ = divmod(gid, NP)
            hd_, qc_ = divmod(c_, NQC)
            return (hd_, qc_, pp_)
        for hd in range(H):
            buf = hd % 2
            wts = ([("pe", p1_pe)] + p1_spill_finals) if hd < 2 else [("pe", headdone[hd - 2])]
            if NRANKS > 1:
                g_ = hd // HG
                hh = hd % HG
                wts = wts + [(f"agk{g_}", 1), (f"agv{g_}", 1)]
                for r in range(NRANKS):
                    kload[hd] = self.dma(
                        "sync", f"kl{buf}",
                        kh_t[buf][:, r * TOWN:(r + 1) * TOWN],
                        agk_out[g_].ap()[r * HG * 128 + hh * 128: r * HG * 128 + (hh + 1) * 128, :],
                        waits=wts if r == 0 else [])
                    vload[hd] = self.dma(
                        "sync", f"vl{buf}",
                        vh_t[buf][:, r * TOWN:(r + 1) * TOWN].rearrange("p (kc f) -> p kc f", f=128),
                        agv_out[g_].ap()[(r * HG + hh) * TOWN:(r * HG + hh + 1) * TOWN, :]
                        .rearrange("(kc p) e -> p kc e", p=128),
                        waits=[])
            else:
                kload[hd] = self.dma(
                    "sync", f"kl{buf}", kh_t[buf][:, 0:TOWN],
                    agk_in.ap()[hd * 128:(hd + 1) * 128, :], waits=wts)
                vload[hd] = self.dma(
                    "sync", f"vl{buf}",
                    vh_t[buf][:, 0:TOWN].rearrange("p (kc f) -> p kc f", f=128),
                    agv_in.ap()[hd * TOWN:(hd + 1) * TOWN, :]
                    .rearrange("(kc p) e -> p kc e", p=128),
                    waits=[])
            for qc in range(NQC):
                exp_base = self.val("ac")
                pe_pair = {}

                def emit_avz(pp):
                    if pp % 2 == 1:
                        ew = [("ac", exp_base + pp + 1)]
                        sa = eslots[gslot(hd, qc, pp - 1)]
                        sb = eslots[gslot(hd, qc, pp)]
                        if pp == 1:
                            wdv = self.op(
                                "vector",
                                lambda v, sa=sa, sb=sb: v.tensor_add(
                                    eaw[:, :], sa[:, :], sb[:, :]),
                                waits=ew, inc=("dv", 1))
                        else:
                            self.op(
                                "vector",
                                lambda v, sa=sa, sb=sb: v.tensor_add(
                                    ewt[:, :], sa[:, :], sb[:, :]),
                                waits=ew, inc=("dv", 1))
                            wdv = self.op(
                                "vector",
                                lambda v: v.tensor_add(eaw[:, :], eaw[:, :], ewt[:, :]),
                                inc=("dv", 1))
                        esdvs[(hd, qc, pp - 1)] = esdvs[(hd, qc, pp)] = wdv
                        if pp == NP - 1:
                            fire("z")
                            fw = [("pe", zm_prev[0])] if zm_prev[0] is not None else []
                            acc_dv[0] = self.op(
                                "vector",
                                lambda v: v.tensor_add(
                                    esacc[:, :], eaw[:, 0:QB], eaw[:, QB:2 * QB]),
                                waits=fw, inc=("dv", 1))
                    for half_k in range(2):
                        kc = pp * 2 + half_k
                        wts = [("ac", exp_base + pp + 1)] if half_k == 0 else []
                        if kc == 0:
                            fire("aca2")
                            if aca2_prev[0] is not None:
                                wts.append(("dv", aca2_prev[0]))
                        fin_av = (half_k == 1)
                        r = self.op(
                            "tensor",
                            lambda t, kc=kc, buf=buf, esl=eslots[gslot(hd, qc, pp)]:
                            t.matmul(PB[:, 0:QB],
                                     vh_t[buf][:, kc * 128:(kc + 1) * 128],
                                     esl[:, (kc % 2) * QB:(kc % 2) * QB + QB],
                                     start=(kc == 0), stop=(kc == NK - 1)),
                            waits=wts, inc=(("pe", 1) if fin_av else None))
                        if fin_av:
                            avms[(hd, qc, pp)] = r

                def sslot(kc):
                    s6 = kc % 6
                    return PA[:, s6 * 512: s6 * 512 + QB] if s6 < 4 \
                        else PB[:, 1024 + (s6 - 4) * 512: 1024 + (s6 - 4) * 512 + QB]

                for pp in range(NP):
                    for half_k in range(2):
                        kc = pp * 2 + half_k
                        wts = []
                        if half_k == 0:
                            if pp == 0:
                                wts = [(f"kl{buf}", kload[hd]), (f"vl{buf}", vload[hd]), ("dv", ones_dv)]
                                if prev_exp_final is not None:
                                    wts.append(("ac", prev_exp_final))
                                if (hd, qc) == (0, 0):
                                    wts.append(("ac", p1_ac))
                            elif pp <= 2:
                                if prev_exp_final is not None:
                                    wts = [("ac", prev_exp_final)]
                            else:
                                wts = [("ac", exp_base + pp - 2)]
                        fin = (half_k == 1)
                        r = self.op(
                            "tensor",
                            lambda t, kc=kc, buf=buf, qc=qc, hd=hd:
                            t.matmul(sslot(kc),
                                     kh_t[buf][:, kc * 128:(kc + 1) * 128],
                                     big1[:, hd * TOWN + qc * QB: hd * TOWN + (qc + 1) * QB],
                                     start=True, stop=True),
                            waits=wts, inc=(("pe", 1) if fin else None))
                        if fin:
                            pe_pair[pp] = r
                    if pp == min(1, NP - 1):
                        fire("z")
                    if pp == 2:
                        fire("aca2")
                    if pp == 4:
                        fire("lzrz")
                    if pp == 7:
                        fire("bca")
                    if pp == 8:
                        fire("aca")
                    if pp >= 3:
                        emit_avz(pp - 3)
                    wts = [("pe", pe_pair[pp])]
                    pu = slot_user_prev(hd, qc, pp)
                    if pu is not None:
                        wts.append(("pe", avms[pu]))
                        wts.append(("dv", esdvs[pu]))
                    s6 = (2 * pp) % 6
                    psrc = (PA[:, :].rearrange("p (s q) -> p s q", s=4)[:, s6:s6 + 2, 0:QB]
                            if s6 < 4 else
                            PB[:, :].rearrange("p (s q) -> p s q", s=4)[:, 2:4, 0:QB])
                    self.op(
                        "scalar",
                        lambda sc, psrc=psrc, esl=eslots[gslot(hd, qc, pp)]:
                        sc.activation(esl[:, :], psrc, AF.Exp, scale=SM),
                        waits=wts, inc=("ac", 1))
                    if pp == 9:
                        fire("mult")
                for tail_pp in range(max(0, NP - 3), NP):
                    emit_avz(tail_pp)
                for key in ("z", "aca2", "lzrz", "bca", "aca", "mult"):
                    fire(key)
                prev_exp_final = exp_base + NP

                def _z(accv=acc_dv[0]):
                    zw = [("dv", accv)]
                    if aca_hold[0] is not None:
                        zw.append(("dv", aca_hold[0]))
                    zm_prev[0] = self.op(
                        "tensor",
                        lambda t: t.matmul(PB[0:1, 512:512 + QB],
                                           ones_col[:, :], esacc[:, :],
                                           start=True, stop=True),
                        waits=zw, inc=("pe", 1))

                def _aca2(avm=avms[(hd, qc, NP - 1)]):
                    aca2_prev[0] = self.op(
                        "vector",
                        lambda v: v.tensor_copy(av_sb[:, :], PB[:, 0:QB]),
                        waits=[("pe", avm)], inc=("dv", 1))

                def _lzrz():
                    def _rz(v):
                        with self.nc.allow_low_precision(
                                "softmax normalizer in bf16 (attn weights are bf16)"):
                            return v.reciprocal(rz_bf[:, :], PB[0:1, 512:512 + QB])
                    rza_hold[0] = self.op("vector", _rz,
                                          waits=[("pe", zm_prev[0])], inc=("dv", 1))

                def _bca():
                    bca_hold[0] = self.op(
                        "tensor",
                        lambda t: t.matmul(PB[:, 512:512 + QB], ones_row[:, :],
                                           rz_bf[:, :], start=True, stop=True),
                        waits=[("dv", rza_hold[0])], inc=("pe", 1))

                def _aca():
                    aca_hold[0] = self.op(
                        "vector",
                        lambda v: v.tensor_copy(rzb_sb[:, :], PB[:, 512:512 + QB]),
                        waits=[("pe", bca_hold[0])], inc=("dv", 1))

                def _mult(hd=hd, qc=qc):
                    mult_hold[0] = self.op(
                        "vector",
                        lambda v: v.tensor_tensor(
                            big0[:, hd * TOWN + qc * QB: hd * TOWN + (qc + 1) * QB],
                            av_sb[:, :], rzb_sb[:, :], ALU.mult),
                        inc=("dv", 1))
                pend.update(z=_z, aca2=_aca2, lzrz=_lzrz, bca=_bca,
                            aca=_aca, mult=_mult)
                dk = c.get("defer_keys")
                if dk is not None:
                    for key in ("z", "aca2", "lzrz", "bca", "aca", "mult"):
                        if key not in dk:
                            fire(key)
            headdone[hd] = self.val("pe")
        for key in ("z", "aca2", "lzrz", "bca", "aca", "mult"):
            fire(key)
        attn_all_dv = self.val("dv")
        attn_all_pe = self.val("pe")
        if dbg:
            self.dma("sync", "sd", dbga_h.ap(), big0[:, 0:H * TOWN],
                     waits=[("dv", attn_all_dv)])

        if phase_limit < 3:
            fin = [(ch, self.val(ch)) for ch in
                   ("sv0", "sv1", "sk0", "sk1", "sd") if self.val(ch)]
            self.op("sync", lambda sy: sy.nop(), waits=fin)
            self.play()
            return nc
        oproj = {}
        oevac_ms = {}
        oevac_t = [oevac, es4_sb]
        pend_otr = []
        pend_ftr = []
        otr = {}
        ocp = {}
        b2_dv = {}
        xf32 = {}
        r1_dv = {}
        h1_dv = {}
        sq_ac = {}
        ln1_dv = {}
        pend_ln1 = []

        def otm_dst(qc, oc):
            if qc % 2 == 0:
                return bigC[:, 0:NQTC * D].rearrange(
                    "p (qt d) -> p qt d", d=D)[:, :, oc * 128:(oc + 1) * 128]
            return big0[:, oc * TOWN + (qc - 1) * QB: oc * TOWN + qc * QB] \
                .rearrange("p (qt e) -> p qt e", e=128)

        def otm_row(qc, qt_):
            if qc % 2 == 0:
                return bigC[:, 0:NQTC * D].rearrange(
                    "p (qt d) -> p qt d", d=D)[:, qt_, :]
            return big0[:, 0:ND * TOWN].rearrange(
                "p (dc t) -> p dc t", dc=ND)[:, :, (qc - 1) * QB + qt_ * 128:
                                             (qc - 1) * QB + (qt_ + 1) * 128]

        def emit_ln1_qt(qc, qt_):
            qt = qc * NQTC + qt_
            xb = qt % 2
            if qt == 0:
                wts = [("pe", headdone[H - 1])]
            else:
                wts = [("dv", r1_dv[qt - 2])] if qt >= 2 else []
            xf32[qt] = self.dma("sync", "sx", xf32_t2[xb],
                                x_h.ap()[qt * 128:(qt + 1) * 128, :], waits=wts)
            wts = [("sx", xf32[qt]), ("dv", ocp[(qc, ND - 1)])]
            if qt > 0:
                wts.append(("dv", sq_ac[qt - 1]))
            r1_dv[qt] = self.op(
                "vector",
                lambda v, row=otm_row(qc, qt_), xb=xb: v.scalar_tensor_tensor(
                    rscr_t, row,
                    1.0, xf32_t2[xb], ALU.mult, ALU.add,
                    accum_out=stat[:, 0:1]),
                waits=wts, inc=("dv", 1))
            sq_ac[qt] = self.op(
                "vector",
                lambda v: v.scalar_tensor_tensor(
                    e_buf[:, 0:D], rscr_t, 1.0, rscr_t, ALU.mult, ALU.mult,
                    accum_out=stat[:, 1:2]),
                waits=[("dv", r1_dv[qt])], inc=("dv", 1))
            self._ln_stats(sq_ac[qt], EPS, D, sem="dv")
            h1_dv[qt] = self.op(
                "vector",
                lambda v, qt=qt: v.tensor_scalar(
                    big1[:, qt * D:(qt + 1) * D], rscr_t,
                    stat[:, 2:3], stat[:, 7:8], ALU.subtract, ALU.mult),
                waits=[("ac", self.val("ac"))], inc=("dv", 1))
            ln1_dv[qt] = h1_dv[qt]

        for qc in range(NQC):
            for oc in range(ND):
                wbuf = oc % 2
                if (qc, oc) == (0, 0):
                    wts = [("pe", kq_mm_last)]
                elif qc == 0 and oc == 1:
                    wts = []
                else:
                    prev = (qc, oc - 2) if oc >= 2 else (qc - 1, ND - 2 + oc)
                    wts = [("pe", oproj[prev])]
                if qc == 0:
                    wts = wts + [("pc", pc_wout)]
                wg = self.dma(
                    "gpsimd", f"gw{wbuf}",
                    wblkv[wbuf], woutbf_h.ap()[oc, :, :],
                    waits=wts)
                if qc == 0:
                    bm = None
                    for dc in range(ND):
                        wts = [(f"gw{wbuf}", wg), ("dv", bvdv)] if dc == 0 else []
                        if dc == 0 and oc == 0:
                            wts.append(("dv", attn_all_dv))
                        if dc == 0 and oc > 0:
                            wts.append(("dv", b2_dv[oc - 1]))
                        fin = (dc == ND - 1)
                        r = self.op(
                            "tensor",
                            lambda t, dc=dc, wbuf=wbuf:
                            t.matmul(PB[:, 1024:1025],
                                     wblkv[wbuf][:, dc * 128:(dc + 1) * 128],
                                     bv_bf[:, dc:dc + 1],
                                     start=(dc == 0), stop=(dc == ND - 1)),
                            waits=wts, inc=(("pe", 1) if fin else None))
                        if fin:
                            bm = r
                    b2_dv[oc] = self.op(
                        "vector",
                        lambda v, oc=oc: v.tensor_add(
                            b2out_sb[:, oc:oc + 1], boutt_sb[:, oc:oc + 1],
                            PB[:, 1024:1025]),
                        waits=[("pe", bm)], inc=("dv", 1))
                for hd in range(H):
                    wts = []
                    if hd == 0:
                        wts = [(f"gw{wbuf}", wg), ("dv", attn_all_dv)]
                        if qc > 0 or oc >= 4:
                            prev = (qc, oc - 4) if oc >= 4 else (qc - 1, ND - 4 + oc)
                            wts.append(("ac", oevac_ms[prev]))
                    fin = (hd == H - 1)
                    r = self.op(
                        "tensor",
                        lambda t, oc=oc, hd=hd, qc=qc, wbuf=wbuf, fin=fin:
                        t.matmul(PA[:, (oc % 4) * 512:(oc % 4) * 512 + QB],
                                 wblkv[wbuf][:, hd * 128:(hd + 1) * 128],
                                 big0[:, hd * TOWN + qc * QB: hd * TOWN + (qc + 1) * QB],
                                 start=(hd == 0), stop=fin),
                        waits=wts, inc=(("pe", 1) if fin else None))
                    if fin:
                        oproj[(qc, oc)] = r
                wts = [("pe", oproj[(qc, oc)]), ("dv", b2_dv[oc])]
                if qc > 0 or oc >= 2:
                    prev = (qc, oc - 2) if oc >= 2 else (qc - 1, ND - 2 + oc)
                    wts.append(("pe", otr[prev]))
                oevac_ms[(qc, oc)] = self.op(
                    "scalar",
                    lambda sc, oc=oc: sc.activation(
                        oevac_t[oc % 2][:, 0:QB],
                        PA[:, (oc % 4) * 512:(oc % 4) * 512 + QB],
                        AF.Identity, bias=b2out_sb[:, oc:oc + 1]),
                    waits=wts, inc=("ac", 1))

                def emit_otr(qc, oc):
                    for qt_ in range(NQTC):
                        wts = [("ac", oevac_ms[(qc, oc)])] if qt_ == 0 else []
                        if qt_ == 0 and (qc > 0 or oc >= 2):
                            prev = (qc, oc - 2) if oc >= 2 else (qc - 1, ND - 2 + oc)
                            wts.append(("dv", ocp[prev]))
                        fin = (qt_ == NQTC - 1)
                        r = self.op(
                            "tensor",
                            lambda t, oc=oc, qt_=qt_:
                            t.transpose(PBb[:, (oc % 2) * 1024 + qt_ * 128:(oc % 2) * 1024 + (qt_ + 1) * 128],
                                        oevac_t[oc % 2][:, qt_ * 128:(qt_ + 1) * 128],
                                        ident_bf[:, :]),
                            waits=wts, inc=(("pe", 1) if fin else None))
                        if fin:
                            otr[(qc, oc)] = r
                    ocp[(qc, oc)] = self.op(
                        "vector",
                        lambda v, dst=otm_dst(qc, oc), oc=oc: v.tensor_copy(
                            dst,
                            PBb[:, (oc % 2) * 1024:(oc % 2) * 1024 + QB].rearrange("p (qt e) -> p qt e", e=128)),
                        waits=[("pe", otr[(qc, oc)])], inc=("dv", 1))
                if pend_otr:
                    emit_otr(*pend_otr.pop(0))
                pend_otr.append((qc, oc))
                if oc in (8, 10, 12, 14) and pend_ln1:
                    emit_ln1_qt(*pend_ln1.pop(0))
            pend_ln1.extend((qc, qt_) for qt_ in range(NQTC))
        while pend_otr:
            emit_otr(*pend_otr.pop(0))
        while pend_ln1:
            emit_ln1_qt(*pend_ln1.pop(0))
        ln1_all_dv = self.val("dv")
        if dbg:
            self.dma("sync", "sd", dbgh_h.ap(), big1[:, :],
                     waits=[("dv", ln1_all_dv)])

        if phase_limit < 4:
            fin = [(ch, self.val(ch)) for ch in
                   ("sv0", "sv1", "sk0", "sk1", "sd", "sx") if self.val(ch)]
            self.op("sync", lambda sy: sy.nop(), waits=fin)
            self.play()
            return nc
        h1t = {}
        h1tdv = {}
        for qt in range(NQT):
            pdst = PAb if qt % 2 == 0 else PBb
            for dc in range(ND):
                wts = []
                if dc == 0:
                    wts = [("dv", ln1_dv[qt])]
                    if qt <= 1:
                        wts.append(("ac", oevac_ms[(NQC - 1, ND - 1)]))
                        wts.append(("dv", ocp[(NQC - 1, ND - 1)]))
                    if qt >= 2:
                        wts.append(("dv", h1tdv[qt - 2]))
                fin = (dc == ND - 1)
                r = self.op(
                    "tensor",
                    lambda t, d=pdst, qt=qt, dc=dc:
                    t.transpose(d[:, dc * 128:(dc + 1) * 128],
                                big1[:, qt * D + dc * 128: qt * D + (dc + 1) * 128],
                                ident_bf[:, :]),
                    waits=wts, inc=(("pe", 1) if fin else None))
                if fin:
                    h1t[qt] = r
            h1tdv[qt] = self.op(
                "vector",
                lambda v, d=pdst, qt=qt: v.tensor_copy(
                    big0[:, 0:ND * TOWN].rearrange("p (dc t) -> p dc t", dc=ND)[:, :, qt * 128:(qt + 1) * 128],
                    d[:, 0:ND * 128].rearrange("p (dc t) -> p dc t", dc=ND)),
                waits=[("pe", h1t[qt])], inc=("dv", 1))
        h1fm_dv = self.val("dv")

        fproj = {}
        fevac = {}
        ftr = {}
        fcp = {}
        gxa_ms = {}
        gexac_ms = {}
        lgtdv_ms = {}
        gtfdv_ms = {}
        xamm_ms = {}
        lgac_ms = {}
        r2_dv = {}
        sq2_ac = {}
        st_s = {}
        pend_ln2 = []
        gxa_t = [gxa_sb, rzb_sb]

        def pregame_a(qc):
            for dc in range(ND):
                wts = []
                if dc == 0:
                    wts = [("dv", h1fm_dv)]
                    if qc > 0:
                        wts.append(("dv", gxa_ms[qc - 1]))
                fin = (dc == ND - 1)
                xamm_ms[qc] = self.op(
                    "tensor",
                    lambda t, dc=dc, qc=qc, fin=fin:
                    t.matmul(PB[:, 0:QB],
                             aall_sb[:, dc * ER:(dc + 1) * ER],
                             big0[:, 0:ND * TOWN].rearrange("p (dc t) -> p dc t", dc=ND)[:, dc, qc * QB:(qc + 1) * QB],
                             start=(dc == 0), stop=fin),
                    waits=wts, inc=(("pe", 1) if fin else None))
            for dc in range(ND):
                wts = []
                if dc == 0 and qc > 0:
                    wts = [("ac", gexac_ms[qc - 1])]
                self.op(
                    "tensor",
                    lambda t, dc=dc, qc=qc:
                    t.matmul(PB[0:NE, 512:512 + QB],
                             gwt_sb[:, dc * NE:(dc + 1) * NE],
                             big0[:, 0:ND * TOWN].rearrange("p (dc t) -> p dc t", dc=ND)[:, dc, qc * QB:(qc + 1) * QB],
                             start=(dc == 0), stop=False),
                    waits=wts)
            lgmm = self.op(
                "tensor",
                lambda t: t.matmul(PB[0:NE, 512:512 + QB], gb_row[:, :], ones_rowq[:, :],
                                   start=False, stop=True),
                waits=[("dv", ones_dv)], inc=("pe", 1))
            lgac_ms[qc] = self.op(
                "scalar",
                lambda sc: sc.activation(lg_fm[:, :], PB[0:NE, 512:512 + QB], AF.Copy),
                waits=[("pe", lgmm)], inc=("ac", 1))

        def pregame_b(qc):
            for qt_ in range(NQTC):
                wts = [("ac", lgac_ms[qc])] if qt_ == 0 else []
                if qt_ == 0 and qc > 0:
                    wts.append(("dv", lgtdv_ms[qc - 1]))
                fin = (qt_ == NQTC - 1)
                lgtp = self.op(
                    "tensor",
                    lambda t, qt_=qt_:
                    t.transpose(PAb[:, 3072 + qt_ * 256: 3072 + qt_ * 256 + NE],
                                lg_fm[:, qt_ * 128:(qt_ + 1) * 128], ident_bf[0:NE, 0:NE]),
                    waits=wts, inc=(("pe", 1) if fin else None))
            lgtdv_ms[qc] = lgtdv = self.op(
                "vector",
                lambda v: v.tensor_copy(
                    lg_tm[:, :].rearrange("p (qt e) -> p qt e", e=NE),
                    PAb[:, 3072:4096].rearrange("p (qt x) -> p qt x", x=256)[:, 0:NQTC, 0:NE]),
                waits=[("pe", lgtp)], inc=("dv", 1))
            gat = {}
            for qt_ in range(NQTC):
                lg = lg_tm[:, qt_ * NE:(qt_ + 1) * NE]
                gt = gt_tm[:, qt_ * NE:(qt_ + 1) * NE]
                wts = [("dv", lgtdv)] if qt_ == 0 else []
                self.op("vector", lambda v, lg=lg: v.reduce_max(stat[:, 8:9], lg, AX.X), waits=wts)
                self.op("vector", lambda v, lg=lg, gt=gt: v.tensor_scalar(gt, lg, stat[:, 8:9], None, ALU.is_equal))
                self.op("vector", lambda v, gt=gt: v.tensor_scalar(gt, gt, -1e30, None, ALU.mult))
                self.op("vector", lambda v, lg=lg, gt=gt: v.tensor_add(gt, gt, lg))
                self.op("vector", lambda v, gt=gt: v.reduce_max(stat[:, 9:10], gt, AX.X))
                vdv = self.op("vector", lambda v: v.tensor_scalar(stat[:, 10:11], stat[:, 8:9], -1.0, None, ALU.mult),
                              inc=("dv", 1))
                self.op("scalar", lambda sc, lg=lg, gt=gt: sc.activation(gt, lg, AF.Exp, bias=stat[:, 10:11]),
                        waits=[("dv", vdv)])
                e2 = self.op("scalar", lambda sc: sc.activation(stat[:, 11:12], stat[:, 9:10], AF.Exp, bias=stat[:, 10:11]),
                             inc=("ac", 1))
                self.op("vector", lambda v: v.tensor_scalar(stat[:, 12:13], stat[:, 11:12], 1.0, None, ALU.add),
                        waits=[("ac", e2)])
                self.op("vector", lambda v: v.reciprocal(stat[:, 13:14], stat[:, 12:13]))
                self.op("vector", lambda v, lg=lg: v.tensor_scalar(lg, lg, stat[:, 9:10], None, ALU.is_ge))
                self.op("vector", lambda v, lg=lg, gt=gt: v.tensor_mul(gt, gt, lg))
                gat[qt_] = self.op("vector", lambda v, gt=gt: v.tensor_scalar(gt, gt, stat[:, 13:14], None, ALU.mult),
                                   inc=("dv", 1))
            self._gat_last = gat[NQTC - 1]

        def pregame_c(qc):
            gbdv = self.op("vector",
                           lambda v: v.tensor_copy(gt_bf[:, :], gt_tm[:, :]),
                           waits=[("dv", self._gat_last)], inc=("dv", 1))
            for qt_ in range(NQTC):
                wts = [("dv", gbdv)]
                if qt_ == 0 and qc > 0:
                    wts.append(("dv", gtfdv_ms[qc - 1]))
                fin = (qt_ == NQTC - 1)
                gttp = self.op(
                    "tensor",
                    lambda t, qt_=qt_:
                    t.transpose(PAb[0:NE, 2048 + qt_ * 256: 2048 + qt_ * 256 + 128],
                                gt_bf[:, qt_ * NE:(qt_ + 1) * NE], ident_bf[:, :]),
                    waits=wts, inc=(("pe", 1) if fin else None))
            gtfdv_ms[qc] = gtfdv = self.op(
                "vector",
                lambda v: v.tensor_copy(
                    gt_fm[:, :].rearrange("o (qt t) -> o qt t", t=128),
                    PAb[0:NE, 2048:3072].rearrange("o (qt x) -> o qt x", x=256)[:, 0:NQTC, 0:128]),
                waits=[("pe", gttp)], inc=("dv", 1))
            gexmm = self.op("tensor",
                            lambda t: t.matmul(PB[:, 512:512 + QB], e16_bf[:, :], gt_fm[:, :],
                                               start=True, stop=True),
                            waits=[("dv", gtfdv)], inc=("pe", 1))
            gexac_ms[qc] = gexac = self.op(
                "scalar",
                lambda sc: sc.activation(gexp_sb[:, :], PB[:, 512:512 + QB], AF.Copy),
                waits=[("pe", gexmm)], inc=("ac", 1))
            gxa_ms[qc] = self.op(
                "vector",
                lambda v, qc=qc: v.tensor_tensor(gxa_t[qc % 2][:, :], PB[:, 0:QB],
                                                 gexp_sb[:, :], ALU.mult),
                waits=[("ac", gexac), ("pe", xamm_ms[qc])], inc=("dv", 1))

        def ftm_dst(qc, oc):
            if qc % 2 == 0:
                return bigC[:, 0:NQTC * D].rearrange(
                    "p (qt d) -> p qt d", d=D)[:, :, oc * 128:(oc + 1) * 128]
            return big0[:, oc * TOWN + (qc - 1) * QB: oc * TOWN + qc * QB] \
                .rearrange("p (qt e) -> p qt e", e=128)

        def ftm_row(qc, qt_):
            if qc % 2 == 0:
                return bigC[:, 0:NQTC * D].rearrange(
                    "p (qt d) -> p qt d", d=D)[:, qt_, :]
            return big0[:, 0:ND * TOWN].rearrange(
                "p (dc t) -> p dc t", dc=ND)[:, :, (qc - 1) * QB + qt_ * 128:
                                             (qc - 1) * QB + (qt_ + 1) * 128]

        def emit_ln2_qt(qc, qt_):
            qt = qc * NQTC + qt_
            obuf = qt % 2
            wts = [("dv", fcp[(qc, ND - 1)])]
            if qt > 0:
                wts.append(("dv", sq2_ac[qt - 1]))
            r2_dv[qt] = self.op(
                "vector",
                lambda v, row=ftm_row(qc, qt_), qt=qt: v.scalar_tensor_tensor(
                    rscr_t, row,
                    1.0, big1[:, qt * D:(qt + 1) * D], ALU.mult, ALU.add,
                    accum_out=stat[:, 0:1]),
                waits=wts, inc=("dv", 1))
            sq2_ac[qt] = self.op(
                "vector",
                lambda v: v.scalar_tensor_tensor(
                    e_buf[:, 0:D], rscr_t, 1.0, rscr_t, ALU.mult, ALU.mult,
                    accum_out=stat[:, 1:2]),
                waits=[("dv", r2_dv[qt])], inc=("dv", 1))
            self._ln_stats(sq2_ac[qt], EPS, D, sem="dv")
            wts = [("ac", self.val("ac"))]
            if qt >= 2:
                wts.append((f"so{obuf}", st_s[qt - 2]))
            odv = self.op(
                "vector",
                lambda v, obuf=obuf: v.tensor_scalar(
                    outf_t[obuf], rscr_t,
                    stat[:, 2:3], stat[:, 7:8], ALU.subtract, ALU.mult),
                waits=wts, inc=("dv", 1))
            st_s[qt] = self.dma("sync", f"so{obuf}",
                                out_h.ap()[qt * 128:(qt + 1) * 128, :],
                                outf_t[obuf], waits=[("dv", odv)])

        pregame_a(0)
        pregame_b(0)
        pregame_c(0)
        for qc in range(NQC):
            for oc in range(ND):
                wbuf = oc % 2
                if qc == 0 and oc < 2:
                    wts = [("pe", oproj[(NQC - 1, ND - 1)])]
                else:
                    prev = (qc, oc - 2) if oc >= 2 else (qc - 1, ND - 2 + oc)
                    wts = [("pe", fproj[prev])]
                if qc == 0:
                    wts = wts + [("pc", pc_all)]
                wg = self.dma(
                    "gpsimd", f"gw{wbuf}",
                    wblkv[wbuf], wffnbf_h.ap()[oc, :, :],
                    waits=wts)
                for dc in range(ND):
                    wts = []
                    if dc == 0:
                        wts = [(f"gw{wbuf}", wg)]
                        if qc == 0 and oc == 0:
                            wts.append(("dv", gtfdv_ms[0]))
                        if qc > 0 or oc >= 2:
                            prev = (qc, oc - 2) if oc >= 2 else (qc - 1, ND - 2 + oc)
                            wts.append(("ac", fevac[prev]))
                    self.op(
                        "tensor",
                        lambda t, oc=oc, dc=dc, qc=qc, wbuf=wbuf:
                        t.matmul(PA[:, (oc % 2) * 512:(oc % 2) * 512 + QB],
                                 wblkv[wbuf][:, dc * 128:(dc + 1) * 128],
                                 big0[:, 0:ND * TOWN].rearrange("p (dc t) -> p dc t", dc=ND)[:, dc, qc * QB:(qc + 1) * QB],
                                 start=(dc == 0), stop=False),
                        waits=wts)
                fproj[(qc, oc)] = self.op(
                    "tensor",
                    lambda t, oc=oc, qc=qc:
                    t.matmul(PA[:, (oc % 2) * 512:(oc % 2) * 512 + QB],
                             ball_sb[:, oc * 128:(oc + 1) * 128],
                             gxa_t[qc % 2][:, :], start=False, stop=True),
                    waits=[("dv", gxa_ms[qc])], inc=("pe", 1))
                wts = [("pe", fproj[(qc, oc)])]
                if qc > 0 or oc >= 2:
                    prev = (qc, oc - 2) if oc >= 2 else (qc - 1, ND - 2 + oc)
                    wts.append(("pe", ftr[prev]))
                fevac[(qc, oc)] = self.op(
                    "scalar",
                    lambda sc, oc=oc: sc.activation(
                        oevac_t[oc % 2][:, 0:QB],
                        PA[:, (oc % 2) * 512:(oc % 2) * 512 + QB],
                        AF.Identity, bias=bffnt_sb[:, oc:oc + 1]),
                    waits=wts, inc=("ac", 1))

                def emit_ftr(qc, oc):
                    for qt_ in range(NQTC):
                        wts = [("ac", fevac[(qc, oc)])] if qt_ == 0 else []
                        if qt_ == 0 and (qc > 0 or oc >= 2):
                            prev = (qc, oc - 2) if oc >= 2 else (qc - 1, ND - 2 + oc)
                            wts.append(("dv", fcp[prev]))
                        fin = (qt_ == NQTC - 1)
                        r = self.op(
                            "tensor",
                            lambda t, oc=oc, qt_=qt_:
                            t.transpose(PBb[:, 2048 + (oc % 2) * 1024 + qt_ * 128:
                                            2048 + (oc % 2) * 1024 + (qt_ + 1) * 128],
                                        oevac_t[oc % 2][:, qt_ * 128:(qt_ + 1) * 128],
                                        ident_bf[:, :]),
                            waits=wts, inc=(("pe", 1) if fin else None))
                        if fin:
                            ftr[(qc, oc)] = r
                    fcp[(qc, oc)] = self.op(
                        "vector",
                        lambda v, oc=oc, dst=ftm_dst(qc, oc): v.tensor_copy(
                            dst,
                            PBb[:, 2048 + (oc % 2) * 1024: 2048 + (oc % 2) * 1024 + QB].rearrange("p (qt e) -> p qt e", e=128)),
                        waits=[("pe", ftr[(qc, oc)])], inc=("dv", 1))
                if pend_ftr:
                    emit_ftr(*pend_ftr.pop(0))
                pend_ftr.append((qc, oc))
                if qc + 1 < NQC:
                    if oc == 2:
                        pregame_a(qc + 1)
                    elif oc == 4:
                        pregame_b(qc + 1)
                    elif oc == 6:
                        pregame_c(qc + 1)
                if oc in (8, 10, 12, 14) and pend_ln2:
                    emit_ln2_qt(*pend_ln2.pop(0))
            pend_ln2.extend((qc, qt_) for qt_ in range(NQTC))
        while pend_ftr:
            emit_ftr(*pend_ftr.pop(0))
        while pend_ln2:
            emit_ln2_qt(*pend_ln2.pop(0))

        fin = [(ch, self.val(ch)) for ch in ("so0", "so1", "sd") if self.val(ch)]
        self.op("sync", lambda sy: sy.nop(), waits=fin)
        self.play()
        return nc

    def _ln_stats(self, sqac, EPS, D, sem="ac"):
        stat = self._smalls[:, 0:16]
        self.op("vector", lambda v: v.tensor_scalar(stat[:, 2:3], stat[:, 0:1], 1.0 / D, None, ALU.mult),
                waits=[(sem, sqac)])
        self.op("vector", lambda v: v.tensor_scalar(stat[:, 3:4], stat[:, 1:2], 1.0 / D, None, ALU.mult))
        self.op("vector", lambda v: v.tensor_scalar(stat[:, 4:5], stat[:, 2:3], stat[:, 2:3], None, ALU.mult))
        vdv = self.op("vector", lambda v: v.tensor_sub(stat[:, 5:6], stat[:, 3:4], stat[:, 4:5]),
                      inc=("dv", 1))
        self.op("scalar", lambda sc: sc.activation(stat[:, 6:7], stat[:, 5:6], AF.Ln, bias=EPS),
                waits=[("dv", vdv)])
        self.op("scalar", lambda sc: sc.activation(stat[:, 7:8], stat[:, 6:7], AF.Exp, scale=-0.5),
                inc=("ac", 1))


def build_core_program(cfg):
    b = Builder(cfg)
    nc = b.build()
    return nc, b


def stage_inputs(L0_attn_x, L0_attn_w_qkv, L0_attn_b_qkv, L0_attn_w_out,
                 L0_attn_b_out, L0_ffn_W, L0_ffn_b, L0_ffn_lora_A,
                 L0_ffn_lora_B, L0_ffn_gate_w, L0_ffn_gate_b, cfg, n_cores=8):
    D, ND, NE, R, ER = cfg["D"], cfg["ND"], cfg["NE"], cfg["R"], cfg["ER"]
    TOWN = cfg["TOWN"]
    x = np.ascontiguousarray(np.asarray(L0_attn_x, dtype=np.float32))
    wqkv = np.ascontiguousarray(np.asarray(L0_attn_w_qkv, dtype=np.float32))
    bqkv = np.ascontiguousarray(np.asarray(L0_attn_b_qkv, dtype=np.float32))
    wout = np.ascontiguousarray(np.asarray(L0_attn_w_out, dtype=np.float32))
    bout = np.ascontiguousarray(np.asarray(L0_attn_b_out, dtype=np.float32))
    W = np.asarray(L0_ffn_W, dtype=np.float32)
    bf = np.ascontiguousarray(np.asarray(L0_ffn_b, dtype=np.float32))
    lA = np.asarray(L0_ffn_lora_A, dtype=np.float32)
    lB = np.asarray(L0_ffn_lora_B, dtype=np.float32)
    gw = np.asarray(L0_ffn_gate_w, dtype=np.float32)
    gb = np.ascontiguousarray(np.asarray(L0_ffn_gate_b, dtype=np.float32))

    VCH = cfg["VCH"]
    NVS = D // VCH

    def block_w(w):
        Dout = w.shape[1]
        nb = Dout // 128
        return np.ascontiguousarray(
            w.reshape(ND, 128, nb, 128).transpose(2, 1, 0, 3).reshape(nb, 128, ND * 128))

    wq_b = block_w(wqkv[:, 0:D])
    wk_b = block_w(wqkv[:, D:2 * D])
    wv_cols = wqkv[:, 2 * D:3 * D]
    wv_b = np.ascontiguousarray(
        wv_cols.reshape(ND, 128, NVS, VCH).transpose(2, 1, 0, 3).reshape(NVS, 128, ND * VCH))
    common = {
        "wkq": np.ascontiguousarray(np.concatenate([wk_b, wq_b], axis=0)),
        "wv": wv_b,
        "bqkvt": np.ascontiguousarray(bqkv.reshape(3 * ND, 128).T),
        "wout": block_w(wout),
        "boutt": np.ascontiguousarray(bout.reshape(ND, 128).T),
        "wffnt": block_w(W.T),
        "bffnt": np.ascontiguousarray(bf.reshape(ND, 128).T),
        "aall": np.ascontiguousarray(lA.transpose(2, 0, 1).reshape(D, ER)),
        "ball": np.ascontiguousarray(lB.transpose(0, 2, 1).reshape(ER, D)),
        "gwt": np.ascontiguousarray(gw.T),
        "gb": gb,
        "ident": np.eye(128, dtype=np.float32),
        "e16": np.ascontiguousarray(
            np.repeat(np.eye(NE, dtype=np.float32), R, axis=1)),
    }
    in_maps = []
    for c in range(n_cores):
        bidx, h = c // 2, c % 2
        xb = x[bidx]
        if cfg["TOTH"] > 0:
            xr = np.ascontiguousarray(xb[h * TOWN:(h + 1) * TOWN])
        else:
            xr = np.ascontiguousarray(xb[:TOWN])
        m = dict(common)
        m["x"] = xr
        in_maps.append(m)
    return in_maps


def kernel(L0_attn_x, L0_attn_w_qkv, L0_attn_b_qkv, L0_attn_w_out, L0_attn_b_out,
           L0_ffn_W, L0_ffn_b, L0_ffn_lora_A, L0_ffn_lora_B, L0_ffn_gate_w,
           L0_ffn_gate_b):
    cfg = CFG_FULL
    in_maps = stage_inputs(L0_attn_x, L0_attn_w_qkv, L0_attn_b_qkv,
                           L0_attn_w_out, L0_attn_b_out, L0_ffn_W, L0_ffn_b,
                           L0_ffn_lora_A, L0_ffn_lora_B, L0_ffn_gate_w,
                           L0_ffn_gate_b, cfg)
    nc, _ = build_core_program(cfg)
    res = run_bass_kernel_spmd(nc, in_maps, core_ids=list(range(8)))
    out = np.empty((4, 4096, 2048), dtype=np.float32)
    for c in range(8):
        bidx, h = c // 2, c % 2
        out[bidx, h * 2048:(h + 1) * 2048, :] = res.results[c]["out"]
    return out
